# version 06
# baseline (speedup 1.0000x reference)
import numpy as np
import ml_dtypes
import concourse.bass as bass
import concourse.mybir as mybir
from concourse.bass_utils import run_bass_kernel_spmd

F32 = mybir.dt.float32
BF16 = mybir.dt.bfloat16
I32 = mybir.dt.int32
AF = mybir.ActivationFunctionType
ALU = mybir.AluOpType
AX = mybir.AxisListType

S = 8192
D = 2048
NH = 16
NOPE = 128
RD = 64
VD = 128
QL = 768
KVL = 512
FI = 8192
NCORE = 8
R = 1024
SCALE = 1.0 / float(np.sqrt(192.0))
EXPB = -16.0
KB = 1024

SEMS = ("ph", "pe", "dve", "act", "cc2", "cc3",
        "qxA", "qxB", "qxC", "qxD", "qxE",
        "qwA", "qwB", "qwC", "qwD", "qwE", "qwT0", "qwT1", "qwT2", "qwR0", "qwR1",
        "qsA", "qsB", "qsC")
ENGS = ("sync", "gpsimd", "vector", "scalar", "tensor")


def ts(i, n):
    return slice(i * n, (i + 1) * n)


class Emit:

    def __init__(self, nc, cur, semh):
        self.nc = nc
        self.cur = cur
        self.h = semh
        self.cnt = {k: 0 for k in SEMS}
        self.wm = {k: 0 for k in SEMS}
        self.E = {"sync": nc.sync, "gpsimd": nc.gpsimd, "vector": nc.vector,
                  "scalar": nc.scalar, "tensor": nc.tensor}[cur]

    def on(self, eng):
        return eng == self.cur

    def w(self, eng, sem, val):
        if not val or val <= 0:
            return
        if not self.on(eng):
            return
        if val <= self.wm[sem]:
            return
        self.wm[sem] = val
        self.E.wait_ge(self.h[sem], val)

    def do(self, eng, fn, sem=None, inc=1):
        if self.on(eng):
            inst = fn(self.E)
            if sem is not None and inst is not None:
                inst.then_inc(self.h[sem], inc)
        if sem is not None:
            self.cnt[sem] += inc
            return self.cnt[sem]
        return None

    def dma(self, qeng, sem, out, in_):
        assert sem in SEMS and sem[0] == "q", sem
        if self.on(qeng):
            self.E.dma_start(out=out, in_=in_).then_inc(self.h[sem], 16)
        self.cnt[sem] += 16
        return self.cnt[sem]

    def join(self):
        own = {"vector": "dve", "scalar": "act", "tensor": "pe"}.get(self.cur)
        if own is not None and self.cnt[own] > self.wm[own]:
            self.wm[own] = self.cnt[own]
            self.E.wait_ge(self.h[own], self.cnt[own])
        self.cnt["ph"] += 5
        tgt = self.cnt["ph"]
        self.E.nop().then_inc(self.h["ph"], 1)
        self.E.wait_ge(self.h["ph"], tgt)
        for s in ("pe", "dve", "act", "ph"):
            self.wm[s] = max(self.wm[s], self.cnt[s])


def _rope_tables_fp32():
    try:
        import jax
        import jax.numpy as jnp
        cpu = jax.devices("cpu")[0]
        with jax.default_device(cpu):
            inv = 1.0 / (10000.0 ** (jnp.arange(0, RD, 2, dtype=jnp.float32) / RD))
            t = jnp.arange(S, dtype=jnp.float32)
            fr = jnp.outer(t, inv)
            c = np.asarray(jnp.cos(fr), dtype=np.float32)
            s = np.asarray(jnp.sin(fr), dtype=np.float32)
        return c, s
    except Exception:
        inv = (1.0 / (10000.0 ** (np.arange(0, RD, 2, dtype=np.float32) /
                                  np.float32(RD)))).astype(np.float32)
        t = np.arange(S, dtype=np.float32)
        fr = np.outer(t, inv).astype(np.float32)
        return np.cos(fr).astype(np.float32), np.sin(fr).astype(np.float32)


def build_nc():
    nc = bass.Bass("TRN2", target_bir_lowering=False)
    g = {}

    def din(name, shape, dtype=F32):
        g[name] = nc.dram_tensor(name, shape, dtype, kind="ExternalInput").ap()

    din("x_own", [R, D]);      din("xT_c", [D, R]);      din("x_rowc", [R, D])
    din("w_dq", [D, QL]);      din("w_uq", [QL, NH * 192]); din("gq_col", [128, 6])
    din("w_dkv", [D, KVL]);    din("w_kr", [D, RD]);     din("gkv_col", [128, 4])
    din("w_uk_own", [KVL, 2 * NOPE]); din("w_uv_own", [KVL, 2 * VD]); din("w_o", [D, D])
    U8 = mybir.dt.uint8
    din("wgTr", [128, 64, 16, 128], U8)
    din("wuTr", [128, 64, 16, 128], U8)
    din("wdTr", [128, 16, 64, 128], U8)
    din("sg_col", [128, 64]);  din("su_col", [128, 64]); din("sd_col", [128, 16])
    din("cos_kc", [R, 32]);    din("sin_kc", [R, 32])
    din("ctab", [RD, R]);      din("stab", [RD, R])
    din("mask16", [128, 16]);  din("idn", [128, 128]);  din("idn2", [128, RD])

    g["y"] = nc.dram_tensor("y", [R, D], F32, kind="ExternalOutput").ap()
    g["xnT_sp"] = nc.dram_tensor("xnT_sp", [128, 16, R], BF16, kind="Internal").ap()
    g["attn_sp"] = nc.dram_tensor("attn_sp", [R, D], BF16, kind="Internal").ap()
    g["kv2_sh"] = nc.dram_tensor("kv2_sh", [128, 5 * R], BF16, kind="Internal").ap()
    g["kv2_gx"] = nc.dram_tensor("kv2_gx", [8, 128, 5 * R], BF16,
                                 kind="Internal", addr_space="Shared").ap()
    for j in range(2):
        g["kv3_sh%d" % j] = nc.dram_tensor("kv3_sh%d" % j, [128, 2 * S], BF16,
                                           kind="Internal").ap()
        g["kv3_gx%d" % j] = nc.dram_tensor("kv3_gx%d" % j, [8, 128, 2 * S], BF16,
                                           kind="Internal", addr_space="Shared").ap()

    P_ = {}

    def sp(name, shape, dtype):
        P_[name] = nc.alloc_sbuf_tensor(name, shape, dtype)

    sp("ones_col", [128, 1], BF16)
    sp("ones_row", [1, 128], BF16)
    sp("neg128_row", [1, 128], BF16)
    sp("idn_f", [128, 128], F32)
    sp("idn_b", [128, 128], BF16)
    sp("idn2_f", [128, RD], F32)
    sp("expb", [128, 1], F32)
    sp("idn2_b", [128, RD], BF16)
    sp("zro", [128, 512], BF16)
    sp("mask_b", [128, 16], BF16)
    sp("gq_sb", [128, 6], F32)
    sp("gkv_sb", [128, 4], F32)
    sp("sg_sb", [128, 64], F32)
    sp("su_sb", [128, 64], F32)
    sp("sd_sb", [128, 16], F32)
    sp("ncol_dkv", [1, KVL], BF16)
    sp("ncol_kr", [1, 128], BF16)

    pb = [nc.alloc_psum_tensor("pb%d" % i, [128, 512], F32).ap() for i in range(7)]
    psD = nc.alloc_psum_tensor("psD", [128, 512], F32).ap()

    def pbv(i):
        return pb[i].bitcast(BF16)

    ARENA = 200 * KB
    arena_t = nc.alloc_sbuf_tensor("arena", [128, ARENA], mybir.dt.uint8)
    abase = nc.lookup_mloc(arena_t).addr
    _seen = []

    def at(name, off_kb, shape, dtype):
        off = int(round(off_kb * KB))
        nb = int(np.prod(shape[1:])) * mybir.dt.size(dtype)
        assert off % 32 == 0 and off + nb <= ARENA, (name, off_kb, nb)
        _seen.append((name, off, nb))
        return nc.alloc_sbuf_tensor_at(name, shape, dtype, offset=abase + off).ap()

    kvT = at("kvT", 0, [128, 4, S], BF16)
    kpeT = at("kpeT", 64, [128, S], BF16)
    QnA = at("QnA", 80, [128, NH, R], BF16)
    QrA = at("QrA", 112, [128, 8, R], BF16)
    atT = at("atT", 128, [128, NH, R], BF16)
    wdkv_b = at("wdkv_b", 184, [128, 16, KVL], BF16)
    wkr_b = at("wkr_b", 180, [128, 16, 128], BF16)
    xo_st = [at("xo_st0", 0, [128, D], F32), at("xo_st1", 8, [128, D], F32)]
    xn_t = [at("xn_t0", 16, [128, D], BF16), at("xn_t1", 20, [128, D], BF16)]
    xnT = at("xnT", 24, [128, 16, R], BF16)
    qlnT = at("qlnT", 56, [128, 6, R], BF16)
    qr_tmp = at("qr_tmp", 72, [RD, R], BF16)
    st1 = at("st1", 74, [128, 64], F32)
    qr_prod = at("qr_prod", 75, [128, R], BF16)
    tabs_f = at("tabs_f", 176, [128, R], F32)
    tabs_b = at("tabs_b", 170, [128, R], BF16)
    wdq_b = at("wdq_b", 128, [128, 16, QL], BF16)
    wuq_st = [at("wuq_st0", 152, [128, 6, 192], BF16),
              at("wuq_st1", 156.5, [128, 6, 192], BF16)]
    wuq_b = [at("wuq_b0", 161, [128, 6, 256], BF16),
             at("wuq_b1", 164, [128, 6, 256], BF16)]
    ql_t = [at("ql_t0", 167, [128, QL], BF16), at("ql_t1", 168.5, [128, QL], BF16)]
    xT_st = [at("xT_st0", 128, [128, 16, 256], BF16),
             at("xT_st1", 136, [128, 16, 256], BF16),
             at("xT_st2", 144, [128, 16, 256], BF16)]
    xr_st = [at("xr_st0", 152, [128, D], BF16), at("xr_st1", 156, [128, D], BF16)]
    ck_st = [at("ck_st0", 160, [128, 4, 32], BF16), at("ck_st1", 160.25, [128, 4, 32], BF16)]
    sk_st = [at("sk_st0", 160.5, [128, 4, 32], BF16), at("sk_st1", 160.75, [128, 4, 32], BF16)]
    kv_b = [at("kv_b0", 161, [128, KVL], BF16), at("kv_b1", 162, [128, KVL], BF16)]
    kpe_sb = at("kpe_sb", 163, [128, 4, 128], BF16)
    rtmp = [at("rtmp0", 164, [128, 4, 32], BF16), at("rtmp1", 164.25, [128, 4, 32], BF16)]
    muT_row = at("muT_row", 166, [1, 512], BF16)
    mu_bf = at("mu_bf", 167.25, [128, 8], BF16)
    st2 = at("st2", 165, [128, 64], F32)
    st2b = at("st2b", 165.25, [128, 16], F32)
    kv_own = at("kv_own", 168, [128, 4, R], BF16)
    kpe_own = at("kpe_own", 176, [128, R], BF16)
    wuk_st1 = at("wuk_st1", 160, [128, 4, 128], F32)
    wuv_st1 = at("wuv_st1", 162, [128, 4, 128], F32)
    wuk_b1 = at("wuk_b1", 164, [128, 4, 128], BF16)
    wuv_b1 = at("wuv_b1", 165, [128, 4, 128], BF16)
    Kst = at("Kst", 166, [128, S], BF16)
    Vst = at("Vst", 182, [128, 64, 128], BF16)
    K_sb = [at("K_sb0", 160, [128, 512], BF16), at("K_sb1", 161, [128, 512], BF16)]
    V4_sb = [at("V4_sb0", 162, [128, 4, 128], BF16), at("V4_sb1", 163, [128, 4, 128], BF16)]
    P_sb = [at("P_sb0", 164, [128, R], BF16), at("P_sb1", 166, [128, R], BF16)]
    den_f = at("den_f", 168, [1, R], F32)
    rec_b = at("rec_b", 172, [1, R], BF16)
    bc_f = at("bc_f", 174, [128, R], F32)
    wo_b = at("wo_b", 0, [128, 16, KB], BF16)
    ao_st = [at("ao_st0", 32, [128, KB], BF16), at("ao_st1", 34, [128, KB], BF16)]
    xnT5 = at("xnT5", 0, [128, 16, R], BF16)
    hT = at("hT", 32, [128, 16, R], BF16)
    acc = at("acc", 64, [128, 16, R], F32)
    wg_b = [at("wg_b0", 128, [128, 16, 128], BF16), at("wg_b1", 132, [128, 16, 128], BF16)]
    wu_b = [at("wu_b0", 136, [128, 16, 128], BF16), at("wu_b1", 140, [128, 16, 128], BF16)]
    wd_b = [at("wd_b0", 144, [128, 16, 128], BF16), at("wd_b1", 148, [128, 16, 128], BF16)]
    hs_t = at("hs_t", 152, [128, KB], BF16)
    x5_st = [at("x5_st0", 154, [128, D], F32), at("x5_st1", 162, [128, D], F32)]
    at_st = [at("at_st0", 170, [128, D], BF16), at("at_st1", 174, [128, D], BF16)]
    out_st = [at("out_st0", 178, [128, D], F32), at("out_st1", 186, [128, D], F32)]
    rowsum_b = at("rowsum_b", 154, [1, R], BF16)
    hsum_hi = at("hsum_hi", 156, [1, R], BF16)
    hsum_lo = at("hsum_lo", 158, [1, R], BF16)
    hsum_f = at("hsum_f", 160, [1, R], F32)

    semh = {s: nc.alloc_semaphore(name=s) for s in SEMS}

    def emit(P):
        CK = {}

        def dvs(fn):
            P.w("vector", "dve", P.cnt["dve"])
            return P.do("vector", fn, "dve")

        def acs(fn):
            P.w("scalar", "act", P.cnt["act"])
            return P.do("scalar", fn, "act")

        P.dma("sync", "qxC", P_["idn_f"][:], g["idn"])
        P.dma("sync", "qxC", P_["gq_sb"][:], g["gq_col"])
        P.dma("sync", "qxC", P_["gkv_sb"][:], g["gkv_col"])
        P.dma("sync", "qxC", P_["sg_sb"][:], g["sg_col"])
        P.dma("sync", "qxC", P_["su_sb"][:], g["su_col"])
        P.dma("sync", "qxC", P_["sd_sb"][:], g["sd_col"])
        P.dma("sync", "qxC", tabs_f[0:RD, :], g["ctab"])
        P.dma("sync", "qxC", tabs_f[RD:128, :], g["stab"])
        v_sync0 = P.dma("sync", "qxC", P_["idn2_f"][:], g["idn2"])
        v_tab = v_idn2 = v_idn = v_sync0
        P.dma("gpsimd", "qwE", P_["mask_b"][:], g["mask16"])
        P.dma("gpsimd", "qwE", wdq_b[:], g["w_dq"].rearrange("(c p) n -> p c n", p=128))
        P.dma("gpsimd", "qwE", wdkv_b[:], g["w_dkv"].rearrange("(c p) n -> p c n", p=128))
        P.dma("gpsimd", "qwE", wkr_b[:, :, 0:RD], g["w_kr"].rearrange("(c p) n -> p c n", p=128))
        w_kr_ld = P.dma("gpsimd", "qwE", wkr_b[:, :, RD:128],
                        g["w_kr"].rearrange("(c p) n -> p c n", p=128))
        w_dkv_ld = w_kr_ld
        P.do("vector", lambda e: e.memset(P_["ones_col"][:], 1.0))
        P.do("vector", lambda e: e.memset(P_["ones_row"][:], 1.0))
        P.do("vector", lambda e: e.memset(P_["neg128_row"][:], -128.0))
        P.do("vector", lambda e: e.memset(P_["zro"][:], 0.0))
        ms_ck = P.do("vector", lambda e: e.memset(P_["expb"][:], EXPB), "dve")
        P.w("tensor", "dve", ms_ck)
        P.w("vector", "qxC", v_idn)
        P.do("vector", lambda e: e.tensor_copy(out=P_["idn_b"][:], in_=P_["idn_f"][:]), "dve")
        P.w("vector", "qxC", v_idn2)
        P.do("vector", lambda e: e.tensor_copy(out=P_["idn2_b"][:], in_=P_["idn2_f"][:]), "dve")
        P.w("vector", "qxC", v_tab)
        P.do("vector", lambda e: e.tensor_copy(out=tabs_b[:], in_=tabs_f[:]), "dve")
        P.w("scalar", "qxC", v_sync0)
        P.w("vector", "qwE", w_kr_ld)
        P.w("tensor", "qwE", w_dkv_ld)
        m1 = 0
        for c in range(16):
            m1 = P.do("tensor", lambda e, c=c: e.matmul(pb[0][0:1, 0:KVL], P_["ones_col"][:],
                                                        wdkv_b[:, c, :], start=(c == 0),
                                                        stop=(c == 15)),
                      "pe" if c == 15 else None)
        P.w("tensor", "qwE", w_kr_ld)
        m2 = 0
        for c in range(16):
            m2 = P.do("tensor", lambda e, c=c: e.matmul(pb[1][0:1, 0:128], P_["ones_col"][:],
                                                        wkr_b[:, c, :], start=(c == 0),
                                                        stop=(c == 15)),
                      "pe" if c == 15 else None)
        P.w("scalar", "pe", m1)
        P.do("scalar", lambda e: e.activation(out=P_["ncol_dkv"][:], in_=pb[0][0:1, 0:KVL],
                                              func=AF.Copy, scale=-1.0), "act")
        P.w("scalar", "pe", m2)
        P.do("scalar", lambda e: e.activation(out=P_["ncol_kr"][:], in_=pb[1][0:1, 0:128],
                                              func=AF.Copy, scale=-1.0), "act")
        P.join()

        apply_ck = [0] * 8
        tr_pe = [0] * 8
        tr_ck = [0] * 8
        xo_dma = [0] * 8
        for t in range(8):
            if t >= 2:
                P.w("sync", "dve", apply_ck[t - 2])
            xo_dma[t] = P.dma("sync", "qxA" if t % 2 == 0 else "qxB",
                              xo_st[t % 2][:], g["x_own"][ts(t, 128), :])
            P.w("vector", "qxA" if t % 2 == 0 else "qxB", xo_dma[t])
            rs_ck = P.do("vector", lambda e, t=t: e.reduce_sum(st1[:, t:t + 1], xo_st[t % 2][:],
                                                               axis=AX.X), "dve")
            P.w("scalar", "qxA" if t % 2 == 0 else "qxB", xo_dma[t])
            if t >= 2:
                P.w("scalar", "pe", tr_pe[t - 2])
            P.do("scalar", lambda e, t=t: e.activation(out=xn_t[t % 2][:], in_=xo_st[t % 2][:],
                                                       func=AF.Square,
                                                       accum_out=st1[:, 8 + t:9 + t]), "act")
            sq_ck = P.cnt["act"]
            P.w("vector", "act", sq_ck)
            dvs(lambda e, t=t: e.tensor_scalar(out=st1[:, 16 + t:17 + t],
                                               in0=st1[:, t:t + 1], scalar1=1.0 / D,
                                               scalar2=None, op0=ALU.mult))
            dvs(lambda e, t=t: e.tensor_scalar(out=st1[:, 40 + t:41 + t],
                                               in0=st1[:, 8 + t:9 + t], scalar1=1.0 / D,
                                               scalar2=1e-5, op0=ALU.mult, op1=ALU.add))
            dvs(lambda e, t=t: e.tensor_tensor(out=st1[:, 48 + t:49 + t],
                                               in0=st1[:, 16 + t:17 + t],
                                               in1=st1[:, 16 + t:17 + t], op=ALU.mult))
            dv = dvs(lambda e, t=t: e.tensor_tensor(out=st1[:, 40 + t:41 + t],
                                                    in0=st1[:, 40 + t:41 + t],
                                                    in1=st1[:, 48 + t:49 + t],
                                                    op=ALU.subtract))
            P.w("scalar", "dve", dv)
            sr = acs(lambda e, t=t: e.activation(out=st1[:, 48 + t:49 + t],
                                                 in_=st1[:, 40 + t:41 + t],
                                                 func=AF.Sqrt))
            P.w("vector", "act", sr)
            rcp = dvs(lambda e, t=t: e.reciprocal(out=st1[:, 24 + t:25 + t],
                                                  in_=st1[:, 48 + t:49 + t]))
            if t >= 2:
                P.w("vector", "pe", tr_pe[t - 2])
            apply_ck[t] = dvs(lambda e, t=t: e.tensor_scalar(
                out=xn_t[t % 2][:], in0=xo_st[t % 2][:],
                scalar1=st1[:, 16 + t:17 + t], scalar2=st1[:, 24 + t:25 + t],
                op0=ALU.subtract, op1=ALU.mult))
            P.w("tensor", "dve", apply_ck[t])
            if t >= 1:
                P.w("tensor", "dve", tr_ck[t - 1])
            for j in range(16):
                v = P.do("tensor", lambda e, t=t, j=j: e.transpose(
                    pbv(4 + j // 8)[:, ts(j % 8, 128)], xn_t[t % 2][:, ts(j, 128)],
                    P_["idn_b"][:]),
                    "pe" if j == 15 else None)
                if j == 15:
                    tr_pe[t] = v
            P.w("vector", "pe", tr_pe[t])
            P.do("vector", lambda e, t=t: e.tensor_copy(
                out=xnT[:, 0:8, ts(t, 128)],
                in_=pbv(4).rearrange("p (c r) -> p c r", c=8)))
            tr_ck[t] = P.do("vector", lambda e, t=t: e.tensor_copy(
                out=xnT[:, 8:16, ts(t, 128)],
                in_=pbv(5).rearrange("p (c r) -> p c r", c=8)), "dve")
        qln_pe = [0] * 8
        qln_ck = [0] * 8
        ap2_ck = [0] * 8
        for t in range(8):
            P.w("tensor", "dve", tr_ck[7] if t == 0 else tr_ck[t])
            if t >= 2:
                P.w("tensor", "act", ap2_ck[t - 2])
            mmq = 0
            for j in range(16):
                P.do("tensor", lambda e, t=t, j=j: e.matmul(pb[0][:, 0:512], xnT[:, j, ts(t, 128)],
                                                            wdq_b[:, j, 0:512],
                                                            start=(j == 0), stop=(j == 15)))
                mmq = P.do("tensor", lambda e, t=t, j=j: e.matmul(pb[1][:, 0:256],
                                                                  xnT[:, j, ts(t, 128)],
                                                                  wdq_b[:, j, 512:768],
                                                                  start=(j == 0), stop=(j == 15)),
                           "pe" if j == 15 else None)
            P.w("scalar", "pe", mmq)
            if t >= 2:
                P.w("scalar", "pe", qln_pe[t - 2])
            P.do("scalar", lambda e, t=t: e.activation(out=ql_t[t % 2][:, 0:512], in_=pb[0][:, 0:512],
                                                       func=AF.Square, accum_out=st1[:, t:t + 1]), "act")
            s2 = P.do("scalar", lambda e, t=t: e.activation(out=ql_t[t % 2][:, 512:768],
                                                            in_=pb[1][:, 0:256], func=AF.Square,
                                                            accum_out=st1[:, 8 + t:9 + t]), "act")
            P.w("vector", "act", s2)
            dvs(lambda e, t=t: e.tensor_tensor(out=st1[:, t:t + 1], in0=st1[:, t:t + 1],
                                               in1=st1[:, 8 + t:9 + t], op=ALU.add))
            dv2 = dvs(lambda e, t=t: e.tensor_scalar(out=st1[:, t:t + 1],
                                                     in0=st1[:, t:t + 1],
                                                     scalar1=1.0 / QL, scalar2=1e-6,
                                                     op0=ALU.mult, op1=ALU.add))
            P.w("scalar", "dve", dv2)
            sr2 = acs(lambda e, t=t: e.activation(out=st1[:, 8 + t:9 + t],
                                                  in_=st1[:, t:t + 1], func=AF.Sqrt))
            P.w("vector", "act", sr2)
            rc2 = dvs(lambda e, t=t: e.reciprocal(out=st1[:, 16 + t:17 + t],
                                                  in_=st1[:, 8 + t:9 + t]))
            P.w("scalar", "dve", rc2)
            acs(lambda e, t=t: e.activation(out=ql_t[t % 2][:, 0:512], in_=pb[0][:, 0:512],
                                            func=AF.Copy, scale=st1[:, 16 + t:17 + t]))
            ap2_ck[t] = acs(lambda e, t=t: e.activation(
                out=ql_t[t % 2][:, 512:768], in_=pb[1][:, 0:256],
                func=AF.Copy, scale=st1[:, 16 + t:17 + t]))
            P.w("tensor", "act", ap2_ck[t])
            if t >= 1:
                P.w("tensor", "dve", qln_ck[t - 1])
            for j in range(6):
                v = P.do("tensor", lambda e, t=t, j=j: e.transpose(
                    pbv(4)[:, ts(j, 128)], ql_t[t % 2][:, ts(j, 128)], P_["idn_b"][:]),
                    "pe" if j == 5 else None)
                if j == 5:
                    qln_pe[t] = v
            P.w("vector", "pe", qln_pe[t])
            qln_ck[t] = P.do("vector", lambda e, t=t: e.tensor_copy(
                out=qlnT[:, :, ts(t, 128)],
                in_=pbv(4)[:, 0:768].rearrange("p (c r) -> p c r", c=6)), "dve")
        P.w("scalar", "dve", tr_ck[7])
        sp_ck = P.dma("scalar", "qsC", g["xnT_sp"][:, :, :], xnT[:, :, :])
        CK["xnT_spill"] = sp_ck
        wq_dma = [0, 0]
        cvtq = [0, 0]
        q_done = [0] * NH
        for h in range(NH):
            par = h % 2
            P.w("gpsimd", "dve", cvtq[par])
            wq_dma[par] = P.dma("gpsimd", "qwA" if par == 0 else "qwB", wuq_st[par][:],
                                g["w_uq"].rearrange("(c p) n -> p c n", p=128)[:, :, ts(h, 192)])
            P.w("vector", "qwA" if par == 0 else "qwB", wq_dma[par])
            if h >= 2:
                P.w("vector", "pe", CK.get(("qmm", h - 2), 0))
            ts6 = 0
            for j in range(6):
                ts6 = P.do("vector", lambda e, j=j, par=par: e.tensor_scalar(
                    out=wuq_b[par][:, j, 0:192], in0=wuq_st[par][:, j, :],
                    scalar1=P_["gq_sb"][:, j:j + 1], scalar2=SCALE, op0=ALU.mult, op1=ALU.mult),
                    "dve" if j == 5 else None)
            P.w("vector", "dve", ts6)
            P.do("vector", lambda e, par=par: e.tensor_copy(out=wuq_b[par][:, :, 192:224],
                                                            in_=wuq_b[par][:, :, 160:192]))
            cvtq[par] = P.do("vector", lambda e, par=par: e.tensor_copy(
                out=wuq_b[par][:, :, 224:256], in_=wuq_b[par][:, :, 128:160]), "dve")
            P.w("tensor", "dve", cvtq[par])
            if h == 0:
                P.w("tensor", "dve", qln_ck[7])
            else:
                P.w("tensor", "dve", q_done[h - 1])
            qn_mm = 0
            for j in range(6):
                for nn in range(2):
                    qn_mm = P.do("tensor", lambda e, j=j, par=par, nn=nn: e.matmul(
                        pb[2 + nn][:, 0:512], wuq_b[par][:, j, 0:128],
                        qlnT[:, j, ts(nn, 512)], start=(j == 0), stop=(j == 5)),
                        "pe" if (j == 5 and nn == 1) else None)
            qr_mm = 0
            for j in range(6):
                for nn in range(2):
                    qr_mm = P.do("tensor", lambda e, j=j, par=par, nn=nn: e.matmul(
                        pb[4 + nn][:, 0:512], wuq_b[par][:, j, 128:256],
                        qlnT[:, j, ts(nn, 512)], start=(j == 0), stop=(j == 5)),
                        "pe" if (j == 5 and nn == 1) else None)
            CK[("qmm", h)] = qr_mm
            P.w("vector", "pe", qn_mm)
            P.do("vector", lambda e, h=h: e.tensor_copy(out=QnA[:, h, 0:512], in_=pb[2][:, 0:512]))
            P.do("vector", lambda e, h=h: e.tensor_copy(out=QnA[:, h, 512:1024], in_=pb[3][:, 0:512]))
            P.w("vector", "pe", qr_mm)
            P.do("vector", lambda e: e.tensor_tensor(
                out=qr_prod[:, 0:512], in0=pb[4][:, 0:512], in1=tabs_b[:, 0:512], op=ALU.mult))
            prod_ck = P.do("vector", lambda e: e.tensor_tensor(
                out=qr_prod[:, 512:1024], in0=pb[5][:, 0:512], in1=tabs_b[:, 512:1024],
                op=ALU.mult), "dve")
            P.w("tensor", "dve", prod_ck)
            P.do("tensor", lambda e: e.matmul(pb[6][0:RD, 0:512], P_["idn2_b"][:],
                                              qr_prod[:, 0:512], start=True, stop=True))
            fold_mm = P.do("tensor", lambda e: e.matmul(psD[0:RD, 0:512], P_["idn2_b"][:],
                                                        qr_prod[:, 512:1024],
                                                        start=True, stop=True), "pe")
            P.w("vector", "pe", fold_mm)
            if h >= 1:
                P.w("vector", "qxC", CK[("qrd", h - 1)])
            P.do("vector", lambda e: e.tensor_copy(out=qr_tmp[:, 0:512], in_=pb[6][0:RD, 0:512]))
            q_done[h] = P.do("vector", lambda e: e.tensor_copy(out=qr_tmp[:, 512:1024],
                                                               in_=psD[0:RD, 0:512]), "dve")
            P.w("sync", "dve", q_done[h])
            CK[("qrd", h)] = P.dma("sync", "qxC", QrA[RD * (h % 2):RD * (h % 2) + RD, h // 2, :],
                                   qr_tmp[:])
        CK["qra_done"] = CK[("qrd", NH - 1)]
        P.join()

        P.w("vector", "qsC", CK["xnT_spill"])
        P.w("vector", "qxC", CK["qra_done"])
        kv_apply = [0, 0]
        xT_free = [0, 0, 0]
        xr_free = [0, 0]
        ckst_free = [0, 0]
        tr_evac = 0
        kpe_evac = 0
        muT_ack = 0
        kvb_pe = [0, 0]
        rs_dve = [0, 0]
        for t in range(2):
            for half in range(2):
                slot = (2 * t + half) % 3
                P.w("gpsimd", "pe", xT_free[slot])
                CK[("xT", t, half)] = P.dma(
                    "gpsimd", "qwT%d" % slot, xT_st[slot][:],
                    g["xT_c"].rearrange("(c p) n -> p c n", p=128)[:, :, 512 * t + 256 * half:
                                                                  512 * t + 256 * half + 256])
            P.w("gpsimd", "dve", ckst_free[t % 2])
            P.dma("gpsimd", "qwA" if t % 2 == 0 else "qwB", ck_st[t % 2][:],
                  g["cos_kc"].rearrange("(s p) r -> p s r", p=128)[:, 4 * t:4 * t + 4, :])
            tab_dma = P.dma("gpsimd", "qwA" if t % 2 == 0 else "qwB", sk_st[t % 2][:],
                            g["sin_kc"].rearrange("(s p) r -> p s r", p=128)[:, 4 * t:4 * t + 4, :])
            for s in range(4):
                sl = (4 * t + s) % 2
                P.w("gpsimd", "act", xr_free[sl])
                xr_dma = P.dma("gpsimd", "qwR%d" % sl, xr_st[sl][:],
                               g["x_rowc"][512 * t + 128 * s:512 * t + 128 * s + 128, :])
                P.w("vector", "qwR%d" % sl, xr_dma)
                red = P.do("vector", lambda e, sl=sl, s=s: e.reduce_sum(
                    st2[:, s:s + 1], xr_st[sl][:], axis=AX.X), "dve")
                P.w("scalar", "qwR%d" % sl, xr_dma)
                P.w("scalar", "dve", red)
                xr_free[sl] = P.do("scalar", lambda e, sl=sl, s=s: e.activation(
                    out=xr_st[sl][:], in_=xr_st[sl][:], func=AF.Square,
                    accum_out=st2[:, 4 + s:5 + s]), "act")
            sq_ck = P.cnt["act"]
            P.w("vector", "act", sq_ck)
            dvs(lambda e: e.tensor_scalar(out=st2[:, 8:12], in0=st2[:, 0:4],
                                          scalar1=1.0 / D, scalar2=None, op0=ALU.mult))
            dvs(lambda e: e.tensor_scalar(out=st2[:, 16:20], in0=st2[:, 4:8],
                                          scalar1=1.0 / D, scalar2=1e-5,
                                          op0=ALU.mult, op1=ALU.add))
            dvs(lambda e: e.tensor_tensor(out=st2[:, 20:24], in0=st2[:, 8:12],
                                          in1=st2[:, 8:12], op=ALU.mult))
            dvs(lambda e: e.tensor_tensor(out=st2[:, 16:20], in0=st2[:, 16:20],
                                          in1=st2[:, 20:24], op=ALU.subtract))
            dv_e = dvs(lambda e: e.tensor_scalar(out=st2[:, 24:28], in0=st2[:, 16:20],
                                                 scalar1=1e-6, scalar2=None, op0=ALU.mult))
            P.w("scalar", "dve", dv_e)
            srs = acs(lambda e: e.activation(out=st2[:, 20:24], in_=st2[:, 16:20],
                                             func=AF.Sqrt))
            P.w("vector", "act", srs)
            invs = dvs(lambda e: e.reciprocal(out=st2[:, 12:16], in_=st2[:, 20:24]))
            mu_cast = dvs(lambda e: e.tensor_copy(out=mu_bf[:, 0:4], in_=st2[:, 8:12]))
            P.w("tensor", "dve", mu_cast)
            P.w("tensor", "act", muT_ack)
            trm = 0
            for s in range(4):
                trm = P.do("tensor", lambda e, s=s: e.matmul(
                    pb[3][0:1, ts(s, 128)], mu_bf[:, s:s + 1],
                    P_["idn_b"][:], start=True, stop=True), "pe" if s == 3 else None)
            P.w("scalar", "pe", trm)
            P.w("scalar", "pe", CK.get(("rank1_pe", t - 1), 0))
            muT_ack = P.do("scalar", lambda e: e.activation(out=muT_row[:], in_=pb[3][0:1, 0:512],
                                                            func=AF.Copy), "act")
            for s in range(4):
                par = s % 2
                pkv = pb[par]
                pkp = pb[2] if par == 0 else pb[6]
                P.w("tensor", "qwT%d" % ((2 * t + s // 2) % 3), CK[("xT", t, s // 2)])
                P.w("tensor", "act", kv_apply[par])
                P.w("tensor", "act", muT_ack)
                for c in range(16):
                    P.do("tensor", lambda e, t=t, s=s, c=c, pkv=pkv: e.matmul(
                        pkv[:, 0:512], xT_st[(2 * t + s // 2) % 3][:, c, ts(s % 2, 128)],
                        wdkv_b[:, c, :], start=(c == 0), stop=False))
                kv_mm = P.do("tensor", lambda e, s=s, pkv=pkv: e.matmul(
                    pkv[:, 0:512], muT_row[0:1, ts(s, 128)], P_["ncol_dkv"][:],
                    start=False, stop=True), "pe")
                for c in range(16):
                    P.do("tensor", lambda e, t=t, s=s, c=c, pkp=pkp: e.matmul(
                        pkp[:, 0:128], xT_st[(2 * t + s // 2) % 3][:, c, ts(s % 2, 128)],
                        wkr_b[:, c, :], start=(c == 0), stop=False))
                kpe_mm = P.do("tensor", lambda e, s=s, pkp=pkp: e.matmul(
                    pkp[:, 0:128], muT_row[0:1, ts(s, 128)], P_["ncol_kr"][:],
                    start=False, stop=True), "pe")
                if s == 3:
                    xT_free[(2 * t) % 3] = P.cnt["pe"]
                    xT_free[(2 * t + 1) % 3] = P.cnt["pe"]
                    CK[("rank1_pe", t)] = P.cnt["pe"]
                P.w("scalar", "pe", kv_mm)
                if (4 * t + s) >= 2:
                    P.w("scalar", "pe", kvb_pe[par])
                P.do("scalar", lambda e, par=par, pkv=pkv, s=s: e.activation(
                    out=kv_b[par][:], in_=pkv[:, 0:512], func=AF.Square,
                    accum_out=st2b[:, 4 * s:4 * s + 1]), "act")
                sq2 = P.cnt["act"]
                P.w("vector", "act", sq2)
                dvs(lambda e, s=s: e.tensor_scalar(
                    out=st2b[:, 4 * s + 1:4 * s + 2], in0=st2b[:, 4 * s:4 * s + 1],
                    scalar1=1.0 / KVL, scalar2=None, op0=ALU.mult))
                dvv = dvs(lambda e, s=s: e.tensor_tensor(
                    out=st2b[:, 4 * s + 1:4 * s + 2], in0=st2b[:, 4 * s + 1:4 * s + 2],
                    in1=st2[:, 24 + s:25 + s], op=ALU.add))
                P.w("scalar", "dve", dvv)
                sr2 = acs(lambda e, s=s: e.activation(
                    out=st2b[:, 4 * s + 2:4 * s + 3], in_=st2b[:, 4 * s + 1:4 * s + 2],
                    func=AF.Sqrt))
                P.w("vector", "act", sr2)
                rc2 = dvs(lambda e, s=s: e.reciprocal(
                    out=st2b[:, 4 * s + 3:4 * s + 4], in_=st2b[:, 4 * s + 2:4 * s + 3]))
                P.w("scalar", "dve", rc2)
                kv_apply[par] = acs(lambda e, par=par, pkv=pkv, s=s: e.activation(
                    out=kv_b[par][:], in_=pkv[:, 0:512], func=AF.Copy,
                    scale=st2b[:, 4 * s + 3:4 * s + 4]))
                P.w("scalar", "pe", kpe_mm)
                if s == 0:
                    P.w("scalar", "dve", CK.get(("rope", t - 1), 0))
                    P.w("scalar", "pe", CK.get(("kpetr", t - 1), 0))
                kpe_ap = P.do("scalar", lambda e, s=s, pkp=pkp: e.activation(
                    out=kpe_sb[:, s, :], in_=pkp[:, 0:128], func=AF.Copy,
                    scale=st2[:, 12 + s:13 + s]), "act")
                if s == 3:
                    CK[("kpe_ap", t)] = kpe_ap
                P.w("tensor", "act", kv_apply[par])
                P.w("tensor", "dve", tr_evac)
                for q in range(4):
                    v = P.do("tensor", lambda e, q=q, par=par: e.transpose(
                        pbv(4)[:, 256 * q + 128 * par:256 * q + 128 * par + 128],
                        kv_b[par][:, ts(q, 128)], P_["idn_b"][:]),
                        "pe" if q == 3 else None)
                kvb_pe[par] = v
                if par == 1:
                    P.w("vector", "pe", kvb_pe[1])
                    tr_evac = P.do("vector", lambda e, t=t, s=s: e.tensor_copy(
                        out=kv_own[:, :, 512 * t + 128 * (s - 1):512 * t + 128 * (s - 1) + 256],
                        in_=pbv(4).rearrange("p (q r) -> p q r", q=4)), "dve")
            P.w("vector", "act", CK[("kpe_ap", t)])
            P.w("vector", "qwA" if t % 2 == 0 else "qwB", tab_dma)

            def _x(i):
                return kpe_sb.rearrange("p s (h r) -> p s h r", h=4)[:, :, i, :]

            ctile = ck_st[t % 2]
            stile = sk_st[t % 2]
            dvs(lambda e: e.tensor_tensor(out=rtmp[0][:], in0=_x(1), in1=stile[:],
                                          op=ALU.mult))
            dvs(lambda e: e.tensor_tensor(out=rtmp[1][:], in0=_x(0), in1=ctile[:],
                                          op=ALU.mult))
            dvs(lambda e: e.tensor_tensor(out=_x(2), in0=_x(2), in1=stile[:],
                                          op=ALU.mult))
            dvs(lambda e: e.tensor_tensor(out=_x(3), in0=_x(3), in1=ctile[:],
                                          op=ALU.mult))
            dvs(lambda e: e.tensor_tensor(out=_x(1), in0=_x(2), in1=_x(3),
                                          op=ALU.add))
            dvs(lambda e: e.tensor_tensor(out=_x(0), in0=rtmp[1][:], in1=rtmp[0][:],
                                          op=ALU.subtract))
            dvs(lambda e: e.tensor_copy(out=_x(2), in_=_x(0)))
            ckst_free[t % 2] = dvs(lambda e: e.tensor_copy(out=_x(3), in_=_x(1)))
            rope_ck = ckst_free[t % 2]
            CK[("rope", t)] = rope_ck
            P.w("tensor", "dve", rope_ck)
            P.w("tensor", "dve", kpe_evac)
            tr2 = 0
            for s in range(4):
                tr2 = P.do("tensor", lambda e, s=s: e.transpose(pbv(5)[:, ts(s, 128)],
                                                                kpe_sb[:, s, :], P_["idn_b"][:]),
                           "pe" if s == 3 else None)
            CK[("kpetr", t)] = tr2
            P.w("vector", "pe", tr2)
            kpe_evac = P.do("vector", lambda e, t=t: e.tensor_copy(
                out=kpe_own[:, ts(t, 512)], in_=pbv(5)[:, 0:512]), "dve")
        P.w("scalar", "dve", max(tr_evac, kpe_evac))
        P.dma("scalar", "qsC", g["kv2_sh"][:, 0:4096],
              kv_own[:].rearrange("p c k -> p (c k)"))
        sp2 = P.dma("scalar", "qsC", g["kv2_sh"][:, 4096:5120], kpe_own[:])
        P.w("gpsimd", "qsC", sp2)
        P.do("gpsimd", lambda e: e.collective_compute(
            "AllGather", ALU.bypass, replica_groups=[list(range(8))],
            ins=[g["kv2_sh"].opt()], outs=[g["kv2_gx"].opt()]), "cc2")
        P.w("sync", "cc2", 1)
        P.w("sync", "qsC", CK["xnT_spill"])
        P.w("sync", "qxC", CK["qra_done"])
        for r in range(8):
            P.dma("sync", "qxC", kvT[:, :, ts(r, 1024)],
                  g["kv2_gx"][r, :, 0:4096].rearrange("p (c k) -> p c k", c=4))
            CK["kv_loaded"] = P.dma("sync", "qxC", kpeT[:, ts(r, 1024)],
                                    g["kv2_gx"][r, :, 4096:5120])
        P.join()

        P.w("tensor", "qxC", CK["kv_loaded"])
        cvt3 = 0
        spill = 0
        prod_pe = 0
        for j in range(2):
            P.w("scalar", "dve", cvt3)
            st_dma = P.dma("scalar", "qsA", wuk_st1[:],
                           g["w_uk_own"].rearrange("(q p) n -> p q n", p=128)[:, :, ts(j, 128)])
            st_dma = P.dma("scalar", "qsA", wuv_st1[:],
                           g["w_uv_own"].rearrange("(q p) n -> p q n", p=128)[:, :, ts(j, 128)])
            P.w("vector", "qsA", st_dma)
            P.w("vector", "pe", prod_pe)
            for q in range(4):
                P.do("vector", lambda e, q=q: e.tensor_scalar(
                    out=wuk_b1[:, q, :], in0=wuk_st1[:, q, :],
                    scalar1=P_["gkv_sb"][:, q:q + 1], scalar2=None, op0=ALU.mult))
            for q in range(4):
                cvt3 = P.do("vector", lambda e, q=q: e.tensor_scalar(
                    out=wuv_b1[:, q, :], in0=wuv_st1[:, q, :],
                    scalar1=P_["gkv_sb"][:, q:q + 1], scalar2=None, op0=ALU.mult),
                    "dve" if q == 3 else None)
            P.w("vector", "qsC", spill)
            kst_ev = [0, 0]
            for gge in range(16):
                bk = 5 + (gge % 2)
                P.w("tensor", "dve", cvt3)
                P.w("tensor", "dve", kst_ev[gge % 2])
                for q in range(4):
                    P.do("tensor", lambda e, q=q, gge=gge, bk=bk: e.matmul(
                        pb[bk][:, 0:512], wuk_b1[:, q, :], kvT[:, q, ts(gge, 512)],
                        start=(q == 0), stop=(q == 3)), "pe" if q == 3 else None)
                P.w("vector", "pe", P.cnt["pe"])
                kst_ev[gge % 2] = P.do("vector", lambda e, gge=gge, bk=bk: e.tensor_copy(
                    out=Kst[:, ts(gge, 512)], in_=pb[bk][:, 0:512]), "dve")
            vst_ev = [kst_ev[0], kst_ev[1]]
            for k in range(64):
                bk = 5 + (k % 2)
                P.w("tensor", "dve", vst_ev[k % 2])
                for q in range(4):
                    P.do("tensor", lambda e, q=q, k=k, bk=bk: e.matmul(
                        pb[bk][:, 0:128], kvT[:, q, ts(k, 128)], wuv_b1[:, q, :],
                        start=(q == 0), stop=(q == 3)), "pe" if q == 3 else None)
                P.w("vector", "pe", P.cnt["pe"])
                vst_ev[k % 2] = P.do("vector", lambda e, k=k, bk=bk: e.tensor_copy(
                    out=Vst[:, k, :], in_=pb[bk][:, 0:128]), "dve")
            prod_pe = P.cnt["pe"]
            P.w("scalar", "dve", max(vst_ev))
            P.dma("scalar", "qsC", g["kv3_sh%d" % j][:, 0:S], Kst[:])
            spill = P.dma("scalar", "qsC", g["kv3_sh%d" % j][:, S:2 * S],
                          Vst[:].rearrange("p t v -> p (t v)"))
            P.w("gpsimd", "qsC", spill)
            P.do("gpsimd", lambda e, j=j: e.collective_compute(
                "AllGather", ALU.bypass, replica_groups=[list(range(8))],
                ins=[g["kv3_sh%d" % j].opt()], outs=[g["kv3_gx%d" % j].opt()]), "cc3")
        P.w("sync", "dve", cvt3)
        P.w("gpsimd", "dve", cvt3)
        P.w("scalar", "qsC", spill)
        P.w("vector", "qsC", spill)
        head_norm = [0] * (NH + 1)
        expk = [0, 0]
        pvk = [0, 0]
        den_evac = 0
        kdma = [0, 0]
        vdma = [0, 0]
        pend = [None]
        tail_h = [None]

        def emit_pv(k, kpar, gpar, s4, mk, npr, hi, gidx, h):
            P.w("tensor", "dve", mk)
            if k == 0:
                P.w("tensor", "dve", CK.get("pv_ev1", 0))
                P.w("tensor", "dve", CK.get("psd_free", 0))
            stf = (k == 0)
            if 16 * k < 512:
                na = 512 - 16 * k
                P.do("tensor", lambda e, kpar=kpar, gpar=gpar, s4=s4, k=k, na=na, stf=stf: e.matmul(
                    pb[0][:, 16 * k:512], V4_sb[gpar][:, s4, :], P_sb[kpar][:, 0:na],
                    start=stf, stop=(k == 31)))
                P.do("tensor", lambda e, kpar=kpar, k=k, na=na: e.matmul(
                    psD[0:1, 16 * k:512], P_["ones_col"][:], P_sb[kpar][:, 0:na],
                    start=(k == 0), stop=(k == 31)))
                P.do("tensor", lambda e, kpar=kpar, gpar=gpar, s4=s4, na=na, npr=npr, stf=stf: e.matmul(
                    pb[1][:, 0:512], V4_sb[gpar][:, s4, :], P_sb[kpar][:, na:npr],
                    start=stf, stop=False))
                pvk[kpar] = P.do("tensor", lambda e, kpar=kpar, k=k, na=na, npr=npr: e.matmul(
                    psD[32:33, 0:512], P_["ones_col"][:], P_sb[kpar][:, na:npr],
                    start=(k == 0), stop=False), "pe")
            else:
                P.do("tensor", lambda e, kpar=kpar, gpar=gpar, s4=s4, k=k, npr=npr: e.matmul(
                    pb[1][:, 16 * k - 512:512], V4_sb[gpar][:, s4, :], P_sb[kpar][:, 0:npr],
                    start=False, stop=(k == 63)))
                pvk[kpar] = P.do("tensor", lambda e, kpar=kpar, k=k, npr=npr: e.matmul(
                    psD[32:33, 16 * k - 512:512], P_["ones_col"][:], P_sb[kpar][:, 0:npr],
                    start=False, stop=(k == 63)), "pe")
            if s4 == 3:
                CK[("pvdone", gidx)] = P.cnt["pe"]
            if k == 31:
                P.w("vector", "pe", pvk[kpar])
                CK["pv_ev0"] = P.do("vector", lambda e, h=h: e.tensor_copy(
                    out=atT[:, h, 0:512], in_=pb[0][:, 0:512]), "dve")
            elif k == 63:
                P.w("vector", "pe", pvk[kpar])
                CK["pv_ev1"] = P.do("vector", lambda e, h=h: e.tensor_copy(
                    out=atT[:, h, 512:1024], in_=pb[1][:, 0:512]), "dve")

        def emit_tail(h, hi):
            P.w("vector", "pe", pvk[0])
            P.w("vector", "pe", pvk[1])
            dvs(lambda e: e.tensor_copy(out=rec_b[:, 0:512], in_=psD[0:1, :]))
            den_ev = dvs(lambda e: e.tensor_copy(out=rec_b[:, 512:1024], in_=psD[32:33, :]))
            P.w("tensor", "dve", den_ev)
            P.do("tensor", lambda e: e.matmul(pb[6][:, 0:512], P_["ones_row"][:],
                                              rec_b[:, 0:512], start=True, stop=True))
            bc_mm = P.do("tensor", lambda e: e.matmul(psD[:, 0:512], P_["ones_row"][:],
                                                      rec_b[:, 512:1024],
                                                      start=True, stop=True), "pe")
            P.w("vector", "pe", bc_mm)
            dvs(lambda e: e.tensor_copy(out=bc_f[:, 0:512], in_=pb[6][:, 0:512]))
            CK["psd_free"] = dvs(lambda e: e.tensor_copy(out=bc_f[:, 512:1024],
                                                         in_=psD[:, 0:512]))
            dvs(lambda e: e.reciprocal(out=bc_f[:, 0:512], in_=bc_f[:, 0:512]))
            dvs(lambda e: e.reciprocal(out=bc_f[:, 512:1024], in_=bc_f[:, 512:1024]))
            dvs(lambda e, h=h: e.tensor_tensor(
                out=atT[:, h, 0:512], in0=atT[:, h, 0:512], in1=bc_f[:, 0:512],
                op=ALU.mult))
            head_norm[hi] = dvs(lambda e, h=h: e.tensor_tensor(
                out=atT[:, h, 512:1024], in0=atT[:, h, 512:1024],
                in1=bc_f[:, 512:1024], op=ALU.mult))

        for hi, h in enumerate(list(range(0, NH, 2)) + list(range(1, NH, 2))):
            j = h % 2
            rk = h // 2
            base = RD * (h % 2)
            hslot = h // 2
            P.w("sync", "cc3", j + 1)
            P.w("gpsimd", "cc3", j + 1)
            for gge in range(16):
                gidx = 16 * hi + gge
                gpar = gidx % 2
                P.w("sync", "pe", CK.get(("scdone", gidx - 2), 0))
                kdma[gpar] = P.dma("sync", "qxA" if gpar == 0 else "qxB", K_sb[gpar][:],
                                   g["kv3_gx%d" % j][rk, :, 512 * gge:512 * gge + 512])
                P.w("gpsimd", "pe", CK.get(("pvdone", gidx - 2), 0))
                vdma[gpar] = P.dma("gpsimd", "qwA" if gpar == 0 else "qwB", V4_sb[gpar][:],
                                   g["kv3_gx%d" % j][rk, :, S + 512 * gge:S + 512 * gge + 512]
                                   .rearrange("p (t v) -> p t v", t=4))
                P.w("tensor", "qxA" if gpar == 0 else "qxB", kdma[gpar])
                P.w("tensor", "qwA" if gpar == 0 else "qwB", vdma[gpar])
                for s4 in range(4):
                    k = 4 * gge + s4
                    kpar = k % 2
                    npr = 1024 - 16 * k
                    sb0 = 2 if kpar == 0 else 4
                    P.w("tensor", "act", expk[kpar])
                    n1 = min(npr, 512)
                    P.do("tensor", lambda e, sb0=sb0, gpar=gpar, s4=s4, h=h, k=k, n1=n1: e.matmul(
                        pb[sb0][:, 0:n1], K_sb[gpar][:, ts(s4, 128)],
                        QnA[:, h, 16 * k:16 * k + n1], start=True, stop=False))
                    if npr > 512:
                        P.do("tensor", lambda e, sb0=sb0, gpar=gpar, s4=s4, h=h, k=k, npr=npr: e.matmul(
                            pb[sb0 + 1][:, 0:npr - 512], K_sb[gpar][:, ts(s4, 128)],
                            QnA[:, h, 16 * k + 512:1024], start=True, stop=False))
                    scmm = P.do("tensor", lambda e, sb0=sb0, base=base, hslot=hslot, k=k, n1=n1, npr=npr: e.matmul(
                        pb[sb0][:, 0:n1], kpeT[base:base + RD, ts(k, 128)],
                        QrA[base:base + RD, hslot, 16 * k:16 * k + n1],
                        start=False, stop=True),
                        "pe" if npr <= 512 else None)
                    if npr > 512:
                        scmm = P.do("tensor", lambda e, sb0=sb0, base=base, hslot=hslot, k=k, npr=npr: e.matmul(
                            pb[sb0 + 1][:, 0:npr - 512], kpeT[base:base + RD, ts(k, 128)],
                            QrA[base:base + RD, hslot, 16 * k + 512:1024],
                            start=False, stop=True), "pe")
                    if s4 == 3:
                        CK[("scdone", gidx)] = scmm
                    P.w("scalar", "pe", scmm)
                    P.w("scalar", "pe", pvk[kpar])
                    expk[kpar] = P.do("scalar", lambda e, kpar=kpar, sb0=sb0, n1=n1: e.activation(
                        out=P_sb[kpar][:, 0:n1], in_=pb[sb0][:, 0:n1],
                        func=AF.Exp, bias=P_["expb"][:]), "act")
                    if npr > 512:
                        expk[kpar] = P.do("scalar", lambda e, kpar=kpar, sb0=sb0, npr=npr: e.activation(
                            out=P_sb[kpar][:, 512:npr], in_=pb[sb0 + 1][:, 0:npr - 512],
                            func=AF.Exp, bias=P_["expb"][:]), "act")
                    P.w("vector", "act", expk[kpar])
                    mk = P.do("vector", lambda e, kpar=kpar: e.tensor_tensor(
                        out=P_sb[kpar][:, 0:16], in0=P_sb[kpar][:, 0:16],
                        in1=P_["mask_b"][:], op=ALU.mult), "dve")
                    if tail_h[0] is not None and gge == 0 and s4 == 1:
                        emit_tail(*tail_h[0])
                        tail_h[0] = None
                    if pend[0] is not None:
                        emit_pv(**pend[0])
                    pend[0] = dict(k=k, kpar=kpar, gpar=gpar, s4=s4, mk=mk,
                                   npr=npr, hi=hi, gidx=gidx, h=h)
            emit_pv(**pend[0])
            pend[0] = None
            tail_h[0] = (h, hi)
        emit_tail(*tail_h[0])
        tail_h[0] = None
        P.join()

        evac4 = [0, 0]
        sp4 = [0, 0]
        wo_pe_done = 0
        for half in range(2):
            P.w("gpsimd", "pe", wo_pe_done)
            wo_dma = P.dma("gpsimd", "qwE", wo_b[:],
                           g["w_o"].rearrange("(c p) n -> p c n", p=128)[:, :, 1024 * half:1024 * half + 1024])
            for qt in range(8):
                ppar = qt % 2
                qsem = "qsA" if ppar == 0 else "qsB"
                b4 = 0 if ppar == 0 else 2
                P.w("tensor", "qwE", wo_dma)
                P.w("tensor", "dve", evac4[ppar])
                mm4 = 0
                for c in range(16):
                    for nn in range(2):
                        mm4 = P.do("tensor", lambda e, c=c, nn=nn, qt=qt, b4=b4: e.matmul(
                            pb[b4 + nn][:, 0:512],
                            atT[:, c, ts(qt, 128)], wo_b[:, c, ts(nn, 512)],
                            start=(c == 0), stop=(c == 15)),
                            "pe" if (c == 15 and nn == 1) else None)
                P.w("vector", "pe", mm4)
                P.w("vector", qsem, sp4[ppar])
                P.do("vector", lambda e, ppar=ppar, b4=b4: e.tensor_copy(
                    out=ao_st[ppar][:, 0:512], in_=pb[b4][:, 0:512]))
                evac4[ppar] = P.do("vector", lambda e, ppar=ppar, b4=b4: e.tensor_copy(
                    out=ao_st[ppar][:, 512:1024], in_=pb[b4 + 1][:, 0:512]), "dve")
                P.w("scalar", "dve", evac4[ppar])
                sp4[ppar] = P.dma("scalar", qsem,
                                  g["attn_sp"][ts(qt, 128), 1024 * half:1024 * half + 1024],
                                  ao_st[ppar][:])
            wo_pe_done = P.cnt["pe"]
        P.join()

        P.w("vector", "qsA", sp4[0])
        P.w("vector", "qsB", sp4[1])
        P.w("sync", "qsC", CK["xnT_spill"])
        rl = P.dma("sync", "qxC", xnT5[:], g["xnT_sp"][:, :, :])
        P.w("tensor", "qxC", rl)
        rs_mm = 0
        for c in range(16):
            for nn in range(2):
                rs_mm = P.do("tensor", lambda e, c=c, nn=nn: e.matmul(
                    psD[0:1, 0:512] if nn == 0 else psD[32:33, 0:512],
                    P_["ones_col"][:], xnT5[:, c, ts(nn, 512)],
                    start=(c == 0), stop=(c == 15)),
                    "pe" if (c == 15 and nn == 1) else None)
        P.w("scalar", "pe", rs_mm)
        P.do("scalar", lambda e: e.activation(out=rowsum_b[:, 0:512], in_=psD[0:1, :],
                                              func=AF.Copy))
        rs_ck = P.do("scalar", lambda e: e.activation(out=rowsum_b[:, 512:1024],
                                                      in_=psD[32:33, :], func=AF.Copy), "act")
        P.w("tensor", "act", rs_ck)
        silu_ck = [0, 0]
        stt_ck = [0, 0]
        g_pe = {}
        d_pe = {}
        dacc_ck = {}
        hs_done = 0
        d_last_pe = 0
        stt_last = 0
        for blk in range(4):
            for it in range(16):
                gi = 16 * blk + it
                wpar = gi % 2
                P.w("gpsimd", "pe", g_pe.get(gi - 2, 0))
                P.dma("gpsimd", "qwA" if wpar == 0 else "qwB", wg_b[wpar][:],
                      g["wgTr"][:, gi, :, :])
                wgu = P.dma("gpsimd", "qwA" if wpar == 0 else "qwB", wu_b[wpar][:],
                            g["wuTr"][:, gi, :, :])
                for rh in range(2):
                    pa = (2 * it + rh) % 2
                    bg = 0 if pa == 0 else 2
                    bu = 1 if pa == 0 else 3
                    P.w("tensor", "qwA" if wpar == 0 else "qwB", wgu)
                    P.w("tensor", "qxC", rl)
                    P.w("tensor", "act", rs_ck)
                    P.w("tensor", "act", silu_ck[pa])
                    P.w("tensor", "dve", stt_ck[pa])
                    if blk >= 1 and it == 0 and rh == 0:
                        P.w("tensor", "dve", dacc_ck.get((blk - 1, 15), 0))
                    for c in range(16):
                        P.do("tensor", lambda e, c=c, rh=rh, wpar=wpar, bg=bg: e.matmul(
                            pb[bg][:, 0:512], wg_b[wpar][:, c, :], xnT5[:, c, ts(rh, 512)],
                            start=(c == 0), stop=False))
                    P.do("tensor", lambda e, rh=rh, bg=bg: e.matmul(
                        pb[bg][:, 0:512], P_["neg128_row"][:],
                        rowsum_b[:, ts(rh, 512)], start=False, stop=True))
                    for c in range(16):
                        P.do("tensor", lambda e, c=c, rh=rh, wpar=wpar, bu=bu: e.matmul(
                            pb[bu][:, 0:512], wu_b[wpar][:, c, :], xnT5[:, c, ts(rh, 512)],
                            start=(c == 0), stop=False))
                    gmm = P.do("tensor", lambda e, rh=rh, bu=bu: e.matmul(
                        pb[bu][:, 0:512], P_["neg128_row"][:],
                        rowsum_b[:, ts(rh, 512)], start=False, stop=True), "pe")
                    if rh == 1:
                        g_pe[gi] = gmm
                    P.w("scalar", "pe", gmm)
                    P.w("scalar", "dve", stt_ck[pa])
                    silu_ck[pa] = P.do("scalar", lambda e, rh=rh, gi=gi, bg=bg: e.activation(
                        out=hs_t[:, ts(rh, 512)], in_=pb[bg][:, 0:512], func=AF.Silu,
                        scale=P_["sg_sb"][:, gi:gi + 1]), "act")
                    P.w("vector", "act", silu_ck[pa])
                    if blk >= 1 and it == 0 and rh <= 1:
                        P.w("vector", "pe", d_last_pe)
                    stt_ck[pa] = P.do("vector", lambda e, rh=rh, gi=gi, it=it, bu=bu: e.scalar_tensor_tensor(
                        out=hT[:, it, ts(rh, 512)], in0=pb[bu][:, 0:512],
                        scalar=P_["su_sb"][:, gi:gi + 1], in1=hs_t[:, ts(rh, 512)],
                        op0=ALU.mult, op1=ALU.mult), "dve")
                P.w("tensor", "dve", stt_ck[0])
                P.w("tensor", "dve", stt_ck[1])
                for nn in range(2):
                    hs_mm = P.do("tensor", lambda e, it=it, nn=nn, blk=blk: e.matmul(
                        psD[0:1, 0:512] if nn == 0 else psD[32:33, 0:512],
                        P_["ones_col"][:], hT[:, it, ts(nn, 512)],
                        start=(blk == 0 and it == 0), stop=(blk == 3 and it == 15)),
                        "pe" if nn == 1 else None)
                if it == 15:
                    stt_last = max(stt_ck[0], stt_ck[1])
            if blk == 3:
                P.w("scalar", "pe", hs_mm)
                hf = P.do("scalar", lambda e: e.activation(out=hsum_f[:, 0:512],
                                                           in_=psD[0:1, :], func=AF.Copy), "act")
                hf = P.do("scalar", lambda e: e.activation(out=hsum_f[:, 512:1024],
                                                           in_=psD[32:33, :], func=AF.Copy), "act")
                P.w("vector", "act", hf)
                dvs(lambda e: e.tensor_copy(out=hsum_hi[:], in_=hsum_f[:]))
                hs_done = dvs(lambda e: e.tensor_tensor(
                    out=hsum_lo[:], in0=hsum_f[:], in1=hsum_hi[:],
                    op=ALU.subtract))
            for ht_i in range(16):
                dpar = ht_i % 2
                P.w("gpsimd", "pe", d_pe.get((blk, ht_i - 2), 0))
                wdd = P.dma("gpsimd", "qwC" if dpar == 0 else "qwD", wd_b[dpar][:],
                            g["wdTr"][:, ht_i, 16 * blk:16 * blk + 16, :])
                P.w("tensor", "qwC" if dpar == 0 else "qwD", wdd)
                P.w("tensor", "dve", stt_last)
                P.w("tensor", "dve", dacc_ck.get((blk, ht_i - 1), 0))
                if blk == 3:
                    P.w("tensor", "dve", hs_done)
                for c in range(16):
                    for nn in range(2):
                        dmm = P.do("tensor", lambda e, c=c, nn=nn, dpar=dpar, blk=blk: e.matmul(
                            pb[4 + nn][:, 0:512], wd_b[dpar][:, c, :],
                            hT[:, c, ts(nn, 512)], start=(c == 0),
                            stop=(blk != 3 and c == 15)),
                            "pe" if (blk != 3 and c == 15 and nn == 1) else None)
                if blk == 3:
                    for nn in range(2):
                        P.do("tensor", lambda e, nn=nn: e.matmul(
                            pb[4 + nn][:, 0:512], P_["neg128_row"][:],
                            hsum_hi[:, ts(nn, 512)], start=False, stop=False))
                    for nn in range(2):
                        dmm = P.do("tensor", lambda e, nn=nn: e.matmul(
                            pb[4 + nn][:, 0:512], P_["neg128_row"][:],
                            hsum_lo[:, ts(nn, 512)], start=False, stop=True),
                            "pe" if nn == 1 else None)
                d_pe[(blk, ht_i)] = dmm
                P.w("vector", "pe", dmm)
                if blk >= 1:
                    P.w("vector", "dve", dacc_ck[(blk - 1, ht_i)])
                if blk == 0:
                    P.do("vector", lambda e, ht_i=ht_i: e.tensor_copy(
                        out=acc[:, ht_i, 0:512], in_=pb[4][:, 0:512]))
                    dacc_ck[(blk, ht_i)] = P.do("vector", lambda e, ht_i=ht_i: e.tensor_copy(
                        out=acc[:, ht_i, 512:1024], in_=pb[5][:, 0:512]), "dve")
                else:
                    P.do("vector", lambda e, ht_i=ht_i: e.tensor_tensor(
                        out=acc[:, ht_i, 0:512], in0=acc[:, ht_i, 0:512],
                        in1=pb[4][:, 0:512], op=ALU.add))
                    dacc_ck[(blk, ht_i)] = P.do("vector", lambda e, ht_i=ht_i: e.tensor_tensor(
                        out=acc[:, ht_i, 512:1024], in0=acc[:, ht_i, 512:1024],
                        in1=pb[5][:, 0:512], op=ALU.add), "dve")
            d_last_pe = P.cnt["pe"]
        sc_ck = 0
        for ht_i in range(16):
            P.w("vector", "dve", dacc_ck[(3, ht_i)])
            sc_ck = dvs(lambda e, ht_i=ht_i: e.tensor_scalar(
                out=acc[:, ht_i, :], in0=acc[:, ht_i, :],
                scalar1=P_["sd_sb"][:, ht_i:ht_i + 1], scalar2=None, op0=ALU.mult))
        fin_dve = [0, 0]
        y_dmas = [0] * 8
        y_dma = 0
        for rt in range(8):
            rpar = rt % 2
            P.w("sync", "dve", sc_ck)
            P.w("sync", "dve", fin_dve[rpar])
            x5d = P.dma("sync", "qxA" if rpar == 0 else "qxB", x5_st[rpar][:],
                        g["x_own"][ts(rt, 128), :])
            P.w("scalar", "qsA" if rpar == 0 else "qsB", sp4[rpar])
            P.w("scalar", "dve", sc_ck)
            P.w("scalar", "dve", fin_dve[rpar])
            atd = P.dma("scalar", "qsA" if rpar == 0 else "qsB", at_st[rpar][:],
                        g["attn_sp"][ts(rt, 128), :])
            P.w("tensor", "dve", sc_ck)
            P.w("tensor", "dve", fin_dve[0])
            P.w("tensor", "dve", fin_dve[1])
            trf = 0
            for ht_i in range(16):
                trf = P.do("tensor", lambda e, ht_i=ht_i, rt=rt: e.transpose(
                    pb[ht_i // 4][:, ts(ht_i % 4, 128)], acc[:, ht_i, ts(rt, 128)],
                    P_["idn_f"][:]),
                    "pe" if ht_i == 15 else None)
            P.w("vector", "pe", trf)
            P.w("vector", "qxA" if rpar == 0 else "qxB", x5d)
            P.w("vector", "qsA" if rpar == 0 else "qsB", atd)
            if rt >= 2:
                P.w("vector", "qxD" if rpar == 0 else "qxE", y_dmas[rt - 2])
            for bb in range(4):
                dvs(lambda e, rpar=rpar, bb=bb: e.tensor_tensor(
                    out=out_st[rpar][:, ts(bb, 512)], in0=pb[bb][:, 0:512],
                    in1=x5_st[rpar][:, ts(bb, 512)], op=ALU.add))
            fin_dve[rpar] = dvs(lambda e, rpar=rpar: e.tensor_tensor(
                out=out_st[rpar][:], in0=out_st[rpar][:], in1=at_st[rpar][:], op=ALU.add))
            P.w("sync", "dve", fin_dve[rpar])
            y_dmas[rt] = y_dma = P.dma("sync", "qxD" if rpar == 0 else "qxE",
                                       g["y"][ts(rt, 128), :], out_st[rpar][:])
        for eng in ENGS:
            P.w(eng, "qxD", y_dmas[6])
            P.w(eng, "qxE", y_dmas[7])
        return P.cnt

    counts = []
    with nc.Block() as block:
        deco = {"sync": block.sync, "gpsimd": block.gpsimd, "vector": block.vector,
                "scalar": block.scalar, "tensor": block.tensor}
        for name in ENGS:
            holder = {}

            def run(eng_obj, name=name, holder=holder):
                Pp = Emit(nc, name, semh)
                holder["cnt"] = emit(Pp)

            deco[name](run)
            counts.append(dict(holder["cnt"]))
    for c in counts[1:]:
        assert c == counts[0], ("pass counter drift", counts[0], c)
    return nc


_CACHE = {}


def _get_nc():
    if "nc" not in _CACHE:
        _CACHE["nc"] = build_nc()
    return _CACHE["nc"]


def kernel(L0_attn_x, L0_attn_w_dq, L0_attn_g_q, L0_attn_w_uq, L0_attn_w_dkv,
           L0_attn_w_kr, L0_attn_g_kv, L0_attn_w_uk, L0_attn_w_uv, L0_attn_w_o,
           L0_ffn_gate_w, L0_ffn_gate_scale, L0_ffn_up_w, L0_ffn_up_scale,
           L0_ffn_down_w, L0_ffn_down_scale):
    x = np.ascontiguousarray(np.asarray(L0_attn_x, dtype=np.float32)[0])
    f32 = np.float32
    ac = np.ascontiguousarray
    cos_full, sin_full = _rope_tables_fp32()

    def _u8(w):
        a = np.ascontiguousarray(np.asarray(w, np.int32))
        return a.view(np.uint8)[:, 0::4]

    shared = {
        "w_dq": ac(np.asarray(L0_attn_w_dq, f32)),
        "w_uq": ac(np.asarray(L0_attn_w_uq, f32)),
        "gq_col": ac(np.asarray(L0_attn_g_q, f32).reshape(6, 128).T),
        "w_dkv": ac(np.asarray(L0_attn_w_dkv, f32)),
        "w_kr": ac(np.asarray(L0_attn_w_kr, f32)),
        "gkv_col": ac(np.asarray(L0_attn_g_kv, f32).reshape(4, 128).T),
        "w_o": ac(np.asarray(L0_attn_w_o, f32)),
        "wgTr": ac(_u8(L0_ffn_gate_w).T.reshape(16, 128, 64, 128).transpose(1, 2, 0, 3)),
        "wuTr": ac(_u8(L0_ffn_up_w).T.reshape(16, 128, 64, 128).transpose(1, 2, 0, 3)),
        "wdTr": ac(_u8(L0_ffn_down_w).T.reshape(64, 128, 16, 128).transpose(1, 2, 0, 3)),
        "sg_col": ac(np.asarray(L0_ffn_gate_scale, f32).reshape(64, 128).T),
        "su_col": ac(np.asarray(L0_ffn_up_scale, f32).reshape(64, 128).T),
        "sd_col": ac(np.asarray(L0_ffn_down_scale, f32).reshape(16, 128).T),
        "idn": np.eye(128, dtype=f32),
        "idn2": ac(np.tile(np.eye(RD, dtype=f32), (2, 1))),
    }

    in_maps = []
    for c in range(NCORE):
        pos = np.arange(c, S, 8)
        cc = cos_full[pos].T
        ss = sin_full[pos].T
        ctab = ac(np.concatenate([cc, cc], axis=0).astype(f32))
        stab = ac(np.concatenate([-ss, ss], axis=0).astype(f32))
        kr = np.arange(128, dtype=np.int64)[:, None]
        jj = np.arange(16, dtype=np.int64)[None, :]
        mask16 = (kr <= (c + 8 * jj)).astype(f32)
        m = dict(shared)
        m["x_own"] = ac(x[c::8])
        m["ctab"] = ctab
        m["stab"] = stab
        m["mask16"] = ac(mask16)
        m["w_uk_own"] = ac(np.asarray(L0_attn_w_uk, f32)[:, 256 * c:256 * c + 256])
        m["w_uv_own"] = ac(np.asarray(L0_attn_w_uv, f32)[:, 256 * c:256 * c + 256])
        m["xT_c"] = ac(x[1024 * c:1024 * c + 1024].T)
        m["x_rowc"] = ac(x[1024 * c:1024 * c + 1024])
        m["cos_kc"] = ac(cos_full[1024 * c:1024 * c + 1024])
        m["sin_kc"] = ac(sin_full[1024 * c:1024 * c + 1024])
        in_maps.append(m)

    nc = _get_nc()
    res = run_bass_kernel_spmd(nc, in_maps, core_ids=list(range(NCORE)))
    out = np.empty((S, D), dtype=np.float32)
    for c in range(NCORE):
        out[c::8] = res.results[c]["y"]
    return out.reshape(1, S, D)


# version 07
# speedup vs baseline: 1.0077x; 1.0077x over previous
import numpy as np
import ml_dtypes
import concourse.bass as bass
import concourse.mybir as mybir
from concourse.bass_utils import run_bass_kernel_spmd

F32 = mybir.dt.float32
BF16 = mybir.dt.bfloat16
I32 = mybir.dt.int32
AF = mybir.ActivationFunctionType
ALU = mybir.AluOpType
AX = mybir.AxisListType

S = 8192
D = 2048
NH = 16
NOPE = 128
RD = 64
VD = 128
QL = 768
KVL = 512
FI = 8192
NCORE = 8
R = 1024
SCALE = 1.0 / float(np.sqrt(192.0))
EXPB = -16.0
KB = 1024

SEMS = ("ph", "pe", "dve", "act", "cc2", "cc3",
        "qxA", "qxB", "qxC", "qxD", "qxE",
        "qwA", "qwB", "qwC", "qwD", "qwE", "qwT0", "qwT1", "qwT2", "qwR0", "qwR1",
        "qsA", "qsB", "qsC")
ENGS = ("sync", "gpsimd", "vector", "scalar", "tensor")


def ts(i, n):
    return slice(i * n, (i + 1) * n)


class Emit:

    def __init__(self, nc, cur, semh):
        self.nc = nc
        self.cur = cur
        self.h = semh
        self.cnt = {k: 0 for k in SEMS}
        self.wm = {k: 0 for k in SEMS}
        self.E = {"sync": nc.sync, "gpsimd": nc.gpsimd, "vector": nc.vector,
                  "scalar": nc.scalar, "tensor": nc.tensor}[cur]

    def on(self, eng):
        return eng == self.cur

    def w(self, eng, sem, val):
        if not val or val <= 0:
            return
        if not self.on(eng):
            return
        if val <= self.wm[sem]:
            return
        self.wm[sem] = val
        self.E.wait_ge(self.h[sem], val)

    def do(self, eng, fn, sem=None, inc=1):
        if self.on(eng):
            inst = fn(self.E)
            if sem is not None and inst is not None:
                inst.then_inc(self.h[sem], inc)
        if sem is not None:
            self.cnt[sem] += inc
            return self.cnt[sem]
        return None

    def dma(self, qeng, sem, out, in_):
        assert sem in SEMS and sem[0] == "q", sem
        if self.on(qeng):
            self.E.dma_start(out=out, in_=in_).then_inc(self.h[sem], 16)
        self.cnt[sem] += 16
        return self.cnt[sem]

    def join(self):
        own = {"vector": "dve", "scalar": "act", "tensor": "pe"}.get(self.cur)
        if own is not None and self.cnt[own] > self.wm[own]:
            self.wm[own] = self.cnt[own]
            self.E.wait_ge(self.h[own], self.cnt[own])
        self.cnt["ph"] += 5
        tgt = self.cnt["ph"]
        self.E.nop().then_inc(self.h["ph"], 1)
        self.E.wait_ge(self.h["ph"], tgt)
        for s in ("pe", "dve", "act", "ph"):
            self.wm[s] = max(self.wm[s], self.cnt[s])


def _rope_tables_fp32():
    try:
        import jax
        import jax.numpy as jnp
        cpu = jax.devices("cpu")[0]
        with jax.default_device(cpu):
            inv = 1.0 / (10000.0 ** (jnp.arange(0, RD, 2, dtype=jnp.float32) / RD))
            t = jnp.arange(S, dtype=jnp.float32)
            fr = jnp.outer(t, inv)
            c = np.asarray(jnp.cos(fr), dtype=np.float32)
            s = np.asarray(jnp.sin(fr), dtype=np.float32)
        return c, s
    except Exception:
        inv = (1.0 / (10000.0 ** (np.arange(0, RD, 2, dtype=np.float32) /
                                  np.float32(RD)))).astype(np.float32)
        t = np.arange(S, dtype=np.float32)
        fr = np.outer(t, inv).astype(np.float32)
        return np.cos(fr).astype(np.float32), np.sin(fr).astype(np.float32)


def build_nc():
    nc = bass.Bass("TRN2", target_bir_lowering=False)
    g = {}

    def din(name, shape, dtype=F32):
        g[name] = nc.dram_tensor(name, shape, dtype, kind="ExternalInput").ap()

    din("x_own", [R, D]);      din("xT_c", [D, R]);      din("x_rowc", [R, D])
    din("w_dq", [D, QL]);      din("w_uq", [QL, NH * 192]); din("gq_col", [128, 6])
    din("w_dkv", [D, KVL]);    din("w_kr", [D, RD]);     din("gkv_col", [128, 4])
    din("w_uk_own", [KVL, 2 * NOPE]); din("w_uv_own", [KVL, 2 * VD]); din("w_o", [D, D])
    U8 = mybir.dt.uint8
    din("wgTr", [128, 64, 16, 128], U8)
    din("wuTr", [128, 64, 16, 128], U8)
    din("wdTr", [128, 16, 64, 128], U8)
    din("sg_col", [128, 64]);  din("su_col", [128, 64]); din("sd_col", [128, 16])
    din("cos_kc", [R, 32]);    din("sin_kc", [R, 32])
    din("ctab", [RD, R]);      din("stab", [RD, R])
    din("mask16", [128, 16]);  din("idn", [128, 128]);  din("idn2", [128, RD])

    g["y"] = nc.dram_tensor("y", [R, D], F32, kind="ExternalOutput").ap()
    g["xnT_sp"] = nc.dram_tensor("xnT_sp", [128, 16, R], BF16, kind="Internal").ap()
    g["attn_sp"] = nc.dram_tensor("attn_sp", [R, D], BF16, kind="Internal").ap()
    g["kv2_sh"] = nc.dram_tensor("kv2_sh", [128, 5 * R], BF16, kind="Internal").ap()
    g["kv2_gx"] = nc.dram_tensor("kv2_gx", [8, 128, 5 * R], BF16,
                                 kind="Internal", addr_space="Shared").ap()
    for j in range(2):
        g["kv3_sh%d" % j] = nc.dram_tensor("kv3_sh%d" % j, [128, 2 * S], BF16,
                                           kind="Internal").ap()
        g["kv3_gx%d" % j] = nc.dram_tensor("kv3_gx%d" % j, [8, 128, 2 * S], BF16,
                                           kind="Internal", addr_space="Shared").ap()

    P_ = {}

    def sp(name, shape, dtype):
        P_[name] = nc.alloc_sbuf_tensor(name, shape, dtype)

    sp("ones_col", [128, 1], BF16)
    sp("ones_row", [1, 128], BF16)
    sp("neg128_row", [1, 128], BF16)
    sp("idn_f", [128, 128], F32)
    sp("idn_b", [128, 128], BF16)
    sp("idn2_f", [128, RD], F32)
    sp("expb", [128, 1], F32)
    sp("idn2_b", [128, RD], BF16)
    sp("zro", [128, 512], BF16)
    sp("mask_b", [128, 16], BF16)
    sp("gq_sb", [128, 6], F32)
    sp("gkv_sb", [128, 4], F32)
    sp("sg_sb", [128, 64], F32)
    sp("su_sb", [128, 64], F32)
    sp("sd_sb", [128, 16], F32)
    sp("ncol_dkv", [1, KVL], BF16)
    sp("ncol_kr", [1, 128], BF16)

    pb = [nc.alloc_psum_tensor("pb%d" % i, [128, 512], F32).ap() for i in range(7)]
    psD = nc.alloc_psum_tensor("psD", [128, 512], F32).ap()

    def pbv(i):
        return pb[i].bitcast(BF16)

    ARENA = 200 * KB
    arena_t = nc.alloc_sbuf_tensor("arena", [128, ARENA], mybir.dt.uint8)
    abase = nc.lookup_mloc(arena_t).addr
    _seen = []

    def at(name, off_kb, shape, dtype):
        off = int(round(off_kb * KB))
        nb = int(np.prod(shape[1:])) * mybir.dt.size(dtype)
        assert off % 32 == 0 and off + nb <= ARENA, (name, off_kb, nb)
        _seen.append((name, off, nb))
        return nc.alloc_sbuf_tensor_at(name, shape, dtype, offset=abase + off).ap()

    kvT = at("kvT", 0, [128, 4, S], BF16)
    kpeT = at("kpeT", 64, [128, S], BF16)
    QnA = at("QnA", 80, [128, NH, R], BF16)
    QrA = at("QrA", 112, [128, 8, R], BF16)
    atT = at("atT", 128, [128, NH, R], BF16)
    wdkv_b = at("wdkv_b", 184, [128, 16, KVL], BF16)
    wkr_b = at("wkr_b", 180, [128, 16, 128], BF16)
    xo_st = [at("xo_st0", 0, [128, D], F32), at("xo_st1", 8, [128, D], F32)]
    xn_t = [at("xn_t0", 16, [128, D], BF16), at("xn_t1", 20, [128, D], BF16)]
    xnT = at("xnT", 24, [128, 16, R], BF16)
    qlnT = at("qlnT", 56, [128, 6, R], BF16)
    qr_tmp = at("qr_tmp", 72, [RD, R], BF16)
    st1 = at("st1", 74, [128, 64], F32)
    qr_prod = at("qr_prod", 75, [128, R], BF16)
    tabs_f = at("tabs_f", 176, [128, R], F32)
    tabs_b = at("tabs_b", 170, [128, R], BF16)
    wdq_b = at("wdq_b", 128, [128, 16, QL], BF16)
    wuq_st = [at("wuq_st0", 152, [128, 6, 192], BF16),
              at("wuq_st1", 156.5, [128, 6, 192], BF16)]
    wuq_b = [at("wuq_b0", 161, [128, 6, 256], BF16),
             at("wuq_b1", 164, [128, 6, 256], BF16)]
    ql_t = [at("ql_t0", 167, [128, QL], BF16), at("ql_t1", 168.5, [128, QL], BF16)]
    xT_st = [at("xT_st0", 128, [128, 16, 256], BF16),
             at("xT_st1", 136, [128, 16, 256], BF16),
             at("xT_st2", 144, [128, 16, 256], BF16)]
    xr_st = [at("xr_st0", 152, [128, D], BF16), at("xr_st1", 156, [128, D], BF16)]
    ck_st = [at("ck_st0", 160, [128, 4, 32], BF16), at("ck_st1", 160.25, [128, 4, 32], BF16)]
    sk_st = [at("sk_st0", 160.5, [128, 4, 32], BF16), at("sk_st1", 160.75, [128, 4, 32], BF16)]
    kv_b = [at("kv_b0", 161, [128, KVL], BF16), at("kv_b1", 162, [128, KVL], BF16)]
    kpe_sb = at("kpe_sb", 163, [128, 4, 128], BF16)
    rtmp = [at("rtmp0", 164, [128, 4, 32], BF16), at("rtmp1", 164.25, [128, 4, 32], BF16)]
    muT_row = at("muT_row", 166, [1, 512], BF16)
    mu_bf = at("mu_bf", 167.25, [128, 8], BF16)
    st2 = at("st2", 165, [128, 64], F32)
    st2b = at("st2b", 165.25, [128, 16], F32)
    kv_own = at("kv_own", 168, [128, 4, R], BF16)
    kpe_own = at("kpe_own", 176, [128, R], BF16)
    wuk_st1 = at("wuk_st1", 160, [128, 4, 128], F32)
    wuv_st1 = at("wuv_st1", 162, [128, 4, 128], F32)
    wuk_b1 = at("wuk_b1", 164, [128, 4, 128], BF16)
    wuv_b1 = at("wuv_b1", 165, [128, 4, 128], BF16)
    Kst = at("Kst", 166, [128, S], BF16)
    Vst = at("Vst", 182, [128, 64, 128], BF16)
    K_sb = [at("K_sb0", 160, [128, 512], BF16), at("K_sb1", 161, [128, 512], BF16)]
    V4_sb = [at("V4_sb0", 162, [128, 4, 128], BF16), at("V4_sb1", 163, [128, 4, 128], BF16)]
    P_sb = [at("P_sb0", 164, [128, R], BF16), at("P_sb1", 166, [128, R], BF16)]
    den_f = at("den_f", 168, [1, R], F32)
    rec_b = at("rec_b", 172, [1, R], BF16)
    bc_f = at("bc_f", 174, [128, R], F32)
    wo_b = at("wo_b", 0, [128, 16, KB], BF16)
    ao_st = [at("ao_st0", 32, [128, KB], BF16), at("ao_st1", 34, [128, KB], BF16)]
    xnT5 = at("xnT5", 0, [128, 16, R], BF16)
    hT = at("hT", 32, [128, 16, R], BF16)
    acc = at("acc", 64, [128, 16, R], F32)
    wg_b = [at("wg_b0", 128, [128, 16, 128], BF16), at("wg_b1", 132, [128, 16, 128], BF16)]
    wu_b = [at("wu_b0", 136, [128, 16, 128], BF16), at("wu_b1", 140, [128, 16, 128], BF16)]
    wd_b = [at("wd_b0", 144, [128, 16, 128], BF16), at("wd_b1", 148, [128, 16, 128], BF16)]
    hs_t = at("hs_t", 152, [128, KB], BF16)
    x5_st = [at("x5_st0", 154, [128, D], F32), at("x5_st1", 162, [128, D], F32)]
    at_st = [at("at_st0", 170, [128, D], BF16), at("at_st1", 174, [128, D], BF16)]
    out_st = [at("out_st0", 178, [128, D], F32), at("out_st1", 186, [128, D], F32)]
    rowsum_b = at("rowsum_b", 154, [1, R], BF16)
    hsum_hi = at("hsum_hi", 156, [1, R], BF16)
    hsum_lo = at("hsum_lo", 158, [1, R], BF16)
    hsum_f = at("hsum_f", 160, [1, R], F32)

    semh = {s: nc.alloc_semaphore(name=s) for s in SEMS}

    def emit(P):
        CK = {}

        def dvs(fn):
            P.w("vector", "dve", P.cnt["dve"])
            return P.do("vector", fn, "dve")

        def acs(fn):
            P.w("scalar", "act", P.cnt["act"])
            return P.do("scalar", fn, "act")

        P.dma("sync", "qxC", P_["idn_f"][:], g["idn"])
        P.dma("sync", "qxC", P_["gq_sb"][:], g["gq_col"])
        P.dma("sync", "qxC", P_["gkv_sb"][:], g["gkv_col"])
        P.dma("sync", "qxC", P_["sg_sb"][:], g["sg_col"])
        P.dma("sync", "qxC", P_["su_sb"][:], g["su_col"])
        P.dma("sync", "qxC", P_["sd_sb"][:], g["sd_col"])
        P.dma("sync", "qxC", tabs_f[0:RD, :], g["ctab"])
        P.dma("sync", "qxC", tabs_f[RD:128, :], g["stab"])
        v_sync0 = P.dma("sync", "qxC", P_["idn2_f"][:], g["idn2"])
        v_tab = v_idn2 = v_idn = v_sync0
        P.dma("gpsimd", "qwE", P_["mask_b"][:], g["mask16"])
        P.dma("gpsimd", "qwE", wdq_b[:], g["w_dq"].rearrange("(c p) n -> p c n", p=128))
        P.dma("gpsimd", "qwE", wdkv_b[:], g["w_dkv"].rearrange("(c p) n -> p c n", p=128))
        P.dma("gpsimd", "qwE", wkr_b[:, :, 0:RD], g["w_kr"].rearrange("(c p) n -> p c n", p=128))
        w_kr_ld = P.dma("gpsimd", "qwE", wkr_b[:, :, RD:128],
                        g["w_kr"].rearrange("(c p) n -> p c n", p=128))
        w_dkv_ld = w_kr_ld
        P.do("vector", lambda e: e.memset(P_["ones_col"][:], 1.0))
        P.do("vector", lambda e: e.memset(P_["ones_row"][:], 1.0))
        P.do("vector", lambda e: e.memset(P_["neg128_row"][:], -128.0))
        P.do("vector", lambda e: e.memset(P_["zro"][:], 0.0))
        ms_ck = P.do("vector", lambda e: e.memset(P_["expb"][:], EXPB), "dve")
        P.w("tensor", "dve", ms_ck)
        P.w("vector", "qxC", v_idn)
        P.do("vector", lambda e: e.tensor_copy(out=P_["idn_b"][:], in_=P_["idn_f"][:]), "dve")
        P.w("vector", "qxC", v_idn2)
        P.do("vector", lambda e: e.tensor_copy(out=P_["idn2_b"][:], in_=P_["idn2_f"][:]), "dve")
        P.w("vector", "qxC", v_tab)
        P.do("vector", lambda e: e.tensor_copy(out=tabs_b[:], in_=tabs_f[:]), "dve")
        P.w("scalar", "qxC", v_sync0)
        P.w("vector", "qwE", w_kr_ld)
        P.w("tensor", "qwE", w_dkv_ld)
        m1 = 0
        for c in range(16):
            m1 = P.do("tensor", lambda e, c=c: e.matmul(pb[0][0:1, 0:KVL], P_["ones_col"][:],
                                                        wdkv_b[:, c, :], start=(c == 0),
                                                        stop=(c == 15)),
                      "pe" if c == 15 else None)
        P.w("tensor", "qwE", w_kr_ld)
        m2 = 0
        for c in range(16):
            m2 = P.do("tensor", lambda e, c=c: e.matmul(pb[1][0:1, 0:128], P_["ones_col"][:],
                                                        wkr_b[:, c, :], start=(c == 0),
                                                        stop=(c == 15)),
                      "pe" if c == 15 else None)
        P.w("scalar", "pe", m1)
        P.do("scalar", lambda e: e.activation(out=P_["ncol_dkv"][:], in_=pb[0][0:1, 0:KVL],
                                              func=AF.Copy, scale=-1.0), "act")
        P.w("scalar", "pe", m2)
        P.do("scalar", lambda e: e.activation(out=P_["ncol_kr"][:], in_=pb[1][0:1, 0:128],
                                              func=AF.Copy, scale=-1.0), "act")
        P.join()

        apply_ck = [0] * 8
        tr_pe = [0] * 8
        tr_ck = [0] * 8
        xo_dma = [0] * 8
        for t in range(8):
            if t >= 2:
                P.w("sync", "dve", apply_ck[t - 2])
            xo_dma[t] = P.dma("sync", "qxA" if t % 2 == 0 else "qxB",
                              xo_st[t % 2][:], g["x_own"][ts(t, 128), :])
            P.w("vector", "qxA" if t % 2 == 0 else "qxB", xo_dma[t])
            rs_ck = P.do("vector", lambda e, t=t: e.reduce_sum(st1[:, t:t + 1], xo_st[t % 2][:],
                                                               axis=AX.X), "dve")
            P.w("scalar", "qxA" if t % 2 == 0 else "qxB", xo_dma[t])
            if t >= 2:
                P.w("scalar", "pe", tr_pe[t - 2])
            P.do("scalar", lambda e, t=t: e.activation(out=xn_t[t % 2][:], in_=xo_st[t % 2][:],
                                                       func=AF.Square,
                                                       accum_out=st1[:, 8 + t:9 + t]), "act")
            sq_ck = P.cnt["act"]
            P.w("vector", "act", sq_ck)
            dvs(lambda e, t=t: e.tensor_scalar(out=st1[:, 16 + t:17 + t],
                                               in0=st1[:, t:t + 1], scalar1=1.0 / D,
                                               scalar2=None, op0=ALU.mult))
            dvs(lambda e, t=t: e.tensor_scalar(out=st1[:, 40 + t:41 + t],
                                               in0=st1[:, 8 + t:9 + t], scalar1=1.0 / D,
                                               scalar2=1e-5, op0=ALU.mult, op1=ALU.add))
            dvs(lambda e, t=t: e.tensor_tensor(out=st1[:, 48 + t:49 + t],
                                               in0=st1[:, 16 + t:17 + t],
                                               in1=st1[:, 16 + t:17 + t], op=ALU.mult))
            dv = dvs(lambda e, t=t: e.tensor_tensor(out=st1[:, 40 + t:41 + t],
                                                    in0=st1[:, 40 + t:41 + t],
                                                    in1=st1[:, 48 + t:49 + t],
                                                    op=ALU.subtract))
            P.w("scalar", "dve", dv)
            sr = acs(lambda e, t=t: e.activation(out=st1[:, 48 + t:49 + t],
                                                 in_=st1[:, 40 + t:41 + t],
                                                 func=AF.Sqrt))
            P.w("vector", "act", sr)
            rcp = dvs(lambda e, t=t: e.reciprocal(out=st1[:, 24 + t:25 + t],
                                                  in_=st1[:, 48 + t:49 + t]))
            if t >= 2:
                P.w("vector", "pe", tr_pe[t - 2])
            apply_ck[t] = dvs(lambda e, t=t: e.tensor_scalar(
                out=xn_t[t % 2][:], in0=xo_st[t % 2][:],
                scalar1=st1[:, 16 + t:17 + t], scalar2=st1[:, 24 + t:25 + t],
                op0=ALU.subtract, op1=ALU.mult))
            P.w("tensor", "dve", apply_ck[t])
            if t >= 1:
                P.w("tensor", "dve", tr_ck[t - 1])
            for j in range(16):
                v = P.do("tensor", lambda e, t=t, j=j: e.transpose(
                    pbv(4 + j // 8)[:, ts(j % 8, 128)], xn_t[t % 2][:, ts(j, 128)],
                    P_["idn_b"][:]),
                    "pe" if j == 15 else None)
                if j == 15:
                    tr_pe[t] = v
            P.w("vector", "pe", tr_pe[t])
            P.do("vector", lambda e, t=t: e.tensor_copy(
                out=xnT[:, 0:8, ts(t, 128)],
                in_=pbv(4).rearrange("p (c r) -> p c r", c=8)))
            tr_ck[t] = P.do("vector", lambda e, t=t: e.tensor_copy(
                out=xnT[:, 8:16, ts(t, 128)],
                in_=pbv(5).rearrange("p (c r) -> p c r", c=8)), "dve")
        qln_pe = [0] * 8
        qln_ck = [0] * 8
        ap2_ck = [0] * 8
        for t in range(8):
            P.w("tensor", "dve", tr_ck[7] if t == 0 else tr_ck[t])
            if t >= 2:
                P.w("tensor", "act", ap2_ck[t - 2])
            mmq = 0
            for j in range(16):
                P.do("tensor", lambda e, t=t, j=j: e.matmul(pb[0][:, 0:512], xnT[:, j, ts(t, 128)],
                                                            wdq_b[:, j, 0:512],
                                                            start=(j == 0), stop=(j == 15)))
                mmq = P.do("tensor", lambda e, t=t, j=j: e.matmul(pb[1][:, 0:256],
                                                                  xnT[:, j, ts(t, 128)],
                                                                  wdq_b[:, j, 512:768],
                                                                  start=(j == 0), stop=(j == 15)),
                           "pe" if j == 15 else None)
            P.w("scalar", "pe", mmq)
            if t >= 2:
                P.w("scalar", "pe", qln_pe[t - 2])
            P.do("scalar", lambda e, t=t: e.activation(out=ql_t[t % 2][:, 0:512], in_=pb[0][:, 0:512],
                                                       func=AF.Square, accum_out=st1[:, t:t + 1]), "act")
            s2 = P.do("scalar", lambda e, t=t: e.activation(out=ql_t[t % 2][:, 512:768],
                                                            in_=pb[1][:, 0:256], func=AF.Square,
                                                            accum_out=st1[:, 8 + t:9 + t]), "act")
            P.w("vector", "act", s2)
            dvs(lambda e, t=t: e.tensor_tensor(out=st1[:, t:t + 1], in0=st1[:, t:t + 1],
                                               in1=st1[:, 8 + t:9 + t], op=ALU.add))
            dv2 = dvs(lambda e, t=t: e.tensor_scalar(out=st1[:, t:t + 1],
                                                     in0=st1[:, t:t + 1],
                                                     scalar1=1.0 / QL, scalar2=1e-6,
                                                     op0=ALU.mult, op1=ALU.add))
            P.w("scalar", "dve", dv2)
            sr2 = acs(lambda e, t=t: e.activation(out=st1[:, 8 + t:9 + t],
                                                  in_=st1[:, t:t + 1], func=AF.Sqrt))
            P.w("vector", "act", sr2)
            rc2 = dvs(lambda e, t=t: e.reciprocal(out=st1[:, 16 + t:17 + t],
                                                  in_=st1[:, 8 + t:9 + t]))
            P.w("scalar", "dve", rc2)
            acs(lambda e, t=t: e.activation(out=ql_t[t % 2][:, 0:512], in_=pb[0][:, 0:512],
                                            func=AF.Copy, scale=st1[:, 16 + t:17 + t]))
            ap2_ck[t] = acs(lambda e, t=t: e.activation(
                out=ql_t[t % 2][:, 512:768], in_=pb[1][:, 0:256],
                func=AF.Copy, scale=st1[:, 16 + t:17 + t]))
            P.w("tensor", "act", ap2_ck[t])
            if t >= 1:
                P.w("tensor", "dve", qln_ck[t - 1])
            for j in range(6):
                v = P.do("tensor", lambda e, t=t, j=j: e.transpose(
                    pbv(4)[:, ts(j, 128)], ql_t[t % 2][:, ts(j, 128)], P_["idn_b"][:]),
                    "pe" if j == 5 else None)
                if j == 5:
                    qln_pe[t] = v
            P.w("vector", "pe", qln_pe[t])
            qln_ck[t] = P.do("vector", lambda e, t=t: e.tensor_copy(
                out=qlnT[:, :, ts(t, 128)],
                in_=pbv(4)[:, 0:768].rearrange("p (c r) -> p c r", c=6)), "dve")
        P.w("scalar", "dve", tr_ck[7])
        sp_ck = P.dma("scalar", "qsC", g["xnT_sp"][:, :, :], xnT[:, :, :])
        CK["xnT_spill"] = sp_ck
        wq_dma = [0, 0]
        cvtq = [0, 0]
        q_done = [0] * NH
        for h in range(NH):
            par = h % 2
            P.w("gpsimd", "dve", cvtq[par])
            wq_dma[par] = P.dma("gpsimd", "qwA" if par == 0 else "qwB", wuq_st[par][:],
                                g["w_uq"].rearrange("(c p) n -> p c n", p=128)[:, :, ts(h, 192)])
            P.w("vector", "qwA" if par == 0 else "qwB", wq_dma[par])
            if h >= 2:
                P.w("vector", "pe", CK.get(("qmm", h - 2), 0))
            ts6 = 0
            for j in range(6):
                ts6 = P.do("vector", lambda e, j=j, par=par: e.tensor_scalar(
                    out=wuq_b[par][:, j, 0:192], in0=wuq_st[par][:, j, :],
                    scalar1=P_["gq_sb"][:, j:j + 1], scalar2=SCALE, op0=ALU.mult, op1=ALU.mult),
                    "dve" if j == 5 else None)
            P.w("vector", "dve", ts6)
            P.do("vector", lambda e, par=par: e.tensor_copy(out=wuq_b[par][:, :, 192:224],
                                                            in_=wuq_b[par][:, :, 160:192]))
            cvtq[par] = P.do("vector", lambda e, par=par: e.tensor_copy(
                out=wuq_b[par][:, :, 224:256], in_=wuq_b[par][:, :, 128:160]), "dve")
            P.w("tensor", "dve", cvtq[par])
            if h == 0:
                P.w("tensor", "dve", qln_ck[7])
            else:
                P.w("tensor", "dve", q_done[h - 1])
            qn_mm = 0
            for j in range(6):
                for nn in range(2):
                    qn_mm = P.do("tensor", lambda e, j=j, par=par, nn=nn: e.matmul(
                        pb[2 + nn][:, 0:512], wuq_b[par][:, j, 0:128],
                        qlnT[:, j, ts(nn, 512)], start=(j == 0), stop=(j == 5)),
                        "pe" if (j == 5 and nn == 1) else None)
            qr_mm = 0
            for j in range(6):
                for nn in range(2):
                    qr_mm = P.do("tensor", lambda e, j=j, par=par, nn=nn: e.matmul(
                        pb[4 + nn][:, 0:512], wuq_b[par][:, j, 128:256],
                        qlnT[:, j, ts(nn, 512)], start=(j == 0), stop=(j == 5)),
                        "pe" if (j == 5 and nn == 1) else None)
            CK[("qmm", h)] = qr_mm
            P.w("vector", "pe", qn_mm)
            P.do("vector", lambda e, h=h: e.tensor_copy(out=QnA[:, h, 0:512], in_=pb[2][:, 0:512]))
            P.do("vector", lambda e, h=h: e.tensor_copy(out=QnA[:, h, 512:1024], in_=pb[3][:, 0:512]))
            P.w("vector", "pe", qr_mm)
            P.do("vector", lambda e: e.tensor_tensor(
                out=qr_prod[:, 0:512], in0=pb[4][:, 0:512], in1=tabs_b[:, 0:512], op=ALU.mult))
            prod_ck = P.do("vector", lambda e: e.tensor_tensor(
                out=qr_prod[:, 512:1024], in0=pb[5][:, 0:512], in1=tabs_b[:, 512:1024],
                op=ALU.mult), "dve")
            P.w("tensor", "dve", prod_ck)
            P.do("tensor", lambda e: e.matmul(pb[6][0:RD, 0:512], P_["idn2_b"][:],
                                              qr_prod[:, 0:512], start=True, stop=True))
            fold_mm = P.do("tensor", lambda e: e.matmul(psD[0:RD, 0:512], P_["idn2_b"][:],
                                                        qr_prod[:, 512:1024],
                                                        start=True, stop=True), "pe")
            P.w("vector", "pe", fold_mm)
            if h >= 1:
                P.w("vector", "qxC", CK[("qrd", h - 1)])
            P.do("vector", lambda e: e.tensor_copy(out=qr_tmp[:, 0:512], in_=pb[6][0:RD, 0:512]))
            q_done[h] = P.do("vector", lambda e: e.tensor_copy(out=qr_tmp[:, 512:1024],
                                                               in_=psD[0:RD, 0:512]), "dve")
            P.w("sync", "dve", q_done[h])
            CK[("qrd", h)] = P.dma("sync", "qxC", QrA[RD * (h % 2):RD * (h % 2) + RD, h // 2, :],
                                   qr_tmp[:])
        CK["qra_done"] = CK[("qrd", NH - 1)]
        P.join()

        P.w("vector", "qsC", CK["xnT_spill"])
        P.w("vector", "qxC", CK["qra_done"])
        kv_apply = [0, 0]
        xT_free = [0, 0, 0]
        xr_free = [0, 0]
        ckst_free = [0, 0]
        tr_evac = 0
        kpe_evac = 0
        muT_ack = 0
        kvb_pe = [0, 0]
        rs_dve = [0, 0]
        for t in range(2):
            for half in range(2):
                slot = (2 * t + half) % 3
                P.w("gpsimd", "pe", xT_free[slot])
                CK[("xT", t, half)] = P.dma(
                    "gpsimd", "qwT%d" % slot, xT_st[slot][:],
                    g["xT_c"].rearrange("(c p) n -> p c n", p=128)[:, :, 512 * t + 256 * half:
                                                                  512 * t + 256 * half + 256])
            P.w("gpsimd", "dve", ckst_free[t % 2])
            P.dma("gpsimd", "qwA" if t % 2 == 0 else "qwB", ck_st[t % 2][:],
                  g["cos_kc"].rearrange("(s p) r -> p s r", p=128)[:, 4 * t:4 * t + 4, :])
            tab_dma = P.dma("gpsimd", "qwA" if t % 2 == 0 else "qwB", sk_st[t % 2][:],
                            g["sin_kc"].rearrange("(s p) r -> p s r", p=128)[:, 4 * t:4 * t + 4, :])
            for s in range(4):
                sl = (4 * t + s) % 2
                P.w("gpsimd", "act", xr_free[sl])
                xr_dma = P.dma("gpsimd", "qwR%d" % sl, xr_st[sl][:],
                               g["x_rowc"][512 * t + 128 * s:512 * t + 128 * s + 128, :])
                P.w("vector", "qwR%d" % sl, xr_dma)
                red = P.do("vector", lambda e, sl=sl, s=s: e.reduce_sum(
                    st2[:, s:s + 1], xr_st[sl][:], axis=AX.X), "dve")
                P.w("scalar", "qwR%d" % sl, xr_dma)
                P.w("scalar", "dve", red)
                xr_free[sl] = P.do("scalar", lambda e, sl=sl, s=s: e.activation(
                    out=xr_st[sl][:], in_=xr_st[sl][:], func=AF.Square,
                    accum_out=st2[:, 4 + s:5 + s]), "act")
            sq_ck = P.cnt["act"]
            P.w("vector", "act", sq_ck)
            dvs(lambda e: e.tensor_scalar(out=st2[:, 8:12], in0=st2[:, 0:4],
                                          scalar1=1.0 / D, scalar2=None, op0=ALU.mult))
            dvs(lambda e: e.tensor_scalar(out=st2[:, 16:20], in0=st2[:, 4:8],
                                          scalar1=1.0 / D, scalar2=1e-5,
                                          op0=ALU.mult, op1=ALU.add))
            dvs(lambda e: e.tensor_tensor(out=st2[:, 20:24], in0=st2[:, 8:12],
                                          in1=st2[:, 8:12], op=ALU.mult))
            dvs(lambda e: e.tensor_tensor(out=st2[:, 16:20], in0=st2[:, 16:20],
                                          in1=st2[:, 20:24], op=ALU.subtract))
            dv_e = dvs(lambda e: e.tensor_scalar(out=st2[:, 24:28], in0=st2[:, 16:20],
                                                 scalar1=1e-6, scalar2=None, op0=ALU.mult))
            P.w("scalar", "dve", dv_e)
            srs = acs(lambda e: e.activation(out=st2[:, 20:24], in_=st2[:, 16:20],
                                             func=AF.Sqrt))
            P.w("vector", "act", srs)
            invs = dvs(lambda e: e.reciprocal(out=st2[:, 12:16], in_=st2[:, 20:24]))
            mu_cast = dvs(lambda e: e.tensor_copy(out=mu_bf[:, 0:4], in_=st2[:, 8:12]))
            P.w("tensor", "dve", mu_cast)
            P.w("tensor", "act", muT_ack)
            trm = 0
            for s in range(4):
                trm = P.do("tensor", lambda e, s=s: e.matmul(
                    pb[3][0:1, ts(s, 128)], mu_bf[:, s:s + 1],
                    P_["idn_b"][:], start=True, stop=True), "pe" if s == 3 else None)
            P.w("scalar", "pe", trm)
            P.w("scalar", "pe", CK.get(("rank1_pe", t - 1), 0))
            muT_ack = P.do("scalar", lambda e: e.activation(out=muT_row[:], in_=pb[3][0:1, 0:512],
                                                            func=AF.Copy), "act")
            for s in range(4):
                par = s % 2
                pkv = pb[par]
                pkp = pb[2] if par == 0 else pb[6]
                P.w("tensor", "qwT%d" % ((2 * t + s // 2) % 3), CK[("xT", t, s // 2)])
                P.w("tensor", "act", kv_apply[par])
                P.w("tensor", "act", muT_ack)
                for c in range(16):
                    P.do("tensor", lambda e, t=t, s=s, c=c, pkv=pkv: e.matmul(
                        pkv[:, 0:512], xT_st[(2 * t + s // 2) % 3][:, c, ts(s % 2, 128)],
                        wdkv_b[:, c, :], start=(c == 0), stop=False))
                kv_mm = P.do("tensor", lambda e, s=s, pkv=pkv: e.matmul(
                    pkv[:, 0:512], muT_row[0:1, ts(s, 128)], P_["ncol_dkv"][:],
                    start=False, stop=True), "pe")
                for c in range(16):
                    P.do("tensor", lambda e, t=t, s=s, c=c, pkp=pkp: e.matmul(
                        pkp[:, 0:128], xT_st[(2 * t + s // 2) % 3][:, c, ts(s % 2, 128)],
                        wkr_b[:, c, :], start=(c == 0), stop=False))
                kpe_mm = P.do("tensor", lambda e, s=s, pkp=pkp: e.matmul(
                    pkp[:, 0:128], muT_row[0:1, ts(s, 128)], P_["ncol_kr"][:],
                    start=False, stop=True), "pe")
                if s == 3:
                    xT_free[(2 * t) % 3] = P.cnt["pe"]
                    xT_free[(2 * t + 1) % 3] = P.cnt["pe"]
                    CK[("rank1_pe", t)] = P.cnt["pe"]
                P.w("scalar", "pe", kv_mm)
                if (4 * t + s) >= 2:
                    P.w("scalar", "pe", kvb_pe[par])
                P.do("scalar", lambda e, par=par, pkv=pkv, s=s: e.activation(
                    out=kv_b[par][:], in_=pkv[:, 0:512], func=AF.Square,
                    accum_out=st2b[:, 4 * s:4 * s + 1]), "act")
                sq2 = P.cnt["act"]
                P.w("vector", "act", sq2)
                dvs(lambda e, s=s: e.tensor_scalar(
                    out=st2b[:, 4 * s + 1:4 * s + 2], in0=st2b[:, 4 * s:4 * s + 1],
                    scalar1=1.0 / KVL, scalar2=None, op0=ALU.mult))
                dvv = dvs(lambda e, s=s: e.tensor_tensor(
                    out=st2b[:, 4 * s + 1:4 * s + 2], in0=st2b[:, 4 * s + 1:4 * s + 2],
                    in1=st2[:, 24 + s:25 + s], op=ALU.add))
                P.w("scalar", "dve", dvv)
                sr2 = acs(lambda e, s=s: e.activation(
                    out=st2b[:, 4 * s + 2:4 * s + 3], in_=st2b[:, 4 * s + 1:4 * s + 2],
                    func=AF.Sqrt))
                P.w("vector", "act", sr2)
                rc2 = dvs(lambda e, s=s: e.reciprocal(
                    out=st2b[:, 4 * s + 3:4 * s + 4], in_=st2b[:, 4 * s + 2:4 * s + 3]))
                P.w("scalar", "dve", rc2)
                kv_apply[par] = acs(lambda e, par=par, pkv=pkv, s=s: e.activation(
                    out=kv_b[par][:], in_=pkv[:, 0:512], func=AF.Copy,
                    scale=st2b[:, 4 * s + 3:4 * s + 4]))
                P.w("scalar", "pe", kpe_mm)
                if s == 0:
                    P.w("scalar", "dve", CK.get(("rope", t - 1), 0))
                    P.w("scalar", "pe", CK.get(("kpetr", t - 1), 0))
                kpe_ap = P.do("scalar", lambda e, s=s, pkp=pkp: e.activation(
                    out=kpe_sb[:, s, :], in_=pkp[:, 0:128], func=AF.Copy,
                    scale=st2[:, 12 + s:13 + s]), "act")
                if s == 3:
                    CK[("kpe_ap", t)] = kpe_ap
                P.w("tensor", "act", kv_apply[par])
                P.w("tensor", "dve", tr_evac)
                for q in range(4):
                    v = P.do("tensor", lambda e, q=q, par=par: e.transpose(
                        pbv(4)[:, 256 * q + 128 * par:256 * q + 128 * par + 128],
                        kv_b[par][:, ts(q, 128)], P_["idn_b"][:]),
                        "pe" if q == 3 else None)
                kvb_pe[par] = v
                if par == 1:
                    P.w("vector", "pe", kvb_pe[1])
                    tr_evac = P.do("vector", lambda e, t=t, s=s: e.tensor_copy(
                        out=kv_own[:, :, 512 * t + 128 * (s - 1):512 * t + 128 * (s - 1) + 256],
                        in_=pbv(4).rearrange("p (q r) -> p q r", q=4)), "dve")
            P.w("vector", "act", CK[("kpe_ap", t)])
            P.w("vector", "qwA" if t % 2 == 0 else "qwB", tab_dma)

            def _x(i):
                return kpe_sb.rearrange("p s (h r) -> p s h r", h=4)[:, :, i, :]

            ctile = ck_st[t % 2]
            stile = sk_st[t % 2]
            dvs(lambda e: e.tensor_tensor(out=rtmp[0][:], in0=_x(1), in1=stile[:],
                                          op=ALU.mult))
            dvs(lambda e: e.tensor_tensor(out=rtmp[1][:], in0=_x(0), in1=ctile[:],
                                          op=ALU.mult))
            dvs(lambda e: e.tensor_tensor(out=_x(2), in0=_x(2), in1=stile[:],
                                          op=ALU.mult))
            dvs(lambda e: e.tensor_tensor(out=_x(3), in0=_x(3), in1=ctile[:],
                                          op=ALU.mult))
            dvs(lambda e: e.tensor_tensor(out=_x(1), in0=_x(2), in1=_x(3),
                                          op=ALU.add))
            dvs(lambda e: e.tensor_tensor(out=_x(0), in0=rtmp[1][:], in1=rtmp[0][:],
                                          op=ALU.subtract))
            dvs(lambda e: e.tensor_copy(out=_x(2), in_=_x(0)))
            ckst_free[t % 2] = dvs(lambda e: e.tensor_copy(out=_x(3), in_=_x(1)))
            rope_ck = ckst_free[t % 2]
            CK[("rope", t)] = rope_ck
            P.w("tensor", "dve", rope_ck)
            P.w("tensor", "dve", kpe_evac)
            tr2 = 0
            for s in range(4):
                tr2 = P.do("tensor", lambda e, s=s: e.transpose(pbv(5)[:, ts(s, 128)],
                                                                kpe_sb[:, s, :], P_["idn_b"][:]),
                           "pe" if s == 3 else None)
            CK[("kpetr", t)] = tr2
            P.w("vector", "pe", tr2)
            kpe_evac = P.do("vector", lambda e, t=t: e.tensor_copy(
                out=kpe_own[:, ts(t, 512)], in_=pbv(5)[:, 0:512]), "dve")
        P.w("scalar", "dve", max(tr_evac, kpe_evac))
        P.dma("scalar", "qsC", g["kv2_sh"][:, 0:4096],
              kv_own[:].rearrange("p c k -> p (c k)"))
        sp2 = P.dma("scalar", "qsC", g["kv2_sh"][:, 4096:5120], kpe_own[:])
        P.w("gpsimd", "qsC", sp2)
        P.do("gpsimd", lambda e: e.collective_compute(
            "AllGather", ALU.bypass, replica_groups=[list(range(8))],
            ins=[g["kv2_sh"].opt()], outs=[g["kv2_gx"].opt()]), "cc2")
        P.w("sync", "cc2", 1)
        P.w("sync", "qsC", CK["xnT_spill"])
        P.w("sync", "qxC", CK["qra_done"])
        for r in range(8):
            P.dma("sync", "qxC", kvT[:, :, ts(r, 1024)],
                  g["kv2_gx"][r, :, 0:4096].rearrange("p (c k) -> p c k", c=4))
            CK["kv_loaded"] = P.dma("sync", "qxC", kpeT[:, ts(r, 1024)],
                                    g["kv2_gx"][r, :, 4096:5120])
        P.join()

        P.w("tensor", "qxC", CK["kv_loaded"])
        cvt3 = 0
        spill = 0
        prod_pe = 0
        for j in range(2):
            P.w("scalar", "dve", cvt3)
            st_dma = P.dma("scalar", "qsA", wuk_st1[:],
                           g["w_uk_own"].rearrange("(q p) n -> p q n", p=128)[:, :, ts(j, 128)])
            st_dma = P.dma("scalar", "qsA", wuv_st1[:],
                           g["w_uv_own"].rearrange("(q p) n -> p q n", p=128)[:, :, ts(j, 128)])
            P.w("vector", "qsA", st_dma)
            P.w("vector", "pe", prod_pe)
            for q in range(4):
                P.do("vector", lambda e, q=q: e.tensor_scalar(
                    out=wuk_b1[:, q, :], in0=wuk_st1[:, q, :],
                    scalar1=P_["gkv_sb"][:, q:q + 1], scalar2=None, op0=ALU.mult))
            for q in range(4):
                cvt3 = P.do("vector", lambda e, q=q: e.tensor_scalar(
                    out=wuv_b1[:, q, :], in0=wuv_st1[:, q, :],
                    scalar1=P_["gkv_sb"][:, q:q + 1], scalar2=None, op0=ALU.mult),
                    "dve" if q == 3 else None)
            P.w("vector", "qsC", spill)
            kst_ev = [0, 0]
            for gge in range(16):
                bk = 5 + (gge % 2)
                P.w("tensor", "dve", cvt3)
                P.w("tensor", "dve", kst_ev[gge % 2])
                for q in range(4):
                    P.do("tensor", lambda e, q=q, gge=gge, bk=bk: e.matmul(
                        pb[bk][:, 0:512], wuk_b1[:, q, :], kvT[:, q, ts(gge, 512)],
                        start=(q == 0), stop=(q == 3)), "pe" if q == 3 else None)
                P.w("vector", "pe", P.cnt["pe"])
                kst_ev[gge % 2] = P.do("vector", lambda e, gge=gge, bk=bk: e.tensor_copy(
                    out=Kst[:, ts(gge, 512)], in_=pb[bk][:, 0:512]), "dve")
            vst_ev = [kst_ev[0], kst_ev[1]]
            for k in range(64):
                bk = 5 + (k % 2)
                P.w("tensor", "dve", vst_ev[k % 2])
                for q in range(4):
                    P.do("tensor", lambda e, q=q, k=k, bk=bk: e.matmul(
                        pb[bk][:, 0:128], kvT[:, q, ts(k, 128)], wuv_b1[:, q, :],
                        start=(q == 0), stop=(q == 3)), "pe" if q == 3 else None)
                P.w("vector", "pe", P.cnt["pe"])
                vst_ev[k % 2] = P.do("vector", lambda e, k=k, bk=bk: e.tensor_copy(
                    out=Vst[:, k, :], in_=pb[bk][:, 0:128]), "dve")
            prod_pe = P.cnt["pe"]
            P.w("scalar", "dve", max(vst_ev))
            P.dma("scalar", "qsC", g["kv3_sh%d" % j][:, 0:S], Kst[:])
            spill = P.dma("scalar", "qsC", g["kv3_sh%d" % j][:, S:2 * S],
                          Vst[:].rearrange("p t v -> p (t v)"))
            P.w("gpsimd", "qsC", spill)
            P.do("gpsimd", lambda e, j=j: e.collective_compute(
                "AllGather", ALU.bypass, replica_groups=[list(range(8))],
                ins=[g["kv3_sh%d" % j].opt()], outs=[g["kv3_gx%d" % j].opt()]), "cc3")
        P.w("sync", "dve", cvt3)
        P.w("gpsimd", "dve", cvt3)
        P.w("scalar", "qsC", spill)
        P.w("vector", "qsC", spill)
        head_norm = [0] * (NH + 1)
        expk = [0, 0]
        pvk = [0, 0]
        den_evac = 0
        kdma = [0, 0]
        vdma = [0, 0]
        pend = [None]
        tail_h = [None]

        def emit_pv(k, kpar, gpar, s4, mk, npr, hi, gidx, h):
            P.w("tensor", "dve", mk)
            if k == 0:
                P.w("tensor", "dve", CK.get("pv_ev1", 0))
                P.w("tensor", "act", CK.get("psd_free", 0))
            stf = (k == 0)
            if 16 * k < 512:
                na = 512 - 16 * k
                P.do("tensor", lambda e, kpar=kpar, gpar=gpar, s4=s4, k=k, na=na, stf=stf: e.matmul(
                    pb[0][:, 16 * k:512], V4_sb[gpar][:, s4, :], P_sb[kpar][:, 0:na],
                    start=stf, stop=(k == 31)))
                P.do("tensor", lambda e, kpar=kpar, k=k, na=na: e.matmul(
                    psD[0:1, 16 * k:512], P_["ones_col"][:], P_sb[kpar][:, 0:na],
                    start=(k == 0), stop=(k == 31)))
                P.do("tensor", lambda e, kpar=kpar, gpar=gpar, s4=s4, na=na, npr=npr, stf=stf: e.matmul(
                    pb[1][:, 0:512], V4_sb[gpar][:, s4, :], P_sb[kpar][:, na:npr],
                    start=stf, stop=False))
                pvk[kpar] = P.do("tensor", lambda e, kpar=kpar, k=k, na=na, npr=npr: e.matmul(
                    psD[32:33, 0:512], P_["ones_col"][:], P_sb[kpar][:, na:npr],
                    start=(k == 0), stop=False), "pe")
            else:
                P.do("tensor", lambda e, kpar=kpar, gpar=gpar, s4=s4, k=k, npr=npr: e.matmul(
                    pb[1][:, 16 * k - 512:512], V4_sb[gpar][:, s4, :], P_sb[kpar][:, 0:npr],
                    start=False, stop=(k == 63)))
                pvk[kpar] = P.do("tensor", lambda e, kpar=kpar, k=k, npr=npr: e.matmul(
                    psD[32:33, 16 * k - 512:512], P_["ones_col"][:], P_sb[kpar][:, 0:npr],
                    start=False, stop=(k == 63)), "pe")
            if s4 == 3:
                CK[("pvdone", gidx)] = P.cnt["pe"]
            if k == 31:
                P.w("vector", "pe", pvk[kpar])
                CK["pv_ev0"] = P.do("vector", lambda e, h=h: e.tensor_copy(
                    out=atT[:, h, 0:512], in_=pb[0][:, 0:512]), "dve")
            elif k == 63:
                P.w("vector", "pe", pvk[kpar])
                CK["pv_ev1"] = P.do("vector", lambda e, h=h: e.tensor_copy(
                    out=atT[:, h, 512:1024], in_=pb[1][:, 0:512]), "dve")

        def emit_tail(h, hi):
            P.w("scalar", "pe", pvk[0])
            P.w("scalar", "pe", pvk[1])
            acs(lambda e: e.activation(out=rec_b[:, 0:512], in_=psD[0:1, :],
                                       func=AF.Copy))
            den_ev = acs(lambda e: e.activation(out=rec_b[:, 512:1024],
                                                in_=psD[32:33, :], func=AF.Copy))
            P.w("tensor", "act", den_ev)
            P.do("tensor", lambda e: e.matmul(pb[6][:, 0:512], P_["ones_row"][:],
                                              rec_b[:, 0:512], start=True, stop=True))
            bc_mm = P.do("tensor", lambda e: e.matmul(psD[:, 0:512], P_["ones_row"][:],
                                                      rec_b[:, 512:1024],
                                                      start=True, stop=True), "pe")
            P.w("scalar", "pe", bc_mm)
            acs(lambda e: e.activation(out=bc_f[:, 0:512], in_=pb[6][:, 0:512],
                                       func=AF.Ln))
            CK["psd_free"] = acs(lambda e: e.activation(out=bc_f[:, 512:1024],
                                                        in_=psD[:, 0:512],
                                                        func=AF.Ln))
            acs(lambda e: e.activation(out=bc_f[:, 0:512], in_=bc_f[:, 0:512],
                                       func=AF.Exp, scale=-1.0))
            lastx = acs(lambda e: e.activation(out=bc_f[:, 512:1024],
                                               in_=bc_f[:, 512:1024],
                                               func=AF.Exp, scale=-1.0))
            P.w("vector", "act", lastx)
            dvs(lambda e, h=h: e.tensor_tensor(
                out=atT[:, h, 0:512], in0=atT[:, h, 0:512], in1=bc_f[:, 0:512],
                op=ALU.mult))
            head_norm[hi] = dvs(lambda e, h=h: e.tensor_tensor(
                out=atT[:, h, 512:1024], in0=atT[:, h, 512:1024],
                in1=bc_f[:, 512:1024], op=ALU.mult))

        for hi, h in enumerate(list(range(0, NH, 2)) + list(range(1, NH, 2))):
            j = h % 2
            rk = h // 2
            base = RD * (h % 2)
            hslot = h // 2
            P.w("sync", "cc3", j + 1)
            P.w("gpsimd", "cc3", j + 1)
            for gge in range(16):
                gidx = 16 * hi + gge
                gpar = gidx % 2
                P.w("sync", "pe", CK.get(("scdone", gidx - 2), 0))
                kdma[gpar] = P.dma("sync", "qxA" if gpar == 0 else "qxB", K_sb[gpar][:],
                                   g["kv3_gx%d" % j][rk, :, 512 * gge:512 * gge + 512])
                P.w("gpsimd", "pe", CK.get(("pvdone", gidx - 2), 0))
                vdma[gpar] = P.dma("gpsimd", "qwA" if gpar == 0 else "qwB", V4_sb[gpar][:],
                                   g["kv3_gx%d" % j][rk, :, S + 512 * gge:S + 512 * gge + 512]
                                   .rearrange("p (t v) -> p t v", t=4))
                P.w("tensor", "qxA" if gpar == 0 else "qxB", kdma[gpar])
                P.w("tensor", "qwA" if gpar == 0 else "qwB", vdma[gpar])
                for s4 in range(4):
                    k = 4 * gge + s4
                    kpar = k % 2
                    npr = 1024 - 16 * k
                    sb0 = 2 if kpar == 0 else 4
                    P.w("tensor", "act", expk[kpar])
                    n1 = min(npr, 512)
                    P.do("tensor", lambda e, sb0=sb0, gpar=gpar, s4=s4, h=h, k=k, n1=n1: e.matmul(
                        pb[sb0][:, 0:n1], K_sb[gpar][:, ts(s4, 128)],
                        QnA[:, h, 16 * k:16 * k + n1], start=True, stop=False))
                    if npr > 512:
                        P.do("tensor", lambda e, sb0=sb0, gpar=gpar, s4=s4, h=h, k=k, npr=npr: e.matmul(
                            pb[sb0 + 1][:, 0:npr - 512], K_sb[gpar][:, ts(s4, 128)],
                            QnA[:, h, 16 * k + 512:1024], start=True, stop=False))
                    scmm = P.do("tensor", lambda e, sb0=sb0, base=base, hslot=hslot, k=k, n1=n1, npr=npr: e.matmul(
                        pb[sb0][:, 0:n1], kpeT[base:base + RD, ts(k, 128)],
                        QrA[base:base + RD, hslot, 16 * k:16 * k + n1],
                        start=False, stop=True),
                        "pe" if npr <= 512 else None)
                    if npr > 512:
                        scmm = P.do("tensor", lambda e, sb0=sb0, base=base, hslot=hslot, k=k, npr=npr: e.matmul(
                            pb[sb0 + 1][:, 0:npr - 512], kpeT[base:base + RD, ts(k, 128)],
                            QrA[base:base + RD, hslot, 16 * k + 512:1024],
                            start=False, stop=True), "pe")
                    if s4 == 3:
                        CK[("scdone", gidx)] = scmm
                    P.w("scalar", "pe", scmm)
                    P.w("scalar", "pe", pvk[kpar])
                    expk[kpar] = P.do("scalar", lambda e, kpar=kpar, sb0=sb0, n1=n1: e.activation(
                        out=P_sb[kpar][:, 0:n1], in_=pb[sb0][:, 0:n1],
                        func=AF.Exp, bias=P_["expb"][:]), "act")
                    if npr > 512:
                        expk[kpar] = P.do("scalar", lambda e, kpar=kpar, sb0=sb0, npr=npr: e.activation(
                            out=P_sb[kpar][:, 512:npr], in_=pb[sb0 + 1][:, 0:npr - 512],
                            func=AF.Exp, bias=P_["expb"][:]), "act")
                    P.w("vector", "act", expk[kpar])
                    mk = P.do("vector", lambda e, kpar=kpar: e.tensor_tensor(
                        out=P_sb[kpar][:, 0:16], in0=P_sb[kpar][:, 0:16],
                        in1=P_["mask_b"][:], op=ALU.mult), "dve")
                    if tail_h[0] is not None and gge == 0 and s4 == 1:
                        emit_tail(*tail_h[0])
                        tail_h[0] = None
                    if pend[0] is not None:
                        emit_pv(**pend[0])
                    pend[0] = dict(k=k, kpar=kpar, gpar=gpar, s4=s4, mk=mk,
                                   npr=npr, hi=hi, gidx=gidx, h=h)
            emit_pv(**pend[0])
            pend[0] = None
            tail_h[0] = (h, hi)
        emit_tail(*tail_h[0])
        tail_h[0] = None
        P.join()

        evac4 = [0, 0]
        sp4 = [0, 0]
        wo_pe_done = 0
        for half in range(2):
            P.w("gpsimd", "pe", wo_pe_done)
            wo_dma = P.dma("gpsimd", "qwE", wo_b[:],
                           g["w_o"].rearrange("(c p) n -> p c n", p=128)[:, :, 1024 * half:1024 * half + 1024])
            for qt in range(8):
                ppar = qt % 2
                qsem = "qsA" if ppar == 0 else "qsB"
                b4 = 0 if ppar == 0 else 2
                P.w("tensor", "qwE", wo_dma)
                P.w("tensor", "dve", evac4[ppar])
                mm4 = 0
                for c in range(16):
                    for nn in range(2):
                        mm4 = P.do("tensor", lambda e, c=c, nn=nn, qt=qt, b4=b4: e.matmul(
                            pb[b4 + nn][:, 0:512],
                            atT[:, c, ts(qt, 128)], wo_b[:, c, ts(nn, 512)],
                            start=(c == 0), stop=(c == 15)),
                            "pe" if (c == 15 and nn == 1) else None)
                P.w("vector", "pe", mm4)
                P.w("vector", qsem, sp4[ppar])
                P.do("vector", lambda e, ppar=ppar, b4=b4: e.tensor_copy(
                    out=ao_st[ppar][:, 0:512], in_=pb[b4][:, 0:512]))
                evac4[ppar] = P.do("vector", lambda e, ppar=ppar, b4=b4: e.tensor_copy(
                    out=ao_st[ppar][:, 512:1024], in_=pb[b4 + 1][:, 0:512]), "dve")
                P.w("scalar", "dve", evac4[ppar])
                sp4[ppar] = P.dma("scalar", qsem,
                                  g["attn_sp"][ts(qt, 128), 1024 * half:1024 * half + 1024],
                                  ao_st[ppar][:])
            wo_pe_done = P.cnt["pe"]
        P.join()

        P.w("vector", "qsA", sp4[0])
        P.w("vector", "qsB", sp4[1])
        P.w("sync", "qsC", CK["xnT_spill"])
        rl = P.dma("sync", "qxC", xnT5[:], g["xnT_sp"][:, :, :])
        P.w("tensor", "qxC", rl)
        rs_mm = 0
        for c in range(16):
            for nn in range(2):
                rs_mm = P.do("tensor", lambda e, c=c, nn=nn: e.matmul(
                    psD[0:1, 0:512] if nn == 0 else psD[32:33, 0:512],
                    P_["ones_col"][:], xnT5[:, c, ts(nn, 512)],
                    start=(c == 0), stop=(c == 15)),
                    "pe" if (c == 15 and nn == 1) else None)
        P.w("scalar", "pe", rs_mm)
        P.do("scalar", lambda e: e.activation(out=rowsum_b[:, 0:512], in_=psD[0:1, :],
                                              func=AF.Copy))
        rs_ck = P.do("scalar", lambda e: e.activation(out=rowsum_b[:, 512:1024],
                                                      in_=psD[32:33, :], func=AF.Copy), "act")
        P.w("tensor", "act", rs_ck)
        silu_ck = [0, 0]
        stt_ck = [0, 0]
        g_pe = {}
        d_pe = {}
        dacc_ck = {}
        hs_done = 0
        d_last_pe = 0
        stt_last = 0
        for blk in range(4):
            for it in range(16):
                gi = 16 * blk + it
                wpar = gi % 2
                P.w("gpsimd", "pe", g_pe.get(gi - 2, 0))
                P.dma("gpsimd", "qwA" if wpar == 0 else "qwB", wg_b[wpar][:],
                      g["wgTr"][:, gi, :, :])
                wgu = P.dma("gpsimd", "qwA" if wpar == 0 else "qwB", wu_b[wpar][:],
                            g["wuTr"][:, gi, :, :])
                for rh in range(2):
                    pa = (2 * it + rh) % 2
                    bg = 0 if pa == 0 else 2
                    bu = 1 if pa == 0 else 3
                    P.w("tensor", "qwA" if wpar == 0 else "qwB", wgu)
                    P.w("tensor", "qxC", rl)
                    P.w("tensor", "act", rs_ck)
                    P.w("tensor", "act", silu_ck[pa])
                    P.w("tensor", "dve", stt_ck[pa])
                    if blk >= 1 and it == 0 and rh == 0:
                        P.w("tensor", "dve", dacc_ck.get((blk - 1, 15), 0))
                    for c in range(16):
                        P.do("tensor", lambda e, c=c, rh=rh, wpar=wpar, bg=bg: e.matmul(
                            pb[bg][:, 0:512], wg_b[wpar][:, c, :], xnT5[:, c, ts(rh, 512)],
                            start=(c == 0), stop=False))
                    P.do("tensor", lambda e, rh=rh, bg=bg: e.matmul(
                        pb[bg][:, 0:512], P_["neg128_row"][:],
                        rowsum_b[:, ts(rh, 512)], start=False, stop=True))
                    for c in range(16):
                        P.do("tensor", lambda e, c=c, rh=rh, wpar=wpar, bu=bu: e.matmul(
                            pb[bu][:, 0:512], wu_b[wpar][:, c, :], xnT5[:, c, ts(rh, 512)],
                            start=(c == 0), stop=False))
                    gmm = P.do("tensor", lambda e, rh=rh, bu=bu: e.matmul(
                        pb[bu][:, 0:512], P_["neg128_row"][:],
                        rowsum_b[:, ts(rh, 512)], start=False, stop=True), "pe")
                    if rh == 1:
                        g_pe[gi] = gmm
                    P.w("scalar", "pe", gmm)
                    P.w("scalar", "dve", stt_ck[pa])
                    silu_ck[pa] = P.do("scalar", lambda e, rh=rh, gi=gi, bg=bg: e.activation(
                        out=hs_t[:, ts(rh, 512)], in_=pb[bg][:, 0:512], func=AF.Silu,
                        scale=P_["sg_sb"][:, gi:gi + 1]), "act")
                    P.w("vector", "act", silu_ck[pa])
                    if blk >= 1 and it == 0 and rh <= 1:
                        P.w("vector", "pe", d_last_pe)
                    stt_ck[pa] = P.do("vector", lambda e, rh=rh, gi=gi, it=it, bu=bu: e.scalar_tensor_tensor(
                        out=hT[:, it, ts(rh, 512)], in0=pb[bu][:, 0:512],
                        scalar=P_["su_sb"][:, gi:gi + 1], in1=hs_t[:, ts(rh, 512)],
                        op0=ALU.mult, op1=ALU.mult), "dve")
                P.w("tensor", "dve", stt_ck[0])
                P.w("tensor", "dve", stt_ck[1])
                for nn in range(2):
                    hs_mm = P.do("tensor", lambda e, it=it, nn=nn, blk=blk: e.matmul(
                        psD[0:1, 0:512] if nn == 0 else psD[32:33, 0:512],
                        P_["ones_col"][:], hT[:, it, ts(nn, 512)],
                        start=(blk == 0 and it == 0), stop=(blk == 3 and it == 15)),
                        "pe" if nn == 1 else None)
                if it == 15:
                    stt_last = max(stt_ck[0], stt_ck[1])
            if blk == 3:
                P.w("scalar", "pe", hs_mm)
                hf = P.do("scalar", lambda e: e.activation(out=hsum_f[:, 0:512],
                                                           in_=psD[0:1, :], func=AF.Copy), "act")
                hf = P.do("scalar", lambda e: e.activation(out=hsum_f[:, 512:1024],
                                                           in_=psD[32:33, :], func=AF.Copy), "act")
                P.w("vector", "act", hf)
                dvs(lambda e: e.tensor_copy(out=hsum_hi[:], in_=hsum_f[:]))
                hs_done = dvs(lambda e: e.tensor_tensor(
                    out=hsum_lo[:], in0=hsum_f[:], in1=hsum_hi[:],
                    op=ALU.subtract))
            for ht_i in range(16):
                dpar = ht_i % 2
                P.w("gpsimd", "pe", d_pe.get((blk, ht_i - 2), 0))
                wdd = P.dma("gpsimd", "qwC" if dpar == 0 else "qwD", wd_b[dpar][:],
                            g["wdTr"][:, ht_i, 16 * blk:16 * blk + 16, :])
                P.w("tensor", "qwC" if dpar == 0 else "qwD", wdd)
                P.w("tensor", "dve", stt_last)
                P.w("tensor", "dve", dacc_ck.get((blk, ht_i - 1), 0))
                if blk == 3:
                    P.w("tensor", "dve", hs_done)
                for c in range(16):
                    for nn in range(2):
                        dmm = P.do("tensor", lambda e, c=c, nn=nn, dpar=dpar, blk=blk: e.matmul(
                            pb[4 + nn][:, 0:512], wd_b[dpar][:, c, :],
                            hT[:, c, ts(nn, 512)], start=(c == 0),
                            stop=(blk != 3 and c == 15)),
                            "pe" if (blk != 3 and c == 15 and nn == 1) else None)
                if blk == 3:
                    for nn in range(2):
                        P.do("tensor", lambda e, nn=nn: e.matmul(
                            pb[4 + nn][:, 0:512], P_["neg128_row"][:],
                            hsum_hi[:, ts(nn, 512)], start=False, stop=False))
                    for nn in range(2):
                        dmm = P.do("tensor", lambda e, nn=nn: e.matmul(
                            pb[4 + nn][:, 0:512], P_["neg128_row"][:],
                            hsum_lo[:, ts(nn, 512)], start=False, stop=True),
                            "pe" if nn == 1 else None)
                d_pe[(blk, ht_i)] = dmm
                P.w("vector", "pe", dmm)
                if blk >= 1:
                    P.w("vector", "dve", dacc_ck[(blk - 1, ht_i)])
                if blk == 0:
                    P.do("vector", lambda e, ht_i=ht_i: e.tensor_copy(
                        out=acc[:, ht_i, 0:512], in_=pb[4][:, 0:512]))
                    dacc_ck[(blk, ht_i)] = P.do("vector", lambda e, ht_i=ht_i: e.tensor_copy(
                        out=acc[:, ht_i, 512:1024], in_=pb[5][:, 0:512]), "dve")
                else:
                    P.do("vector", lambda e, ht_i=ht_i: e.tensor_tensor(
                        out=acc[:, ht_i, 0:512], in0=acc[:, ht_i, 0:512],
                        in1=pb[4][:, 0:512], op=ALU.add))
                    dacc_ck[(blk, ht_i)] = P.do("vector", lambda e, ht_i=ht_i: e.tensor_tensor(
                        out=acc[:, ht_i, 512:1024], in0=acc[:, ht_i, 512:1024],
                        in1=pb[5][:, 0:512], op=ALU.add), "dve")
            d_last_pe = P.cnt["pe"]
        sc_ck = 0
        for ht_i in range(16):
            P.w("vector", "dve", dacc_ck[(3, ht_i)])
            sc_ck = dvs(lambda e, ht_i=ht_i: e.tensor_scalar(
                out=acc[:, ht_i, :], in0=acc[:, ht_i, :],
                scalar1=P_["sd_sb"][:, ht_i:ht_i + 1], scalar2=None, op0=ALU.mult))
        fin_dve = [0, 0]
        y_dmas = [0] * 8
        y_dma = 0
        for rt in range(8):
            rpar = rt % 2
            P.w("sync", "dve", sc_ck)
            P.w("sync", "dve", fin_dve[rpar])
            x5d = P.dma("sync", "qxA" if rpar == 0 else "qxB", x5_st[rpar][:],
                        g["x_own"][ts(rt, 128), :])
            P.w("scalar", "qsA" if rpar == 0 else "qsB", sp4[rpar])
            P.w("scalar", "dve", sc_ck)
            P.w("scalar", "dve", fin_dve[rpar])
            atd = P.dma("scalar", "qsA" if rpar == 0 else "qsB", at_st[rpar][:],
                        g["attn_sp"][ts(rt, 128), :])
            P.w("tensor", "dve", sc_ck)
            P.w("tensor", "dve", fin_dve[0])
            P.w("tensor", "dve", fin_dve[1])
            trf = 0
            for ht_i in range(16):
                trf = P.do("tensor", lambda e, ht_i=ht_i, rt=rt: e.transpose(
                    pb[ht_i // 4][:, ts(ht_i % 4, 128)], acc[:, ht_i, ts(rt, 128)],
                    P_["idn_f"][:]),
                    "pe" if ht_i == 15 else None)
            P.w("vector", "pe", trf)
            P.w("vector", "qxA" if rpar == 0 else "qxB", x5d)
            P.w("vector", "qsA" if rpar == 0 else "qsB", atd)
            if rt >= 2:
                P.w("vector", "qxD" if rpar == 0 else "qxE", y_dmas[rt - 2])
            for bb in range(4):
                dvs(lambda e, rpar=rpar, bb=bb: e.tensor_tensor(
                    out=out_st[rpar][:, ts(bb, 512)], in0=pb[bb][:, 0:512],
                    in1=x5_st[rpar][:, ts(bb, 512)], op=ALU.add))
            fin_dve[rpar] = dvs(lambda e, rpar=rpar: e.tensor_tensor(
                out=out_st[rpar][:], in0=out_st[rpar][:], in1=at_st[rpar][:], op=ALU.add))
            P.w("sync", "dve", fin_dve[rpar])
            y_dmas[rt] = y_dma = P.dma("sync", "qxD" if rpar == 0 else "qxE",
                                       g["y"][ts(rt, 128), :], out_st[rpar][:])
        for eng in ENGS:
            P.w(eng, "qxD", y_dmas[6])
            P.w(eng, "qxE", y_dmas[7])
        return P.cnt

    counts = []
    with nc.Block() as block:
        deco = {"sync": block.sync, "gpsimd": block.gpsimd, "vector": block.vector,
                "scalar": block.scalar, "tensor": block.tensor}
        for name in ENGS:
            holder = {}

            def run(eng_obj, name=name, holder=holder):
                Pp = Emit(nc, name, semh)
                holder["cnt"] = emit(Pp)

            deco[name](run)
            counts.append(dict(holder["cnt"]))
    for c in counts[1:]:
        assert c == counts[0], ("pass counter drift", counts[0], c)
    return nc


_CACHE = {}


def _get_nc():
    if "nc" not in _CACHE:
        _CACHE["nc"] = build_nc()
    return _CACHE["nc"]


def kernel(L0_attn_x, L0_attn_w_dq, L0_attn_g_q, L0_attn_w_uq, L0_attn_w_dkv,
           L0_attn_w_kr, L0_attn_g_kv, L0_attn_w_uk, L0_attn_w_uv, L0_attn_w_o,
           L0_ffn_gate_w, L0_ffn_gate_scale, L0_ffn_up_w, L0_ffn_up_scale,
           L0_ffn_down_w, L0_ffn_down_scale):
    x = np.ascontiguousarray(np.asarray(L0_attn_x, dtype=np.float32)[0])
    f32 = np.float32
    ac = np.ascontiguousarray
    cos_full, sin_full = _rope_tables_fp32()

    def _u8(w):
        a = np.ascontiguousarray(np.asarray(w, np.int32))
        return a.view(np.uint8)[:, 0::4]

    shared = {
        "w_dq": ac(np.asarray(L0_attn_w_dq, f32)),
        "w_uq": ac(np.asarray(L0_attn_w_uq, f32)),
        "gq_col": ac(np.asarray(L0_attn_g_q, f32).reshape(6, 128).T),
        "w_dkv": ac(np.asarray(L0_attn_w_dkv, f32)),
        "w_kr": ac(np.asarray(L0_attn_w_kr, f32)),
        "gkv_col": ac(np.asarray(L0_attn_g_kv, f32).reshape(4, 128).T),
        "w_o": ac(np.asarray(L0_attn_w_o, f32)),
        "wgTr": ac(_u8(L0_ffn_gate_w).T.reshape(16, 128, 64, 128).transpose(1, 2, 0, 3)),
        "wuTr": ac(_u8(L0_ffn_up_w).T.reshape(16, 128, 64, 128).transpose(1, 2, 0, 3)),
        "wdTr": ac(_u8(L0_ffn_down_w).T.reshape(64, 128, 16, 128).transpose(1, 2, 0, 3)),
        "sg_col": ac(np.asarray(L0_ffn_gate_scale, f32).reshape(64, 128).T),
        "su_col": ac(np.asarray(L0_ffn_up_scale, f32).reshape(64, 128).T),
        "sd_col": ac(np.asarray(L0_ffn_down_scale, f32).reshape(16, 128).T),
        "idn": np.eye(128, dtype=f32),
        "idn2": ac(np.tile(np.eye(RD, dtype=f32), (2, 1))),
    }

    in_maps = []
    for c in range(NCORE):
        pos = np.arange(c, S, 8)
        cc = cos_full[pos].T
        ss = sin_full[pos].T
        ctab = ac(np.concatenate([cc, cc], axis=0).astype(f32))
        stab = ac(np.concatenate([-ss, ss], axis=0).astype(f32))
        kr = np.arange(128, dtype=np.int64)[:, None]
        jj = np.arange(16, dtype=np.int64)[None, :]
        mask16 = (kr <= (c + 8 * jj)).astype(f32)
        m = dict(shared)
        m["x_own"] = ac(x[c::8])
        m["ctab"] = ctab
        m["stab"] = stab
        m["mask16"] = ac(mask16)
        m["w_uk_own"] = ac(np.asarray(L0_attn_w_uk, f32)[:, 256 * c:256 * c + 256])
        m["w_uv_own"] = ac(np.asarray(L0_attn_w_uv, f32)[:, 256 * c:256 * c + 256])
        m["xT_c"] = ac(x[1024 * c:1024 * c + 1024].T)
        m["x_rowc"] = ac(x[1024 * c:1024 * c + 1024])
        m["cos_kc"] = ac(cos_full[1024 * c:1024 * c + 1024])
        m["sin_kc"] = ac(sin_full[1024 * c:1024 * c + 1024])
        in_maps.append(m)

    nc = _get_nc()
    res = run_bass_kernel_spmd(nc, in_maps, core_ids=list(range(NCORE)))
    out = np.empty((S, D), dtype=np.float32)
    for c in range(NCORE):
        out[c::8] = res.results[c]["y"]
    return out.reshape(1, S, D)


# version 08
# speedup vs baseline: 1.0574x; 1.0493x over previous
import numpy as np
import ml_dtypes
import concourse.bass as bass
import concourse.mybir as mybir
from concourse.bass_utils import run_bass_kernel_spmd

F32 = mybir.dt.float32
BF16 = mybir.dt.bfloat16
I32 = mybir.dt.int32
AF = mybir.ActivationFunctionType
ALU = mybir.AluOpType
AX = mybir.AxisListType

S = 8192
D = 2048
NH = 16
NOPE = 128
RD = 64
VD = 128
QL = 768
KVL = 512
FI = 8192
NCORE = 8
R = 1024
SCALE = 1.0 / float(np.sqrt(192.0))
EXPB = -16.0
KB = 1024

SEMS = ("ph", "pe", "dve", "act", "cc2", "cc3",
        "qxA", "qxB", "qxC", "qxD", "qxE",
        "qwA", "qwB", "qwC", "qwD", "qwE", "qwT0", "qwT1", "qwT2", "qwR0", "qwR1",
        "qsA", "qsB", "qsC")
ENGS = ("sync", "gpsimd", "vector", "scalar", "tensor")


def ts(i, n):
    return slice(i * n, (i + 1) * n)


class Emit:

    def __init__(self, nc, cur, semh):
        self.nc = nc
        self.cur = cur
        self.h = semh
        self.cnt = {k: 0 for k in SEMS}
        self.wm = {k: 0 for k in SEMS}
        self.E = {"sync": nc.sync, "gpsimd": nc.gpsimd, "vector": nc.vector,
                  "scalar": nc.scalar, "tensor": nc.tensor}[cur]

    def on(self, eng):
        return eng == self.cur

    def w(self, eng, sem, val):
        if not val or val <= 0:
            return
        if not self.on(eng):
            return
        if val <= self.wm[sem]:
            return
        self.wm[sem] = val
        self.E.wait_ge(self.h[sem], val)

    def do(self, eng, fn, sem=None, inc=1):
        if self.on(eng):
            inst = fn(self.E)
            if sem is not None and inst is not None:
                inst.then_inc(self.h[sem], inc)
        if sem is not None:
            self.cnt[sem] += inc
            return self.cnt[sem]
        return None

    def dma(self, qeng, sem, out, in_):
        assert sem in SEMS and sem[0] == "q", sem
        if self.on(qeng):
            self.E.dma_start(out=out, in_=in_).then_inc(self.h[sem], 16)
        self.cnt[sem] += 16
        return self.cnt[sem]

    def join(self):
        own = {"vector": "dve", "scalar": "act", "tensor": "pe"}.get(self.cur)
        if own is not None and self.cnt[own] > self.wm[own]:
            self.wm[own] = self.cnt[own]
            self.E.wait_ge(self.h[own], self.cnt[own])
        self.cnt["ph"] += 5
        tgt = self.cnt["ph"]
        self.E.nop().then_inc(self.h["ph"], 1)
        self.E.wait_ge(self.h["ph"], tgt)
        for s in ("pe", "dve", "act", "ph"):
            self.wm[s] = max(self.wm[s], self.cnt[s])


def _rope_tables_fp32():
    try:
        import jax
        import jax.numpy as jnp
        cpu = jax.devices("cpu")[0]
        with jax.default_device(cpu):
            inv = 1.0 / (10000.0 ** (jnp.arange(0, RD, 2, dtype=jnp.float32) / RD))
            t = jnp.arange(S, dtype=jnp.float32)
            fr = jnp.outer(t, inv)
            c = np.asarray(jnp.cos(fr), dtype=np.float32)
            s = np.asarray(jnp.sin(fr), dtype=np.float32)
        return c, s
    except Exception:
        inv = (1.0 / (10000.0 ** (np.arange(0, RD, 2, dtype=np.float32) /
                                  np.float32(RD)))).astype(np.float32)
        t = np.arange(S, dtype=np.float32)
        fr = np.outer(t, inv).astype(np.float32)
        return np.cos(fr).astype(np.float32), np.sin(fr).astype(np.float32)


def build_nc():
    nc = bass.Bass("TRN2", target_bir_lowering=False)
    g = {}

    def din(name, shape, dtype=F32):
        g[name] = nc.dram_tensor(name, shape, dtype, kind="ExternalInput").ap()

    din("x_own", [R, D]);      din("xT_c", [D, R]);      din("x_rowc", [R, D])
    din("w_dq", [D, QL]);      din("w_uq", [QL, NH * 192]); din("gq_col", [128, 6])
    din("w_dkv", [D, KVL]);    din("w_kr", [D, RD]);     din("gkv_col", [128, 4])
    din("w_uk_own", [KVL, 2 * NOPE]); din("w_uv_own", [KVL, 2 * VD]); din("w_o", [D, D])
    U8 = mybir.dt.uint8
    din("wgTr", [128, 64, 16, 128], U8)
    din("wuTr", [128, 64, 16, 128], U8)
    din("wdTr", [128, 16, 64, 128], U8)
    din("sg_col", [128, 64]);  din("su_col", [128, 64]); din("sd_col", [128, 16])
    din("cos_kc", [R, 32]);    din("sin_kc", [R, 32])
    din("ctab", [RD, R]);      din("stab", [RD, R])
    din("mask16", [128, 16]);  din("idn", [128, 128]);  din("idn2", [128, RD])

    g["y"] = nc.dram_tensor("y", [R, D], F32, kind="ExternalOutput").ap()
    g["xnT_sp"] = nc.dram_tensor("xnT_sp", [128, 16, R], BF16, kind="Internal").ap()
    g["attn_sp"] = nc.dram_tensor("attn_sp", [R, D], BF16, kind="Internal").ap()
    g["kv2_sh"] = nc.dram_tensor("kv2_sh", [128, 5 * R], BF16, kind="Internal").ap()
    g["kv2_gx"] = nc.dram_tensor("kv2_gx", [8, 128, 5 * R], BF16,
                                 kind="Internal", addr_space="Shared").ap()
    for j in range(2):
        g["kv3_sh%d" % j] = nc.dram_tensor("kv3_sh%d" % j, [128, 2 * S], BF16,
                                           kind="Internal").ap()
        g["kv3_gx%d" % j] = nc.dram_tensor("kv3_gx%d" % j, [8, 128, 2 * S], BF16,
                                           kind="Internal", addr_space="Shared").ap()

    P_ = {}

    def sp(name, shape, dtype):
        P_[name] = nc.alloc_sbuf_tensor(name, shape, dtype)

    sp("ones_col", [128, 1], BF16)
    sp("ones_row", [1, 128], BF16)
    sp("neg128_row", [1, 128], BF16)
    sp("idn_f", [128, 128], F32)
    sp("idn_b", [128, 128], BF16)
    sp("idn2_f", [128, RD], F32)
    sp("expb", [128, 1], F32)
    sp("idn2_b", [128, RD], BF16)
    sp("zro", [128, 512], BF16)
    sp("mask_b", [128, 16], BF16)
    sp("gq_sb", [128, 6], F32)
    sp("gkv_sb", [128, 4], F32)
    sp("sg_sb", [128, 64], F32)
    sp("su_sb", [128, 64], F32)
    sp("sd_sb", [128, 16], F32)
    sp("ncol_dkv", [1, KVL], BF16)
    sp("ncol_kr", [1, 128], BF16)

    pb = [nc.alloc_psum_tensor("pb%d" % i, [128, 512], F32).ap() for i in range(7)]
    psD = nc.alloc_psum_tensor("psD", [128, 512], F32).ap()

    def pbv(i):
        return pb[i].bitcast(BF16)

    ARENA = 200 * KB
    arena_t = nc.alloc_sbuf_tensor("arena", [128, ARENA], mybir.dt.uint8)
    abase = nc.lookup_mloc(arena_t).addr
    _seen = []

    def at(name, off_kb, shape, dtype):
        off = int(round(off_kb * KB))
        nb = int(np.prod(shape[1:])) * mybir.dt.size(dtype)
        assert off % 32 == 0 and off + nb <= ARENA, (name, off_kb, nb)
        _seen.append((name, off, nb))
        return nc.alloc_sbuf_tensor_at(name, shape, dtype, offset=abase + off).ap()

    kvT = at("kvT", 0, [128, 4, S], BF16)
    kpeT = at("kpeT", 64, [128, S], BF16)
    QnA = at("QnA", 80, [128, NH, R], BF16)
    QrA = at("QrA", 112, [128, 8, R], BF16)
    atT = at("atT", 128, [128, NH, R], BF16)
    wdkv_b = at("wdkv_b", 184, [128, 16, KVL], BF16)
    wkr_b = at("wkr_b", 180, [128, 16, 128], BF16)
    xo_st = [at("xo_st0", 0, [128, D], F32), at("xo_st1", 8, [128, D], F32)]
    xn_t = [at("xn_t0", 16, [128, D], BF16), at("xn_t1", 20, [128, D], BF16)]
    xnT = at("xnT", 24, [128, 16, R], BF16)
    qlnT = at("qlnT", 56, [128, 6, R], BF16)
    qr_tmp = at("qr_tmp", 72, [RD, R], BF16)
    st1 = at("st1", 74, [128, 64], F32)
    qr_prod = at("qr_prod", 75, [128, R], BF16)
    tabs_f = at("tabs_f", 176, [128, R], F32)
    tabs_b = at("tabs_b", 170, [128, R], BF16)
    wdq_b = at("wdq_b", 128, [128, 16, QL], BF16)
    wuq_st = [at("wuq_st0", 152, [128, 6, 192], BF16),
              at("wuq_st1", 156.5, [128, 6, 192], BF16)]
    wuq_b = [at("wuq_b0", 161, [128, 6, 256], BF16),
             at("wuq_b1", 164, [128, 6, 256], BF16)]
    ql_t = [at("ql_t0", 167, [128, QL], BF16), at("ql_t1", 168.5, [128, QL], BF16)]
    xT_st = [at("xT_st0", 128, [128, 16, 256], BF16),
             at("xT_st1", 136, [128, 16, 256], BF16),
             at("xT_st2", 144, [128, 16, 256], BF16)]
    xr_st = [at("xr_st0", 152, [128, D], BF16), at("xr_st1", 156, [128, D], BF16)]
    ck_st = [at("ck_st0", 160, [128, 4, 32], BF16), at("ck_st1", 160.25, [128, 4, 32], BF16)]
    sk_st = [at("sk_st0", 160.5, [128, 4, 32], BF16), at("sk_st1", 160.75, [128, 4, 32], BF16)]
    kv_b = [at("kv_b0", 161, [128, KVL], BF16), at("kv_b1", 162, [128, KVL], BF16)]
    kpe_sb = at("kpe_sb", 163, [128, 4, 128], BF16)
    rtmp = [at("rtmp0", 164, [128, 4, 32], BF16), at("rtmp1", 164.25, [128, 4, 32], BF16)]
    muT_row = at("muT_row", 166, [1, 512], BF16)
    mu_bf = at("mu_bf", 167.25, [128, 8], BF16)
    st2 = at("st2", 165, [128, 64], F32)
    st2b = at("st2b", 165.25, [128, 16], F32)
    kv_own = at("kv_own", 168, [128, 4, R], BF16)
    kpe_own = at("kpe_own", 176, [128, R], BF16)
    wuk_st1 = at("wuk_st1", 160, [128, 4, 128], F32)
    wuv_st1 = at("wuv_st1", 162, [128, 4, 128], F32)
    wuk_b1 = at("wuk_b1", 164, [128, 4, 128], BF16)
    wuv_b1 = at("wuv_b1", 165, [128, 4, 128], BF16)
    Kst = at("Kst", 166, [128, S], BF16)
    Vst = at("Vst", 182, [128, 64, 128], BF16)
    K_sb = [at("K_sb0", 160, [128, 512], BF16), at("K_sb1", 161, [128, 512], BF16)]
    V4_sb = [at("V4_sb0", 162, [128, 4, 128], BF16), at("V4_sb1", 163, [128, 4, 128], BF16)]
    P_sb = [at("P_sb0", 164, [128, R], BF16), at("P_sb1", 166, [128, R], BF16)]
    den_f = at("den_f", 168, [1, R], F32)
    rec_b = at("rec_b", 172, [1, R], BF16)
    bc_f = at("bc_f", 174, [128, R], F32)
    wo_b = at("wo_b", 0, [128, 16, KB], BF16)
    ao_st = [at("ao_st0", 32, [128, KB], BF16), at("ao_st1", 34, [128, KB], BF16)]
    xnT5 = at("xnT5", 0, [128, 16, R], BF16)
    hT = at("hT", 32, [128, 16, R], BF16)
    acc = at("acc", 64, [128, 16, R], F32)
    wg_b = [at("wg_b0", 128, [128, 16, 128], BF16), at("wg_b1", 132, [128, 16, 128], BF16)]
    wu_b = [at("wu_b0", 136, [128, 16, 128], BF16), at("wu_b1", 140, [128, 16, 128], BF16)]
    wd_b = [at("wd_b0", 144, [128, 16, 128], BF16), at("wd_b1", 148, [128, 16, 128], BF16)]
    hs_t = at("hs_t", 152, [128, KB], BF16)
    x5_st = [at("x5_st0", 154, [128, D], F32), at("x5_st1", 162, [128, D], F32)]
    at_st = [at("at_st0", 170, [128, D], BF16), at("at_st1", 174, [128, D], BF16)]
    out_st = [at("out_st0", 178, [128, D], F32), at("out_st1", 186, [128, D], F32)]
    rowsum_b = at("rowsum_b", 154, [1, R], BF16)
    hsum_hi = at("hsum_hi", 156, [1, R], BF16)
    hsum_lo = at("hsum_lo", 158, [1, R], BF16)
    hsum_f = at("hsum_f", 160, [1, R], F32)

    semh = {s: nc.alloc_semaphore(name=s) for s in SEMS}

    def emit(P):
        CK = {}

        def dvs(fn):
            P.w("vector", "dve", P.cnt["dve"])
            return P.do("vector", fn, "dve")

        def acs(fn):
            P.w("scalar", "act", P.cnt["act"])
            return P.do("scalar", fn, "act")

        P.dma("sync", "qxC", P_["idn_f"][:], g["idn"])
        P.dma("sync", "qxC", P_["gq_sb"][:], g["gq_col"])
        P.dma("sync", "qxC", P_["gkv_sb"][:], g["gkv_col"])
        P.dma("sync", "qxC", P_["sg_sb"][:], g["sg_col"])
        P.dma("sync", "qxC", P_["su_sb"][:], g["su_col"])
        P.dma("sync", "qxC", P_["sd_sb"][:], g["sd_col"])
        P.dma("sync", "qxC", tabs_f[0:RD, :], g["ctab"])
        P.dma("sync", "qxC", tabs_f[RD:128, :], g["stab"])
        v_sync0 = P.dma("sync", "qxC", P_["idn2_f"][:], g["idn2"])
        v_tab = v_idn2 = v_idn = v_sync0
        P.dma("gpsimd", "qwE", P_["mask_b"][:], g["mask16"])
        P.dma("gpsimd", "qwE", wdq_b[:], g["w_dq"].rearrange("(c p) n -> p c n", p=128))
        P.dma("gpsimd", "qwE", wdkv_b[:], g["w_dkv"].rearrange("(c p) n -> p c n", p=128))
        P.dma("gpsimd", "qwE", wkr_b[:, :, 0:RD], g["w_kr"].rearrange("(c p) n -> p c n", p=128))
        w_kr_ld = P.dma("gpsimd", "qwE", wkr_b[:, :, RD:128],
                        g["w_kr"].rearrange("(c p) n -> p c n", p=128))
        w_dkv_ld = w_kr_ld
        P.do("vector", lambda e: e.memset(P_["ones_col"][:], 1.0))
        P.do("vector", lambda e: e.memset(P_["ones_row"][:], 1.0))
        P.do("vector", lambda e: e.memset(P_["neg128_row"][:], -128.0))
        P.do("vector", lambda e: e.memset(P_["zro"][:], 0.0))
        ms_ck = P.do("vector", lambda e: e.memset(P_["expb"][:], EXPB), "dve")
        P.w("tensor", "dve", ms_ck)
        P.w("vector", "qxC", v_idn)
        P.do("vector", lambda e: e.tensor_copy(out=P_["idn_b"][:], in_=P_["idn_f"][:]), "dve")
        P.w("vector", "qxC", v_idn2)
        P.do("vector", lambda e: e.tensor_copy(out=P_["idn2_b"][:], in_=P_["idn2_f"][:]), "dve")
        P.w("vector", "qxC", v_tab)
        P.do("vector", lambda e: e.tensor_copy(out=tabs_b[:], in_=tabs_f[:]), "dve")
        P.w("scalar", "qxC", v_sync0)
        P.w("vector", "qwE", w_kr_ld)
        P.w("tensor", "qwE", w_dkv_ld)
        m1 = 0
        for c in range(16):
            m1 = P.do("tensor", lambda e, c=c: e.matmul(pb[0][0:1, 0:KVL], P_["ones_col"][:],
                                                        wdkv_b[:, c, :], start=(c == 0),
                                                        stop=(c == 15)),
                      "pe" if c == 15 else None)
        P.w("tensor", "qwE", w_kr_ld)
        m2 = 0
        for c in range(16):
            m2 = P.do("tensor", lambda e, c=c: e.matmul(pb[1][0:1, 0:128], P_["ones_col"][:],
                                                        wkr_b[:, c, :], start=(c == 0),
                                                        stop=(c == 15)),
                      "pe" if c == 15 else None)
        P.w("scalar", "pe", m1)
        P.do("scalar", lambda e: e.activation(out=P_["ncol_dkv"][:], in_=pb[0][0:1, 0:KVL],
                                              func=AF.Copy, scale=-1.0), "act")
        P.w("scalar", "pe", m2)
        P.do("scalar", lambda e: e.activation(out=P_["ncol_kr"][:], in_=pb[1][0:1, 0:128],
                                              func=AF.Copy, scale=-1.0), "act")
        P.join()

        apply_ck = [0] * 8
        tr_pe = [0] * 8
        tr_ck = [0] * 8
        xo_dma = [0] * 8
        for t in range(8):
            if t >= 2:
                P.w("sync", "dve", apply_ck[t - 2])
            xo_dma[t] = P.dma("sync", "qxA" if t % 2 == 0 else "qxB",
                              xo_st[t % 2][:], g["x_own"][ts(t, 128), :])
            P.w("vector", "qxA" if t % 2 == 0 else "qxB", xo_dma[t])
            rs_ck = P.do("vector", lambda e, t=t: e.reduce_sum(st1[:, t:t + 1], xo_st[t % 2][:],
                                                               axis=AX.X), "dve")
            P.w("scalar", "qxA" if t % 2 == 0 else "qxB", xo_dma[t])
            if t >= 2:
                P.w("scalar", "pe", tr_pe[t - 2])
            P.do("scalar", lambda e, t=t: e.activation(out=xn_t[t % 2][:], in_=xo_st[t % 2][:],
                                                       func=AF.Square,
                                                       accum_out=st1[:, 8 + t:9 + t]), "act")
            sq_ck = P.cnt["act"]
            P.w("vector", "act", sq_ck)
            dvs(lambda e, t=t: e.tensor_scalar(out=st1[:, 16 + t:17 + t],
                                               in0=st1[:, t:t + 1], scalar1=1.0 / D,
                                               scalar2=None, op0=ALU.mult))
            dvs(lambda e, t=t: e.tensor_scalar(out=st1[:, 40 + t:41 + t],
                                               in0=st1[:, 8 + t:9 + t], scalar1=1.0 / D,
                                               scalar2=1e-5, op0=ALU.mult, op1=ALU.add))
            dvs(lambda e, t=t: e.tensor_tensor(out=st1[:, 48 + t:49 + t],
                                               in0=st1[:, 16 + t:17 + t],
                                               in1=st1[:, 16 + t:17 + t], op=ALU.mult))
            dv = dvs(lambda e, t=t: e.tensor_tensor(out=st1[:, 40 + t:41 + t],
                                                    in0=st1[:, 40 + t:41 + t],
                                                    in1=st1[:, 48 + t:49 + t],
                                                    op=ALU.subtract))
            P.w("scalar", "dve", dv)
            sr = acs(lambda e, t=t: e.activation(out=st1[:, 48 + t:49 + t],
                                                 in_=st1[:, 40 + t:41 + t],
                                                 func=AF.Sqrt))
            P.w("vector", "act", sr)
            rcp = dvs(lambda e, t=t: e.reciprocal(out=st1[:, 24 + t:25 + t],
                                                  in_=st1[:, 48 + t:49 + t]))
            if t >= 2:
                P.w("vector", "pe", tr_pe[t - 2])
            apply_ck[t] = dvs(lambda e, t=t: e.tensor_scalar(
                out=xn_t[t % 2][:], in0=xo_st[t % 2][:],
                scalar1=st1[:, 16 + t:17 + t], scalar2=st1[:, 24 + t:25 + t],
                op0=ALU.subtract, op1=ALU.mult))
            P.w("tensor", "dve", apply_ck[t])
            if t >= 1:
                P.w("tensor", "dve", tr_ck[t - 1])
            for j in range(16):
                v = P.do("tensor", lambda e, t=t, j=j: e.transpose(
                    pbv(4 + j // 8)[:, ts(j % 8, 128)], xn_t[t % 2][:, ts(j, 128)],
                    P_["idn_b"][:]),
                    "pe" if j == 15 else None)
                if j == 15:
                    tr_pe[t] = v
            P.w("vector", "pe", tr_pe[t])
            P.do("vector", lambda e, t=t: e.tensor_copy(
                out=xnT[:, 0:8, ts(t, 128)],
                in_=pbv(4).rearrange("p (c r) -> p c r", c=8)))
            tr_ck[t] = P.do("vector", lambda e, t=t: e.tensor_copy(
                out=xnT[:, 8:16, ts(t, 128)],
                in_=pbv(5).rearrange("p (c r) -> p c r", c=8)), "dve")
        qln_pe = [0] * 8
        qln_ck = [0] * 8
        ap2_ck = [0] * 8
        for t in range(8):
            P.w("tensor", "dve", tr_ck[7] if t == 0 else tr_ck[t])
            if t >= 2:
                P.w("tensor", "act", ap2_ck[t - 2])
            mmq = 0
            for j in range(16):
                P.do("tensor", lambda e, t=t, j=j: e.matmul(pb[0][:, 0:512], xnT[:, j, ts(t, 128)],
                                                            wdq_b[:, j, 0:512],
                                                            start=(j == 0), stop=(j == 15)))
                mmq = P.do("tensor", lambda e, t=t, j=j: e.matmul(pb[1][:, 0:256],
                                                                  xnT[:, j, ts(t, 128)],
                                                                  wdq_b[:, j, 512:768],
                                                                  start=(j == 0), stop=(j == 15)),
                           "pe" if j == 15 else None)
            P.w("scalar", "pe", mmq)
            if t >= 2:
                P.w("scalar", "pe", qln_pe[t - 2])
            P.do("scalar", lambda e, t=t: e.activation(out=ql_t[t % 2][:, 0:512], in_=pb[0][:, 0:512],
                                                       func=AF.Square, accum_out=st1[:, t:t + 1]), "act")
            s2 = P.do("scalar", lambda e, t=t: e.activation(out=ql_t[t % 2][:, 512:768],
                                                            in_=pb[1][:, 0:256], func=AF.Square,
                                                            accum_out=st1[:, 8 + t:9 + t]), "act")
            P.w("vector", "act", s2)
            dvs(lambda e, t=t: e.tensor_tensor(out=st1[:, t:t + 1], in0=st1[:, t:t + 1],
                                               in1=st1[:, 8 + t:9 + t], op=ALU.add))
            dv2 = dvs(lambda e, t=t: e.tensor_scalar(out=st1[:, t:t + 1],
                                                     in0=st1[:, t:t + 1],
                                                     scalar1=1.0 / QL, scalar2=1e-6,
                                                     op0=ALU.mult, op1=ALU.add))
            P.w("scalar", "dve", dv2)
            sr2 = acs(lambda e, t=t: e.activation(out=st1[:, 8 + t:9 + t],
                                                  in_=st1[:, t:t + 1], func=AF.Sqrt))
            P.w("vector", "act", sr2)
            rc2 = dvs(lambda e, t=t: e.reciprocal(out=st1[:, 16 + t:17 + t],
                                                  in_=st1[:, 8 + t:9 + t]))
            P.w("scalar", "dve", rc2)
            acs(lambda e, t=t: e.activation(out=ql_t[t % 2][:, 0:512], in_=pb[0][:, 0:512],
                                            func=AF.Copy, scale=st1[:, 16 + t:17 + t]))
            ap2_ck[t] = acs(lambda e, t=t: e.activation(
                out=ql_t[t % 2][:, 512:768], in_=pb[1][:, 0:256],
                func=AF.Copy, scale=st1[:, 16 + t:17 + t]))
            P.w("tensor", "act", ap2_ck[t])
            if t >= 1:
                P.w("tensor", "dve", qln_ck[t - 1])
            for j in range(6):
                v = P.do("tensor", lambda e, t=t, j=j: e.transpose(
                    pbv(4)[:, ts(j, 128)], ql_t[t % 2][:, ts(j, 128)], P_["idn_b"][:]),
                    "pe" if j == 5 else None)
                if j == 5:
                    qln_pe[t] = v
            P.w("vector", "pe", qln_pe[t])
            qln_ck[t] = P.do("vector", lambda e, t=t: e.tensor_copy(
                out=qlnT[:, :, ts(t, 128)],
                in_=pbv(4)[:, 0:768].rearrange("p (c r) -> p c r", c=6)), "dve")
        P.w("scalar", "dve", tr_ck[7])
        sp_ck = P.dma("scalar", "qsC", g["xnT_sp"][:, :, :], xnT[:, :, :])
        CK["xnT_spill"] = sp_ck
        wq_dma = [0, 0]
        cvtq = [0, 0]
        q_done = [0] * NH
        for h in range(NH):
            par = h % 2
            P.w("gpsimd", "dve", cvtq[par])
            wq_dma[par] = P.dma("gpsimd", "qwA" if par == 0 else "qwB", wuq_st[par][:],
                                g["w_uq"].rearrange("(c p) n -> p c n", p=128)[:, :, ts(h, 192)])
            P.w("vector", "qwA" if par == 0 else "qwB", wq_dma[par])
            if h >= 2:
                P.w("vector", "pe", CK.get(("qmm", h - 2), 0))
            ts6 = 0
            for j in range(6):
                ts6 = P.do("vector", lambda e, j=j, par=par: e.tensor_scalar(
                    out=wuq_b[par][:, j, 0:192], in0=wuq_st[par][:, j, :],
                    scalar1=P_["gq_sb"][:, j:j + 1], scalar2=SCALE, op0=ALU.mult, op1=ALU.mult),
                    "dve" if j == 5 else None)
            P.w("vector", "dve", ts6)
            P.do("vector", lambda e, par=par: e.tensor_copy(out=wuq_b[par][:, :, 192:224],
                                                            in_=wuq_b[par][:, :, 160:192]))
            cvtq[par] = P.do("vector", lambda e, par=par: e.tensor_copy(
                out=wuq_b[par][:, :, 224:256], in_=wuq_b[par][:, :, 128:160]), "dve")
            P.w("tensor", "dve", cvtq[par])
            if h == 0:
                P.w("tensor", "dve", qln_ck[7])
            else:
                P.w("tensor", "dve", q_done[h - 1])
            qn_mm = 0
            for j in range(6):
                for nn in range(2):
                    qn_mm = P.do("tensor", lambda e, j=j, par=par, nn=nn: e.matmul(
                        pb[2 + nn][:, 0:512], wuq_b[par][:, j, 0:128],
                        qlnT[:, j, ts(nn, 512)], start=(j == 0), stop=(j == 5)),
                        "pe" if (j == 5 and nn == 1) else None)
            qr_mm = 0
            for j in range(6):
                for nn in range(2):
                    qr_mm = P.do("tensor", lambda e, j=j, par=par, nn=nn: e.matmul(
                        pb[4 + nn][:, 0:512], wuq_b[par][:, j, 128:256],
                        qlnT[:, j, ts(nn, 512)], start=(j == 0), stop=(j == 5)),
                        "pe" if (j == 5 and nn == 1) else None)
            CK[("qmm", h)] = qr_mm
            P.w("vector", "pe", qn_mm)
            P.do("vector", lambda e, h=h: e.tensor_copy(out=QnA[:, h, 0:512], in_=pb[2][:, 0:512]))
            P.do("vector", lambda e, h=h: e.tensor_copy(out=QnA[:, h, 512:1024], in_=pb[3][:, 0:512]))
            P.w("vector", "pe", qr_mm)
            P.do("vector", lambda e: e.tensor_tensor(
                out=qr_prod[:, 0:512], in0=pb[4][:, 0:512], in1=tabs_b[:, 0:512], op=ALU.mult))
            prod_ck = P.do("vector", lambda e: e.tensor_tensor(
                out=qr_prod[:, 512:1024], in0=pb[5][:, 0:512], in1=tabs_b[:, 512:1024],
                op=ALU.mult), "dve")
            P.w("tensor", "dve", prod_ck)
            P.do("tensor", lambda e: e.matmul(pb[6][0:RD, 0:512], P_["idn2_b"][:],
                                              qr_prod[:, 0:512], start=True, stop=True))
            fold_mm = P.do("tensor", lambda e: e.matmul(psD[0:RD, 0:512], P_["idn2_b"][:],
                                                        qr_prod[:, 512:1024],
                                                        start=True, stop=True), "pe")
            P.w("vector", "pe", fold_mm)
            if h >= 1:
                P.w("vector", "qxC", CK[("qrd", h - 1)])
            P.do("vector", lambda e: e.tensor_copy(out=qr_tmp[:, 0:512], in_=pb[6][0:RD, 0:512]))
            q_done[h] = P.do("vector", lambda e: e.tensor_copy(out=qr_tmp[:, 512:1024],
                                                               in_=psD[0:RD, 0:512]), "dve")
            P.w("sync", "dve", q_done[h])
            CK[("qrd", h)] = P.dma("sync", "qxC", QrA[RD * (h % 2):RD * (h % 2) + RD, h // 2, :],
                                   qr_tmp[:])
        CK["qra_done"] = CK[("qrd", NH - 1)]
        P.join()

        P.w("vector", "qsC", CK["xnT_spill"])
        P.w("vector", "qxC", CK["qra_done"])
        kv_apply = [0, 0]
        xT_free = [0, 0, 0]
        xr_free = [0, 0]
        ckst_free = [0, 0]
        tr_evac = 0
        kpe_evac = 0
        muT_ack = 0
        kvb_pe = [0, 0]
        rs_dve = [0, 0]
        for t in range(2):
            for half in range(2):
                slot = (2 * t + half) % 3
                P.w("gpsimd", "pe", xT_free[slot])
                CK[("xT", t, half)] = P.dma(
                    "gpsimd", "qwT%d" % slot, xT_st[slot][:],
                    g["xT_c"].rearrange("(c p) n -> p c n", p=128)[:, :, 512 * t + 256 * half:
                                                                  512 * t + 256 * half + 256])
            P.w("gpsimd", "dve", ckst_free[t % 2])
            P.dma("gpsimd", "qwA" if t % 2 == 0 else "qwB", ck_st[t % 2][:],
                  g["cos_kc"].rearrange("(s p) r -> p s r", p=128)[:, 4 * t:4 * t + 4, :])
            tab_dma = P.dma("gpsimd", "qwA" if t % 2 == 0 else "qwB", sk_st[t % 2][:],
                            g["sin_kc"].rearrange("(s p) r -> p s r", p=128)[:, 4 * t:4 * t + 4, :])
            for s in range(4):
                sl = (4 * t + s) % 2
                P.w("gpsimd", "act", xr_free[sl])
                xr_dma = P.dma("gpsimd", "qwR%d" % sl, xr_st[sl][:],
                               g["x_rowc"][512 * t + 128 * s:512 * t + 128 * s + 128, :])
                P.w("vector", "qwR%d" % sl, xr_dma)
                red = P.do("vector", lambda e, sl=sl, s=s: e.reduce_sum(
                    st2[:, s:s + 1], xr_st[sl][:], axis=AX.X), "dve")
                P.w("scalar", "qwR%d" % sl, xr_dma)
                P.w("scalar", "dve", red)
                xr_free[sl] = P.do("scalar", lambda e, sl=sl, s=s: e.activation(
                    out=xr_st[sl][:], in_=xr_st[sl][:], func=AF.Square,
                    accum_out=st2[:, 4 + s:5 + s]), "act")
            sq_ck = P.cnt["act"]
            P.w("vector", "act", sq_ck)
            dvs(lambda e: e.tensor_scalar(out=st2[:, 8:12], in0=st2[:, 0:4],
                                          scalar1=1.0 / D, scalar2=None, op0=ALU.mult))
            dvs(lambda e: e.tensor_scalar(out=st2[:, 16:20], in0=st2[:, 4:8],
                                          scalar1=1.0 / D, scalar2=1e-5,
                                          op0=ALU.mult, op1=ALU.add))
            dvs(lambda e: e.tensor_tensor(out=st2[:, 20:24], in0=st2[:, 8:12],
                                          in1=st2[:, 8:12], op=ALU.mult))
            dvs(lambda e: e.tensor_tensor(out=st2[:, 16:20], in0=st2[:, 16:20],
                                          in1=st2[:, 20:24], op=ALU.subtract))
            dv_e = dvs(lambda e: e.tensor_scalar(out=st2[:, 24:28], in0=st2[:, 16:20],
                                                 scalar1=1e-6, scalar2=None, op0=ALU.mult))
            P.w("scalar", "dve", dv_e)
            srs = acs(lambda e: e.activation(out=st2[:, 20:24], in_=st2[:, 16:20],
                                             func=AF.Sqrt))
            P.w("vector", "act", srs)
            invs = dvs(lambda e: e.reciprocal(out=st2[:, 12:16], in_=st2[:, 20:24]))
            mu_cast = dvs(lambda e: e.tensor_copy(out=mu_bf[:, 0:4], in_=st2[:, 8:12]))
            P.w("tensor", "dve", mu_cast)
            P.w("tensor", "act", muT_ack)
            trm = 0
            for s in range(4):
                trm = P.do("tensor", lambda e, s=s: e.matmul(
                    pb[3][0:1, ts(s, 128)], mu_bf[:, s:s + 1],
                    P_["idn_b"][:], start=True, stop=True), "pe" if s == 3 else None)
            P.w("scalar", "pe", trm)
            P.w("scalar", "pe", CK.get(("rank1_pe", t - 1), 0))
            muT_ack = P.do("scalar", lambda e: e.activation(out=muT_row[:], in_=pb[3][0:1, 0:512],
                                                            func=AF.Copy), "act")
            for s in range(4):
                par = s % 2
                pkv = pb[par]
                pkp = pb[2] if par == 0 else pb[6]
                P.w("tensor", "qwT%d" % ((2 * t + s // 2) % 3), CK[("xT", t, s // 2)])
                P.w("tensor", "act", kv_apply[par])
                P.w("tensor", "act", muT_ack)
                for c in range(16):
                    P.do("tensor", lambda e, t=t, s=s, c=c, pkv=pkv: e.matmul(
                        pkv[:, 0:512], xT_st[(2 * t + s // 2) % 3][:, c, ts(s % 2, 128)],
                        wdkv_b[:, c, :], start=(c == 0), stop=False))
                    P.do("tensor", lambda e, t=t, s=s, c=c, pkp=pkp: e.matmul(
                        pkp[:, 0:128], xT_st[(2 * t + s // 2) % 3][:, c, ts(s % 2, 128)],
                        wkr_b[:, c, :], start=(c == 0), stop=False))
                kv_mm = P.do("tensor", lambda e, s=s, pkv=pkv: e.matmul(
                    pkv[:, 0:512], muT_row[0:1, ts(s, 128)], P_["ncol_dkv"][:],
                    start=False, stop=True), "pe")
                kpe_mm = P.do("tensor", lambda e, s=s, pkp=pkp: e.matmul(
                    pkp[:, 0:128], muT_row[0:1, ts(s, 128)], P_["ncol_kr"][:],
                    start=False, stop=True), "pe")
                if s == 3:
                    xT_free[(2 * t) % 3] = P.cnt["pe"]
                    xT_free[(2 * t + 1) % 3] = P.cnt["pe"]
                    CK[("rank1_pe", t)] = P.cnt["pe"]
                P.w("scalar", "pe", kv_mm)
                if (4 * t + s) >= 2:
                    P.w("scalar", "pe", kvb_pe[par])
                P.do("scalar", lambda e, par=par, pkv=pkv, s=s: e.activation(
                    out=kv_b[par][:], in_=pkv[:, 0:512], func=AF.Square,
                    accum_out=st2b[:, 4 * s:4 * s + 1]), "act")
                sq2 = P.cnt["act"]
                P.w("vector", "act", sq2)
                dvs(lambda e, s=s: e.tensor_scalar(
                    out=st2b[:, 4 * s + 1:4 * s + 2], in0=st2b[:, 4 * s:4 * s + 1],
                    scalar1=1.0 / KVL, scalar2=None, op0=ALU.mult))
                dvv = dvs(lambda e, s=s: e.tensor_tensor(
                    out=st2b[:, 4 * s + 1:4 * s + 2], in0=st2b[:, 4 * s + 1:4 * s + 2],
                    in1=st2[:, 24 + s:25 + s], op=ALU.add))
                P.w("scalar", "dve", dvv)
                sr2 = acs(lambda e, s=s: e.activation(
                    out=st2b[:, 4 * s + 2:4 * s + 3], in_=st2b[:, 4 * s + 1:4 * s + 2],
                    func=AF.Sqrt))
                P.w("vector", "act", sr2)
                rc2 = dvs(lambda e, s=s: e.reciprocal(
                    out=st2b[:, 4 * s + 3:4 * s + 4], in_=st2b[:, 4 * s + 2:4 * s + 3]))
                P.w("scalar", "dve", rc2)
                kv_apply[par] = acs(lambda e, par=par, pkv=pkv, s=s: e.activation(
                    out=kv_b[par][:], in_=pkv[:, 0:512], func=AF.Copy,
                    scale=st2b[:, 4 * s + 3:4 * s + 4]))
                P.w("scalar", "pe", kpe_mm)
                if s == 0:
                    P.w("scalar", "dve", CK.get(("rope", t - 1), 0))
                    P.w("scalar", "pe", CK.get(("kpetr", t - 1), 0))
                kpe_ap = P.do("scalar", lambda e, s=s, pkp=pkp: e.activation(
                    out=kpe_sb[:, s, :], in_=pkp[:, 0:128], func=AF.Copy,
                    scale=st2[:, 12 + s:13 + s]), "act")
                if s == 3:
                    CK[("kpe_ap", t)] = kpe_ap
                P.w("tensor", "act", kv_apply[par])
                P.w("tensor", "dve", tr_evac)
                for q in range(4):
                    v = P.do("tensor", lambda e, q=q, par=par: e.transpose(
                        pbv(4)[:, 256 * q + 128 * par:256 * q + 128 * par + 128],
                        kv_b[par][:, ts(q, 128)], P_["idn_b"][:]),
                        "pe" if q == 3 else None)
                kvb_pe[par] = v
                if par == 1:
                    P.w("vector", "pe", kvb_pe[1])
                    tr_evac = P.do("vector", lambda e, t=t, s=s: e.tensor_copy(
                        out=kv_own[:, :, 512 * t + 128 * (s - 1):512 * t + 128 * (s - 1) + 256],
                        in_=pbv(4).rearrange("p (q r) -> p q r", q=4)), "dve")
            P.w("vector", "act", CK[("kpe_ap", t)])
            P.w("vector", "qwA" if t % 2 == 0 else "qwB", tab_dma)

            def _x(i):
                return kpe_sb.rearrange("p s (h r) -> p s h r", h=4)[:, :, i, :]

            ctile = ck_st[t % 2]
            stile = sk_st[t % 2]
            dvs(lambda e: e.tensor_tensor(out=rtmp[0][:], in0=_x(1), in1=stile[:],
                                          op=ALU.mult))
            dvs(lambda e: e.tensor_tensor(out=rtmp[1][:], in0=_x(0), in1=ctile[:],
                                          op=ALU.mult))
            dvs(lambda e: e.tensor_tensor(out=_x(2), in0=_x(2), in1=stile[:],
                                          op=ALU.mult))
            dvs(lambda e: e.tensor_tensor(out=_x(3), in0=_x(3), in1=ctile[:],
                                          op=ALU.mult))
            dvs(lambda e: e.tensor_tensor(out=_x(1), in0=_x(2), in1=_x(3),
                                          op=ALU.add))
            dvs(lambda e: e.tensor_tensor(out=_x(0), in0=rtmp[1][:], in1=rtmp[0][:],
                                          op=ALU.subtract))
            dvs(lambda e: e.tensor_copy(out=_x(2), in_=_x(0)))
            ckst_free[t % 2] = dvs(lambda e: e.tensor_copy(out=_x(3), in_=_x(1)))
            rope_ck = ckst_free[t % 2]
            CK[("rope", t)] = rope_ck
            P.w("tensor", "dve", rope_ck)
            P.w("tensor", "dve", kpe_evac)
            tr2 = 0
            for s in range(4):
                tr2 = P.do("tensor", lambda e, s=s: e.transpose(pbv(5)[:, ts(s, 128)],
                                                                kpe_sb[:, s, :], P_["idn_b"][:]),
                           "pe" if s == 3 else None)
            CK[("kpetr", t)] = tr2
            P.w("vector", "pe", tr2)
            kpe_evac = P.do("vector", lambda e, t=t: e.tensor_copy(
                out=kpe_own[:, ts(t, 512)], in_=pbv(5)[:, 0:512]), "dve")
        P.w("scalar", "dve", max(tr_evac, kpe_evac))
        P.dma("scalar", "qsC", g["kv2_sh"][:, 0:4096],
              kv_own[:].rearrange("p c k -> p (c k)"))
        sp2 = P.dma("scalar", "qsC", g["kv2_sh"][:, 4096:5120], kpe_own[:])
        P.w("gpsimd", "qsC", sp2)
        P.do("gpsimd", lambda e: e.collective_compute(
            "AllGather", ALU.bypass, replica_groups=[list(range(8))],
            ins=[g["kv2_sh"].opt()], outs=[g["kv2_gx"].opt()]), "cc2")
        P.w("sync", "cc2", 1)
        P.w("sync", "qsC", CK["xnT_spill"])
        P.w("sync", "qxC", CK["qra_done"])
        for r in range(8):
            P.dma("sync", "qxC", kvT[:, :, ts(r, 1024)],
                  g["kv2_gx"][r, :, 0:4096].rearrange("p (c k) -> p c k", c=4))
            CK["kv_loaded"] = P.dma("sync", "qxC", kpeT[:, ts(r, 1024)],
                                    g["kv2_gx"][r, :, 4096:5120])
        P.join()

        P.w("tensor", "qxC", CK["kv_loaded"])
        cvt3 = 0
        spill = 0
        prod_pe = 0
        for j in range(2):
            P.w("scalar", "dve", cvt3)
            st_dma = P.dma("scalar", "qsA", wuk_st1[:],
                           g["w_uk_own"].rearrange("(q p) n -> p q n", p=128)[:, :, ts(j, 128)])
            st_dma = P.dma("scalar", "qsA", wuv_st1[:],
                           g["w_uv_own"].rearrange("(q p) n -> p q n", p=128)[:, :, ts(j, 128)])
            P.w("vector", "qsA", st_dma)
            P.w("vector", "pe", prod_pe)
            for q in range(4):
                P.do("vector", lambda e, q=q: e.tensor_scalar(
                    out=wuk_b1[:, q, :], in0=wuk_st1[:, q, :],
                    scalar1=P_["gkv_sb"][:, q:q + 1], scalar2=None, op0=ALU.mult))
            for q in range(4):
                cvt3 = P.do("vector", lambda e, q=q: e.tensor_scalar(
                    out=wuv_b1[:, q, :], in0=wuv_st1[:, q, :],
                    scalar1=P_["gkv_sb"][:, q:q + 1], scalar2=None, op0=ALU.mult),
                    "dve" if q == 3 else None)
            P.w("vector", "qsC", spill)
            kst_ev = [0, 0]
            for gge in range(16):
                bk = 5 + (gge % 2)
                P.w("tensor", "dve", cvt3)
                P.w("tensor", "dve", kst_ev[gge % 2])
                for q in range(4):
                    P.do("tensor", lambda e, q=q, gge=gge, bk=bk: e.matmul(
                        pb[bk][:, 0:512], wuk_b1[:, q, :], kvT[:, q, ts(gge, 512)],
                        start=(q == 0), stop=(q == 3)), "pe" if q == 3 else None)
                P.w("vector", "pe", P.cnt["pe"])
                kst_ev[gge % 2] = P.do("vector", lambda e, gge=gge, bk=bk: e.tensor_copy(
                    out=Kst[:, ts(gge, 512)], in_=pb[bk][:, 0:512]), "dve")
            vst_ev = [kst_ev[0], kst_ev[1]]
            for k in range(64):
                bk = 5 + (k % 2)
                P.w("tensor", "dve", vst_ev[k % 2])
                for q in range(4):
                    P.do("tensor", lambda e, q=q, k=k, bk=bk: e.matmul(
                        pb[bk][:, 0:128], kvT[:, q, ts(k, 128)], wuv_b1[:, q, :],
                        start=(q == 0), stop=(q == 3)), "pe" if q == 3 else None)
                P.w("vector", "pe", P.cnt["pe"])
                vst_ev[k % 2] = P.do("vector", lambda e, k=k, bk=bk: e.tensor_copy(
                    out=Vst[:, k, :], in_=pb[bk][:, 0:128]), "dve")
            prod_pe = P.cnt["pe"]
            P.w("scalar", "dve", max(vst_ev))
            P.dma("scalar", "qsC", g["kv3_sh%d" % j][:, 0:S], Kst[:])
            spill = P.dma("scalar", "qsC", g["kv3_sh%d" % j][:, S:2 * S],
                          Vst[:].rearrange("p t v -> p (t v)"))
            P.w("gpsimd", "qsC", spill)
            P.do("gpsimd", lambda e, j=j: e.collective_compute(
                "AllGather", ALU.bypass, replica_groups=[list(range(8))],
                ins=[g["kv3_sh%d" % j].opt()], outs=[g["kv3_gx%d" % j].opt()]), "cc3")
        P.w("sync", "dve", cvt3)
        P.w("gpsimd", "dve", cvt3)
        P.w("scalar", "qsC", spill)
        P.w("vector", "qsC", spill)
        head_norm = [0] * (NH + 1)
        expk = [0, 0]
        pvk = [0, 0]
        den_evac = 0
        kdma = [0, 0]
        vdma = [0, 0]
        pend = [None]
        tail_h = [None]

        def emit_pv(k, kpar, gpar, s4, mk, npr, hi, gidx, h):
            P.w("tensor", "dve", mk)
            if k == 0:
                P.w("tensor", "dve", CK.get("pv_ev1", 0))
                P.w("tensor", "act", CK.get("psd_free", 0))
            stf = (k == 0)
            if 16 * k < 512:
                na = 512 - 16 * k
                P.do("tensor", lambda e, kpar=kpar, gpar=gpar, s4=s4, k=k, na=na, stf=stf: e.matmul(
                    pb[0][:, 16 * k:512], V4_sb[gpar][:, s4, :], P_sb[kpar][:, 0:na],
                    start=stf, stop=(k == 31)))
                P.do("tensor", lambda e, kpar=kpar, gpar=gpar, s4=s4, na=na, npr=npr, stf=stf: e.matmul(
                    pb[1][:, 0:512], V4_sb[gpar][:, s4, :], P_sb[kpar][:, na:npr],
                    start=stf, stop=False))
                P.do("tensor", lambda e, kpar=kpar, k=k, na=na: e.matmul(
                    psD[0:1, 16 * k:512], P_["ones_col"][:], P_sb[kpar][:, 0:na],
                    start=(k == 0), stop=(k == 31)))
                pvk[kpar] = P.do("tensor", lambda e, kpar=kpar, k=k, na=na, npr=npr: e.matmul(
                    psD[32:33, 0:512], P_["ones_col"][:], P_sb[kpar][:, na:npr],
                    start=(k == 0), stop=False), "pe")
            else:
                P.do("tensor", lambda e, kpar=kpar, gpar=gpar, s4=s4, k=k, npr=npr: e.matmul(
                    pb[1][:, 16 * k - 512:512], V4_sb[gpar][:, s4, :], P_sb[kpar][:, 0:npr],
                    start=False, stop=(k == 63)))
                pvk[kpar] = P.do("tensor", lambda e, kpar=kpar, k=k, npr=npr: e.matmul(
                    psD[32:33, 16 * k - 512:512], P_["ones_col"][:], P_sb[kpar][:, 0:npr],
                    start=False, stop=(k == 63)), "pe")
            if s4 == 3:
                CK[("pvdone", gidx)] = P.cnt["pe"]
            if k == 31:
                P.w("vector", "pe", pvk[kpar])
                CK["pv_ev0"] = P.do("vector", lambda e, h=h: e.tensor_copy(
                    out=atT[:, h, 0:512], in_=pb[0][:, 0:512]), "dve")
            elif k == 63:
                P.w("vector", "pe", pvk[kpar])
                CK["pv_ev1"] = P.do("vector", lambda e, h=h: e.tensor_copy(
                    out=atT[:, h, 512:1024], in_=pb[1][:, 0:512]), "dve")

        def emit_tail(h, hi):
            P.w("scalar", "pe", pvk[0])
            P.w("scalar", "pe", pvk[1])
            acs(lambda e: e.activation(out=rec_b[:, 0:512], in_=psD[0:1, :],
                                       func=AF.Copy))
            den_ev = acs(lambda e: e.activation(out=rec_b[:, 512:1024],
                                                in_=psD[32:33, :], func=AF.Copy))
            P.w("tensor", "act", den_ev)
            P.do("tensor", lambda e: e.matmul(pb[6][:, 0:512], P_["ones_row"][:],
                                              rec_b[:, 0:512], start=True, stop=True))
            bc_mm = P.do("tensor", lambda e: e.matmul(psD[:, 0:512], P_["ones_row"][:],
                                                      rec_b[:, 512:1024],
                                                      start=True, stop=True), "pe")
            P.w("scalar", "pe", bc_mm)
            acs(lambda e: e.activation(out=bc_f[:, 0:512], in_=pb[6][:, 0:512],
                                       func=AF.Ln))
            CK["psd_free"] = acs(lambda e: e.activation(out=bc_f[:, 512:1024],
                                                        in_=psD[:, 0:512],
                                                        func=AF.Ln))
            acs(lambda e: e.activation(out=bc_f[:, 0:512], in_=bc_f[:, 0:512],
                                       func=AF.Exp, scale=-1.0))
            lastx = acs(lambda e: e.activation(out=bc_f[:, 512:1024],
                                               in_=bc_f[:, 512:1024],
                                               func=AF.Exp, scale=-1.0))
            P.w("vector", "act", lastx)
            dvs(lambda e, h=h: e.tensor_tensor(
                out=atT[:, h, 0:512], in0=atT[:, h, 0:512], in1=bc_f[:, 0:512],
                op=ALU.mult))
            head_norm[hi] = dvs(lambda e, h=h: e.tensor_tensor(
                out=atT[:, h, 512:1024], in0=atT[:, h, 512:1024],
                in1=bc_f[:, 512:1024], op=ALU.mult))

        for hi, h in enumerate(list(range(0, NH, 2)) + list(range(1, NH, 2))):
            j = h % 2
            rk = h // 2
            base = RD * (h % 2)
            hslot = h // 2
            P.w("sync", "cc3", j + 1)
            P.w("gpsimd", "cc3", j + 1)
            for gge in range(16):
                gidx = 16 * hi + gge
                gpar = gidx % 2
                P.w("sync", "pe", CK.get(("scdone", gidx - 2), 0))
                kdma[gpar] = P.dma("sync", "qxA" if gpar == 0 else "qxB", K_sb[gpar][:],
                                   g["kv3_gx%d" % j][rk, :, 512 * gge:512 * gge + 512])
                P.w("gpsimd", "pe", CK.get(("pvdone", gidx - 2), 0))
                vdma[gpar] = P.dma("gpsimd", "qwA" if gpar == 0 else "qwB", V4_sb[gpar][:],
                                   g["kv3_gx%d" % j][rk, :, S + 512 * gge:S + 512 * gge + 512]
                                   .rearrange("p (t v) -> p t v", t=4))
                P.w("tensor", "qxA" if gpar == 0 else "qxB", kdma[gpar])
                P.w("tensor", "qwA" if gpar == 0 else "qwB", vdma[gpar])
                for s4 in range(4):
                    k = 4 * gge + s4
                    kpar = k % 2
                    npr = 1024 - 16 * k
                    sb0 = 2 if kpar == 0 else 4
                    P.w("tensor", "act", expk[kpar])
                    n1 = min(npr, 512)
                    P.do("tensor", lambda e, sb0=sb0, gpar=gpar, s4=s4, h=h, k=k, n1=n1: e.matmul(
                        pb[sb0][:, 0:n1], K_sb[gpar][:, ts(s4, 128)],
                        QnA[:, h, 16 * k:16 * k + n1], start=True, stop=False))
                    if npr > 512:
                        P.do("tensor", lambda e, sb0=sb0, gpar=gpar, s4=s4, h=h, k=k, npr=npr: e.matmul(
                            pb[sb0 + 1][:, 0:npr - 512], K_sb[gpar][:, ts(s4, 128)],
                            QnA[:, h, 16 * k + 512:1024], start=True, stop=False))
                    scmm = P.do("tensor", lambda e, sb0=sb0, base=base, hslot=hslot, k=k, n1=n1, npr=npr: e.matmul(
                        pb[sb0][:, 0:n1], kpeT[base:base + RD, ts(k, 128)],
                        QrA[base:base + RD, hslot, 16 * k:16 * k + n1],
                        start=False, stop=True),
                        "pe" if npr <= 512 else None)
                    if npr > 512:
                        scmm = P.do("tensor", lambda e, sb0=sb0, base=base, hslot=hslot, k=k, npr=npr: e.matmul(
                            pb[sb0 + 1][:, 0:npr - 512], kpeT[base:base + RD, ts(k, 128)],
                            QrA[base:base + RD, hslot, 16 * k + 512:1024],
                            start=False, stop=True), "pe")
                    if s4 == 3:
                        CK[("scdone", gidx)] = scmm
                    P.w("scalar", "pe", scmm)
                    P.w("scalar", "pe", pvk[kpar])
                    expk[kpar] = P.do("scalar", lambda e, kpar=kpar, sb0=sb0, n1=n1: e.activation(
                        out=P_sb[kpar][:, 0:n1], in_=pb[sb0][:, 0:n1],
                        func=AF.Exp, bias=P_["expb"][:]), "act")
                    if npr > 512:
                        expk[kpar] = P.do("scalar", lambda e, kpar=kpar, sb0=sb0, npr=npr: e.activation(
                            out=P_sb[kpar][:, 512:npr], in_=pb[sb0 + 1][:, 0:npr - 512],
                            func=AF.Exp, bias=P_["expb"][:]), "act")
                    P.w("vector", "act", expk[kpar])
                    mk = P.do("vector", lambda e, kpar=kpar: e.tensor_tensor(
                        out=P_sb[kpar][:, 0:16], in0=P_sb[kpar][:, 0:16],
                        in1=P_["mask_b"][:], op=ALU.mult), "dve")
                    if tail_h[0] is not None and gge == 0 and s4 == 1:
                        emit_tail(*tail_h[0])
                        tail_h[0] = None
                    if pend[0] is not None:
                        emit_pv(**pend[0])
                    pend[0] = dict(k=k, kpar=kpar, gpar=gpar, s4=s4, mk=mk,
                                   npr=npr, hi=hi, gidx=gidx, h=h)
            emit_pv(**pend[0])
            pend[0] = None
            tail_h[0] = (h, hi)
        emit_tail(*tail_h[0])
        tail_h[0] = None
        P.join()

        evac4 = [0, 0]
        sp4 = [0, 0]
        wo_pe_done = 0
        for half in range(2):
            P.w("gpsimd", "pe", wo_pe_done)
            wo_dma = P.dma("gpsimd", "qwE", wo_b[:],
                           g["w_o"].rearrange("(c p) n -> p c n", p=128)[:, :, 1024 * half:1024 * half + 1024])
            for qt in range(8):
                ppar = qt % 2
                qsem = "qsA" if ppar == 0 else "qsB"
                b4 = 0 if ppar == 0 else 2
                P.w("tensor", "qwE", wo_dma)
                P.w("tensor", "dve", evac4[ppar])
                mm4 = 0
                for c in range(16):
                    for nn in range(2):
                        mm4 = P.do("tensor", lambda e, c=c, nn=nn, qt=qt, b4=b4: e.matmul(
                            pb[b4 + nn][:, 0:512],
                            atT[:, c, ts(qt, 128)], wo_b[:, c, ts(nn, 512)],
                            start=(c == 0), stop=(c == 15)),
                            "pe" if (c == 15 and nn == 1) else None)
                P.w("vector", "pe", mm4)
                P.w("vector", qsem, sp4[ppar])
                P.do("vector", lambda e, ppar=ppar, b4=b4: e.tensor_copy(
                    out=ao_st[ppar][:, 0:512], in_=pb[b4][:, 0:512]))
                evac4[ppar] = P.do("vector", lambda e, ppar=ppar, b4=b4: e.tensor_copy(
                    out=ao_st[ppar][:, 512:1024], in_=pb[b4 + 1][:, 0:512]), "dve")
                P.w("scalar", "dve", evac4[ppar])
                sp4[ppar] = P.dma("scalar", qsem,
                                  g["attn_sp"][ts(qt, 128), 1024 * half:1024 * half + 1024],
                                  ao_st[ppar][:])
            wo_pe_done = P.cnt["pe"]
        P.join()

        P.w("vector", "qsA", sp4[0])
        P.w("vector", "qsB", sp4[1])
        P.w("sync", "qsC", CK["xnT_spill"])
        rl = P.dma("sync", "qxC", xnT5[:], g["xnT_sp"][:, :, :])
        P.w("tensor", "qxC", rl)
        rs_mm = 0
        for c in range(16):
            for nn in range(2):
                rs_mm = P.do("tensor", lambda e, c=c, nn=nn: e.matmul(
                    psD[0:1, 0:512] if nn == 0 else psD[32:33, 0:512],
                    P_["ones_col"][:], xnT5[:, c, ts(nn, 512)],
                    start=(c == 0), stop=(c == 15)),
                    "pe" if (c == 15 and nn == 1) else None)
        P.w("scalar", "pe", rs_mm)
        P.do("scalar", lambda e: e.activation(out=rowsum_b[:, 0:512], in_=psD[0:1, :],
                                              func=AF.Copy))
        rs_ck = P.do("scalar", lambda e: e.activation(out=rowsum_b[:, 512:1024],
                                                      in_=psD[32:33, :], func=AF.Copy), "act")
        P.w("tensor", "act", rs_ck)
        silu_ck = [0, 0]
        stt_ck = [0, 0]
        g_pe = {}
        d_pe = {}
        dacc_ck = {}
        hs_done = 0
        d_last_pe = 0
        stt_last = 0
        for blk in range(4):
            for it in range(16):
                gi = 16 * blk + it
                wpar = gi % 2
                P.w("gpsimd", "pe", g_pe.get(gi - 2, 0))
                P.dma("gpsimd", "qwA" if wpar == 0 else "qwB", wg_b[wpar][:],
                      g["wgTr"][:, gi, :, :])
                wgu = P.dma("gpsimd", "qwA" if wpar == 0 else "qwB", wu_b[wpar][:],
                            g["wuTr"][:, gi, :, :])
                P.w("tensor", "qwA" if wpar == 0 else "qwB", wgu)
                P.w("tensor", "qxC", rl)
                P.w("tensor", "act", rs_ck)
                P.w("tensor", "act", silu_ck[0])
                P.w("tensor", "act", silu_ck[1])
                P.w("tensor", "dve", stt_ck[0])
                P.w("tensor", "dve", stt_ck[1])
                if blk >= 1 and it == 0:
                    P.w("tensor", "dve", dacc_ck.get((blk - 1, 15), 0))
                for c in range(16):
                    for rh in range(2):
                        P.do("tensor", lambda e, c=c, rh=rh, wpar=wpar: e.matmul(
                            pb[2 * rh][:, 0:512], wg_b[wpar][:, c, :],
                            xnT5[:, c, ts(rh, 512)], start=(c == 0), stop=False))
                for rh in range(2):
                    P.do("tensor", lambda e, rh=rh: e.matmul(
                        pb[2 * rh][:, 0:512], P_["neg128_row"][:],
                        rowsum_b[:, ts(rh, 512)], start=False, stop=True))
                for c in range(16):
                    for rh in range(2):
                        P.do("tensor", lambda e, c=c, rh=rh, wpar=wpar: e.matmul(
                            pb[2 * rh + 1][:, 0:512], wu_b[wpar][:, c, :],
                            xnT5[:, c, ts(rh, 512)], start=(c == 0), stop=False))
                gmm = 0
                for rh in range(2):
                    gmm = P.do("tensor", lambda e, rh=rh: e.matmul(
                        pb[2 * rh + 1][:, 0:512], P_["neg128_row"][:],
                        rowsum_b[:, ts(rh, 512)], start=False, stop=True),
                        "pe" if rh == 1 else None)
                g_pe[gi] = gmm
                P.w("scalar", "pe", gmm)
                P.w("scalar", "dve", stt_ck[0])
                P.w("scalar", "dve", stt_ck[1])
                for rh in range(2):
                    silu_ck[rh] = P.do("scalar", lambda e, rh=rh, gi=gi: e.activation(
                        out=hs_t[:, ts(rh, 512)], in_=pb[2 * rh][:, 0:512], func=AF.Silu,
                        scale=P_["sg_sb"][:, gi:gi + 1]), "act")
                P.w("vector", "act", silu_ck[1])
                if blk >= 1 and it == 0:
                    P.w("vector", "pe", d_last_pe)
                for rh in range(2):
                    stt_ck[rh] = P.do("vector", lambda e, rh=rh, gi=gi, it=it: e.scalar_tensor_tensor(
                        out=hT[:, it, ts(rh, 512)], in0=pb[2 * rh + 1][:, 0:512],
                        scalar=P_["su_sb"][:, gi:gi + 1], in1=hs_t[:, ts(rh, 512)],
                        op0=ALU.mult, op1=ALU.mult), "dve")
                P.w("tensor", "dve", stt_ck[0])
                P.w("tensor", "dve", stt_ck[1])
                for nn in range(2):
                    hs_mm = P.do("tensor", lambda e, it=it, nn=nn, blk=blk: e.matmul(
                        psD[0:1, 0:512] if nn == 0 else psD[32:33, 0:512],
                        P_["ones_col"][:], hT[:, it, ts(nn, 512)],
                        start=(blk == 0 and it == 0), stop=(blk == 3 and it == 15)),
                        "pe" if nn == 1 else None)
                if it == 15:
                    stt_last = max(stt_ck[0], stt_ck[1])
            if blk == 3:
                P.w("scalar", "pe", hs_mm)
                hf = P.do("scalar", lambda e: e.activation(out=hsum_f[:, 0:512],
                                                           in_=psD[0:1, :], func=AF.Copy), "act")
                hf = P.do("scalar", lambda e: e.activation(out=hsum_f[:, 512:1024],
                                                           in_=psD[32:33, :], func=AF.Copy), "act")
                P.w("vector", "act", hf)
                dvs(lambda e: e.tensor_copy(out=hsum_hi[:], in_=hsum_f[:]))
                hs_done = dvs(lambda e: e.tensor_tensor(
                    out=hsum_lo[:], in0=hsum_f[:], in1=hsum_hi[:],
                    op=ALU.subtract))
            for ht_i in range(16):
                dpar = ht_i % 2
                P.w("gpsimd", "pe", d_pe.get((blk, ht_i - 2), 0))
                wdd = P.dma("gpsimd", "qwC" if dpar == 0 else "qwD", wd_b[dpar][:],
                            g["wdTr"][:, ht_i, 16 * blk:16 * blk + 16, :])
                P.w("tensor", "qwC" if dpar == 0 else "qwD", wdd)
                P.w("tensor", "dve", stt_last)
                P.w("tensor", "dve", dacc_ck.get((blk, ht_i - 1), 0))
                if blk == 3:
                    P.w("tensor", "dve", hs_done)
                for c in range(16):
                    for nn in range(2):
                        dmm = P.do("tensor", lambda e, c=c, nn=nn, dpar=dpar, blk=blk: e.matmul(
                            pb[4 + nn][:, 0:512], wd_b[dpar][:, c, :],
                            hT[:, c, ts(nn, 512)], start=(c == 0),
                            stop=(blk != 3 and c == 15)),
                            "pe" if (blk != 3 and c == 15 and nn == 1) else None)
                if blk == 3:
                    for nn in range(2):
                        P.do("tensor", lambda e, nn=nn: e.matmul(
                            pb[4 + nn][:, 0:512], P_["neg128_row"][:],
                            hsum_hi[:, ts(nn, 512)], start=False, stop=False))
                    for nn in range(2):
                        dmm = P.do("tensor", lambda e, nn=nn: e.matmul(
                            pb[4 + nn][:, 0:512], P_["neg128_row"][:],
                            hsum_lo[:, ts(nn, 512)], start=False, stop=True),
                            "pe" if nn == 1 else None)
                d_pe[(blk, ht_i)] = dmm
                P.w("vector", "pe", dmm)
                if blk >= 1:
                    P.w("vector", "dve", dacc_ck[(blk - 1, ht_i)])
                if blk == 0:
                    P.do("vector", lambda e, ht_i=ht_i: e.tensor_copy(
                        out=acc[:, ht_i, 0:512], in_=pb[4][:, 0:512]))
                    dacc_ck[(blk, ht_i)] = P.do("vector", lambda e, ht_i=ht_i: e.tensor_copy(
                        out=acc[:, ht_i, 512:1024], in_=pb[5][:, 0:512]), "dve")
                else:
                    P.do("vector", lambda e, ht_i=ht_i: e.tensor_tensor(
                        out=acc[:, ht_i, 0:512], in0=acc[:, ht_i, 0:512],
                        in1=pb[4][:, 0:512], op=ALU.add))
                    dacc_ck[(blk, ht_i)] = P.do("vector", lambda e, ht_i=ht_i: e.tensor_tensor(
                        out=acc[:, ht_i, 512:1024], in0=acc[:, ht_i, 512:1024],
                        in1=pb[5][:, 0:512], op=ALU.add), "dve")
            d_last_pe = P.cnt["pe"]
        sc_ck = 0
        for ht_i in range(16):
            P.w("vector", "dve", dacc_ck[(3, ht_i)])
            sc_ck = dvs(lambda e, ht_i=ht_i: e.tensor_scalar(
                out=acc[:, ht_i, :], in0=acc[:, ht_i, :],
                scalar1=P_["sd_sb"][:, ht_i:ht_i + 1], scalar2=None, op0=ALU.mult))
        fin_dve = [0, 0]
        y_dmas = [0] * 8
        y_dma = 0
        for rt in range(8):
            rpar = rt % 2
            P.w("sync", "dve", sc_ck)
            P.w("sync", "dve", fin_dve[rpar])
            x5d = P.dma("sync", "qxA" if rpar == 0 else "qxB", x5_st[rpar][:],
                        g["x_own"][ts(rt, 128), :])
            P.w("scalar", "qsA" if rpar == 0 else "qsB", sp4[rpar])
            P.w("scalar", "dve", sc_ck)
            P.w("scalar", "dve", fin_dve[rpar])
            atd = P.dma("scalar", "qsA" if rpar == 0 else "qsB", at_st[rpar][:],
                        g["attn_sp"][ts(rt, 128), :])
            P.w("tensor", "dve", sc_ck)
            P.w("tensor", "dve", fin_dve[0])
            P.w("tensor", "dve", fin_dve[1])
            trf = 0
            for ht_i in range(16):
                trf = P.do("tensor", lambda e, ht_i=ht_i, rt=rt: e.transpose(
                    pb[ht_i // 4][:, ts(ht_i % 4, 128)], acc[:, ht_i, ts(rt, 128)],
                    P_["idn_f"][:]),
                    "pe" if ht_i == 15 else None)
            P.w("vector", "pe", trf)
            P.w("vector", "qxA" if rpar == 0 else "qxB", x5d)
            P.w("vector", "qsA" if rpar == 0 else "qsB", atd)
            if rt >= 2:
                P.w("vector", "qxD" if rpar == 0 else "qxE", y_dmas[rt - 2])
            for bb in range(4):
                dvs(lambda e, rpar=rpar, bb=bb: e.tensor_tensor(
                    out=out_st[rpar][:, ts(bb, 512)], in0=pb[bb][:, 0:512],
                    in1=x5_st[rpar][:, ts(bb, 512)], op=ALU.add))
            fin_dve[rpar] = dvs(lambda e, rpar=rpar: e.tensor_tensor(
                out=out_st[rpar][:], in0=out_st[rpar][:], in1=at_st[rpar][:], op=ALU.add))
            P.w("sync", "dve", fin_dve[rpar])
            y_dmas[rt] = y_dma = P.dma("sync", "qxD" if rpar == 0 else "qxE",
                                       g["y"][ts(rt, 128), :], out_st[rpar][:])
        for eng in ENGS:
            P.w(eng, "qxD", y_dmas[6])
            P.w(eng, "qxE", y_dmas[7])
        return P.cnt

    counts = []
    with nc.Block() as block:
        deco = {"sync": block.sync, "gpsimd": block.gpsimd, "vector": block.vector,
                "scalar": block.scalar, "tensor": block.tensor}
        for name in ENGS:
            holder = {}

            def run(eng_obj, name=name, holder=holder):
                Pp = Emit(nc, name, semh)
                holder["cnt"] = emit(Pp)

            deco[name](run)
            counts.append(dict(holder["cnt"]))
    for c in counts[1:]:
        assert c == counts[0], ("pass counter drift", counts[0], c)
    return nc


_CACHE = {}


def _get_nc():
    if "nc" not in _CACHE:
        _CACHE["nc"] = build_nc()
    return _CACHE["nc"]


def kernel(L0_attn_x, L0_attn_w_dq, L0_attn_g_q, L0_attn_w_uq, L0_attn_w_dkv,
           L0_attn_w_kr, L0_attn_g_kv, L0_attn_w_uk, L0_attn_w_uv, L0_attn_w_o,
           L0_ffn_gate_w, L0_ffn_gate_scale, L0_ffn_up_w, L0_ffn_up_scale,
           L0_ffn_down_w, L0_ffn_down_scale):
    x = np.ascontiguousarray(np.asarray(L0_attn_x, dtype=np.float32)[0])
    f32 = np.float32
    ac = np.ascontiguousarray
    cos_full, sin_full = _rope_tables_fp32()

    def _u8(w):
        a = np.ascontiguousarray(np.asarray(w, np.int32))
        return a.view(np.uint8)[:, 0::4]

    shared = {
        "w_dq": ac(np.asarray(L0_attn_w_dq, f32)),
        "w_uq": ac(np.asarray(L0_attn_w_uq, f32)),
        "gq_col": ac(np.asarray(L0_attn_g_q, f32).reshape(6, 128).T),
        "w_dkv": ac(np.asarray(L0_attn_w_dkv, f32)),
        "w_kr": ac(np.asarray(L0_attn_w_kr, f32)),
        "gkv_col": ac(np.asarray(L0_attn_g_kv, f32).reshape(4, 128).T),
        "w_o": ac(np.asarray(L0_attn_w_o, f32)),
        "wgTr": ac(_u8(L0_ffn_gate_w).T.reshape(16, 128, 64, 128).transpose(1, 2, 0, 3)),
        "wuTr": ac(_u8(L0_ffn_up_w).T.reshape(16, 128, 64, 128).transpose(1, 2, 0, 3)),
        "wdTr": ac(_u8(L0_ffn_down_w).T.reshape(64, 128, 16, 128).transpose(1, 2, 0, 3)),
        "sg_col": ac(np.asarray(L0_ffn_gate_scale, f32).reshape(64, 128).T),
        "su_col": ac(np.asarray(L0_ffn_up_scale, f32).reshape(64, 128).T),
        "sd_col": ac(np.asarray(L0_ffn_down_scale, f32).reshape(16, 128).T),
        "idn": np.eye(128, dtype=f32),
        "idn2": ac(np.tile(np.eye(RD, dtype=f32), (2, 1))),
    }

    in_maps = []
    for c in range(NCORE):
        pos = np.arange(c, S, 8)
        cc = cos_full[pos].T
        ss = sin_full[pos].T
        ctab = ac(np.concatenate([cc, cc], axis=0).astype(f32))
        stab = ac(np.concatenate([-ss, ss], axis=0).astype(f32))
        kr = np.arange(128, dtype=np.int64)[:, None]
        jj = np.arange(16, dtype=np.int64)[None, :]
        mask16 = (kr <= (c + 8 * jj)).astype(f32)
        m = dict(shared)
        m["x_own"] = ac(x[c::8])
        m["ctab"] = ctab
        m["stab"] = stab
        m["mask16"] = ac(mask16)
        m["w_uk_own"] = ac(np.asarray(L0_attn_w_uk, f32)[:, 256 * c:256 * c + 256])
        m["w_uv_own"] = ac(np.asarray(L0_attn_w_uv, f32)[:, 256 * c:256 * c + 256])
        m["xT_c"] = ac(x[1024 * c:1024 * c + 1024].T)
        m["x_rowc"] = ac(x[1024 * c:1024 * c + 1024])
        m["cos_kc"] = ac(cos_full[1024 * c:1024 * c + 1024])
        m["sin_kc"] = ac(sin_full[1024 * c:1024 * c + 1024])
        in_maps.append(m)

    nc = _get_nc()
    res = run_bass_kernel_spmd(nc, in_maps, core_ids=list(range(NCORE)))
    out = np.empty((S, D), dtype=np.float32)
    for c in range(NCORE):
        out[c::8] = res.results[c]["y"]
    return out.reshape(1, S, D)


# version 09 (final)
# speedup vs baseline: 1.0774x; 1.0189x over previous
import numpy as np
import ml_dtypes
import concourse.bass as bass
import concourse.mybir as mybir
from concourse.bass_utils import run_bass_kernel_spmd

F32 = mybir.dt.float32
BF16 = mybir.dt.bfloat16
I32 = mybir.dt.int32
AF = mybir.ActivationFunctionType
ALU = mybir.AluOpType
AX = mybir.AxisListType

S = 8192
D = 2048
NH = 16
NOPE = 128
RD = 64
VD = 128
QL = 768
KVL = 512
FI = 8192
NCORE = 8
R = 1024
SCALE = 1.0 / float(np.sqrt(192.0))
EXPB = -16.0
KB = 1024

SEMS = ("ph", "pe", "dve", "act", "cc2", "cc3",
        "qxA", "qxB", "qxC", "qxD", "qxE",
        "qwA", "qwB", "qwC", "qwD", "qwE", "qwT0", "qwT1", "qwT2", "qwR0", "qwR1",
        "qsA", "qsB", "qsC")
ENGS = ("sync", "gpsimd", "vector", "scalar", "tensor")


def ts(i, n):
    return slice(i * n, (i + 1) * n)


class Emit:

    def __init__(self, nc, cur, semh):
        self.nc = nc
        self.cur = cur
        self.h = semh
        self.cnt = {k: 0 for k in SEMS}
        self.wm = {k: 0 for k in SEMS}
        self.E = {"sync": nc.sync, "gpsimd": nc.gpsimd, "vector": nc.vector,
                  "scalar": nc.scalar, "tensor": nc.tensor}[cur]

    def on(self, eng):
        return eng == self.cur

    def w(self, eng, sem, val):
        if not val or val <= 0:
            return
        if not self.on(eng):
            return
        if val <= self.wm[sem]:
            return
        self.wm[sem] = val
        self.E.wait_ge(self.h[sem], val)

    def do(self, eng, fn, sem=None, inc=1):
        if self.on(eng):
            inst = fn(self.E)
            if sem is not None and inst is not None:
                inst.then_inc(self.h[sem], inc)
        if sem is not None:
            self.cnt[sem] += inc
            return self.cnt[sem]
        return None

    def dma(self, qeng, sem, out, in_):
        assert sem in SEMS and sem[0] == "q", sem
        if self.on(qeng):
            self.E.dma_start(out=out, in_=in_).then_inc(self.h[sem], 16)
        self.cnt[sem] += 16
        return self.cnt[sem]

    def join(self):
        own = {"vector": "dve", "scalar": "act", "tensor": "pe"}.get(self.cur)
        if own is not None and self.cnt[own] > self.wm[own]:
            self.wm[own] = self.cnt[own]
            self.E.wait_ge(self.h[own], self.cnt[own])
        self.cnt["ph"] += 5
        tgt = self.cnt["ph"]
        self.E.nop().then_inc(self.h["ph"], 1)
        self.E.wait_ge(self.h["ph"], tgt)
        for s in ("pe", "dve", "act", "ph"):
            self.wm[s] = max(self.wm[s], self.cnt[s])


def _rope_tables_fp32():
    try:
        import jax
        import jax.numpy as jnp
        cpu = jax.devices("cpu")[0]
        with jax.default_device(cpu):
            inv = 1.0 / (10000.0 ** (jnp.arange(0, RD, 2, dtype=jnp.float32) / RD))
            t = jnp.arange(S, dtype=jnp.float32)
            fr = jnp.outer(t, inv)
            c = np.asarray(jnp.cos(fr), dtype=np.float32)
            s = np.asarray(jnp.sin(fr), dtype=np.float32)
        return c, s
    except Exception:
        inv = (1.0 / (10000.0 ** (np.arange(0, RD, 2, dtype=np.float32) /
                                  np.float32(RD)))).astype(np.float32)
        t = np.arange(S, dtype=np.float32)
        fr = np.outer(t, inv).astype(np.float32)
        return np.cos(fr).astype(np.float32), np.sin(fr).astype(np.float32)


def build_nc():
    nc = bass.Bass("TRN2", target_bir_lowering=False)
    g = {}

    def din(name, shape, dtype=F32):
        g[name] = nc.dram_tensor(name, shape, dtype, kind="ExternalInput").ap()

    din("x_own", [R, D]);      din("xT_c", [D, R]);      din("x_rowc", [R, D])
    din("w_dq", [D, QL]);      din("w_uq", [QL, NH * 192]); din("gq_col", [128, 6])
    din("w_dkv", [D, KVL]);    din("w_kr", [D, RD]);     din("gkv_col", [128, 4])
    din("w_uk_own", [KVL, 2 * NOPE]); din("w_uv_own", [KVL, 2 * VD]); din("w_o", [D, D])
    U8 = mybir.dt.uint8
    din("wgTr", [128, 64, 16, 128], U8)
    din("wuTr", [128, 64, 16, 128], U8)
    din("wdTr", [128, 16, 64, 128], U8)
    din("sg_col", [128, 64]);  din("su_col", [128, 64]); din("sd_col", [128, 16])
    din("cos_kc", [R, 32]);    din("sin_kc", [R, 32])
    din("ctab", [RD, R]);      din("stab", [RD, R])
    din("mask16", [128, 16]);  din("idn", [128, 128]);  din("idn2", [128, RD])

    g["y"] = nc.dram_tensor("y", [R, D], F32, kind="ExternalOutput").ap()
    g["xnT_sp"] = nc.dram_tensor("xnT_sp", [128, 16, R], BF16, kind="Internal").ap()
    g["attn_sp"] = nc.dram_tensor("attn_sp", [R, D], BF16, kind="Internal").ap()
    g["kv2_sh"] = nc.dram_tensor("kv2_sh", [128, 5 * R], BF16, kind="Internal").ap()
    g["kv2_gx"] = nc.dram_tensor("kv2_gx", [8, 128, 5 * R], BF16,
                                 kind="Internal", addr_space="Shared").ap()
    for j in range(2):
        g["kv3_sh%d" % j] = nc.dram_tensor("kv3_sh%d" % j, [128, 2 * S], BF16,
                                           kind="Internal").ap()
        g["kv3_gx%d" % j] = nc.dram_tensor("kv3_gx%d" % j, [8, 128, 2 * S], BF16,
                                           kind="Internal", addr_space="Shared").ap()

    P_ = {}

    def sp(name, shape, dtype):
        P_[name] = nc.alloc_sbuf_tensor(name, shape, dtype)

    sp("ones_col", [128, 1], BF16)
    sp("ones_row", [1, 128], BF16)
    sp("neg128_row", [1, 128], BF16)
    sp("idn_f", [128, 128], F32)
    sp("idn_b", [128, 128], BF16)
    sp("idn2_f", [128, RD], F32)
    sp("expb", [128, 1], F32)
    sp("idn2_b", [128, RD], BF16)
    sp("zro", [128, 512], BF16)
    sp("mask_b", [128, 16], BF16)
    sp("gq_sb", [128, 6], F32)
    sp("gkv_sb", [128, 4], F32)
    sp("sg_sb", [128, 64], F32)
    sp("su_sb", [128, 64], F32)
    sp("sd_sb", [128, 16], F32)
    sp("ncol_dkv", [1, KVL], BF16)
    sp("ncol_kr", [1, 128], BF16)

    pb = [nc.alloc_psum_tensor("pb%d" % i, [128, 512], F32).ap() for i in range(7)]
    psD = nc.alloc_psum_tensor("psD", [128, 512], F32).ap()

    def pbv(i):
        return pb[i].bitcast(BF16)

    ARENA = 200 * KB
    arena_t = nc.alloc_sbuf_tensor("arena", [128, ARENA], mybir.dt.uint8)
    abase = nc.lookup_mloc(arena_t).addr
    _seen = []

    def at(name, off_kb, shape, dtype):
        off = int(round(off_kb * KB))
        nb = int(np.prod(shape[1:])) * mybir.dt.size(dtype)
        assert off % 32 == 0 and off + nb <= ARENA, (name, off_kb, nb)
        _seen.append((name, off, nb))
        return nc.alloc_sbuf_tensor_at(name, shape, dtype, offset=abase + off).ap()

    kvT = at("kvT", 0, [128, 4, S], BF16)
    kpeT = at("kpeT", 64, [128, S], BF16)
    QnA = at("QnA", 80, [128, NH, R], BF16)
    QrA = at("QrA", 112, [128, 8, R], BF16)
    atT = at("atT", 128, [128, NH, R], BF16)
    wdkv_b = at("wdkv_b", 184, [128, 16, KVL], BF16)
    wkr_b = at("wkr_b", 180, [128, 16, 128], BF16)
    xo_st = [at("xo_st0", 0, [128, D], F32), at("xo_st1", 8, [128, D], F32)]
    xn_t = [at("xn_t0", 16, [128, D], BF16), at("xn_t1", 20, [128, D], BF16)]
    xnT = at("xnT", 24, [128, 16, R], BF16)
    qlnT = at("qlnT", 56, [128, 6, R], BF16)
    qr_tmp = at("qr_tmp", 72, [RD, R], BF16)
    st1 = at("st1", 74, [128, 64], F32)
    qr_prod = at("qr_prod", 75, [128, R], BF16)
    tabs_f = at("tabs_f", 176, [128, R], F32)
    tabs_b = at("tabs_b", 170, [128, R], BF16)
    wdq_b = at("wdq_b", 128, [128, 16, QL], BF16)
    wuq_st = [at("wuq_st0", 152, [128, 6, 192], BF16),
              at("wuq_st1", 156.5, [128, 6, 192], BF16)]
    wuq_b = [at("wuq_b0", 161, [128, 6, 256], BF16),
             at("wuq_b1", 164, [128, 6, 256], BF16)]
    ql_t = [at("ql_t0", 167, [128, QL], BF16), at("ql_t1", 168.5, [128, QL], BF16)]
    xT_st = [at("xT_st0", 128, [128, 16, 256], BF16),
             at("xT_st1", 136, [128, 16, 256], BF16),
             at("xT_st2", 144, [128, 16, 256], BF16)]
    xr_st = [at("xr_st0", 0, [128, D], BF16), at("xr_st1", 4, [128, D], BF16)]
    ck_st = [at("ck_st0", 8, [128, 4, 32], BF16), at("ck_st1", 8.25, [128, 4, 32], BF16)]
    sk_st = [at("sk_st0", 8.5, [128, 4, 32], BF16), at("sk_st1", 8.75, [128, 4, 32], BF16)]
    kv_b = [at("kv_b0", 9, [128, KVL], BF16), at("kv_b1", 10, [128, KVL], BF16)]
    kpe_sb = at("kpe_sb", 11, [128, 4, 128], BF16)
    rtmp = [at("rtmp0", 12, [128, 4, 32], BF16), at("rtmp1", 12.25, [128, 4, 32], BF16)]
    muT_row = at("muT_row", 13, [1, 512], BF16)
    mu_bf = at("mu_bf", 14.25, [128, 8], BF16)
    st2 = at("st2", 12.5, [128, 64], F32)
    st2b = at("st2b", 14.5, [128, 16], F32)
    kv_own = at("kv_own", 16, [128, 4, R], BF16)
    kpe_own = at("kpe_own", 24, [128, R], BF16)
    wuk_st1 = at("wuk_st1", 160, [128, 4, 128], F32)
    wuv_st1 = at("wuv_st1", 162, [128, 4, 128], F32)
    wuk_b1 = at("wuk_b1", 164, [128, 4, 128], BF16)
    wuv_b1 = at("wuv_b1", 165, [128, 4, 128], BF16)
    Kst = at("Kst", 166, [128, S], BF16)
    Vst = at("Vst", 182, [128, 64, 128], BF16)
    K_sb = [at("K_sb0", 160, [128, 512], BF16), at("K_sb1", 161, [128, 512], BF16)]
    V4_sb = [at("V4_sb0", 162, [128, 4, 128], BF16), at("V4_sb1", 163, [128, 4, 128], BF16)]
    P_sb = [at("P_sb0", 164, [128, R], BF16), at("P_sb1", 166, [128, R], BF16)]
    den_f = at("den_f", 168, [1, R], F32)
    rec_b = at("rec_b", 172, [1, R], BF16)
    bc_f = at("bc_f", 174, [128, R], F32)
    wo_b = at("wo_b", 0, [128, 16, KB], BF16)
    ao_st = [at("ao_st0", 32, [128, KB], BF16), at("ao_st1", 34, [128, KB], BF16)]
    xnT5 = at("xnT5", 0, [128, 16, R], BF16)
    hT = at("hT", 32, [128, 16, R], BF16)
    acc = at("acc", 64, [128, 16, R], F32)
    wg_b = [at("wg_b0", 128, [128, 16, 128], BF16), at("wg_b1", 132, [128, 16, 128], BF16)]
    wu_b = [at("wu_b0", 136, [128, 16, 128], BF16), at("wu_b1", 140, [128, 16, 128], BF16)]
    wd_b = [at("wd_b0", 144, [128, 16, 128], BF16), at("wd_b1", 148, [128, 16, 128], BF16)]
    hs_t = at("hs_t", 152, [128, KB], BF16)
    x5_st = [at("x5_st0", 154, [128, D], F32), at("x5_st1", 162, [128, D], F32)]
    at_st = [at("at_st0", 170, [128, D], BF16), at("at_st1", 174, [128, D], BF16)]
    out_st = [at("out_st0", 178, [128, D], F32), at("out_st1", 186, [128, D], F32)]
    rowsum_b = at("rowsum_b", 154, [1, R], BF16)
    hsum_hi = at("hsum_hi", 156, [1, R], BF16)
    hsum_lo = at("hsum_lo", 158, [1, R], BF16)
    hsum_f = at("hsum_f", 160, [1, R], F32)

    semh = {s: nc.alloc_semaphore(name=s) for s in SEMS}

    def emit(P):
        CK = {}

        def dvs(fn):
            P.w("vector", "dve", P.cnt["dve"])
            return P.do("vector", fn, "dve")

        def acs(fn):
            P.w("scalar", "act", P.cnt["act"])
            return P.do("scalar", fn, "act")

        P.dma("sync", "qxC", P_["idn_f"][:], g["idn"])
        P.dma("sync", "qxC", P_["gq_sb"][:], g["gq_col"])
        P.dma("sync", "qxC", P_["gkv_sb"][:], g["gkv_col"])
        P.dma("sync", "qxC", P_["sg_sb"][:], g["sg_col"])
        P.dma("sync", "qxC", P_["su_sb"][:], g["su_col"])
        P.dma("sync", "qxC", P_["sd_sb"][:], g["sd_col"])
        P.dma("sync", "qxC", tabs_f[0:RD, :], g["ctab"])
        P.dma("sync", "qxC", tabs_f[RD:128, :], g["stab"])
        v_sync0 = P.dma("sync", "qxC", P_["idn2_f"][:], g["idn2"])
        v_tab = v_idn2 = v_idn = v_sync0
        P.dma("gpsimd", "qwE", P_["mask_b"][:], g["mask16"])
        P.dma("gpsimd", "qwE", wdq_b[:], g["w_dq"].rearrange("(c p) n -> p c n", p=128))
        P.dma("gpsimd", "qwE", wdkv_b[:], g["w_dkv"].rearrange("(c p) n -> p c n", p=128))
        P.dma("gpsimd", "qwE", wkr_b[:, :, 0:RD], g["w_kr"].rearrange("(c p) n -> p c n", p=128))
        w_kr_ld = P.dma("gpsimd", "qwE", wkr_b[:, :, RD:128],
                        g["w_kr"].rearrange("(c p) n -> p c n", p=128))
        w_dkv_ld = w_kr_ld
        P.do("vector", lambda e: e.memset(P_["ones_col"][:], 1.0))
        P.do("vector", lambda e: e.memset(P_["ones_row"][:], 1.0))
        P.do("vector", lambda e: e.memset(P_["neg128_row"][:], -128.0))
        P.do("vector", lambda e: e.memset(P_["zro"][:], 0.0))
        ms_ck = P.do("vector", lambda e: e.memset(P_["expb"][:], EXPB), "dve")
        P.w("tensor", "dve", ms_ck)
        P.w("vector", "qxC", v_idn)
        P.do("vector", lambda e: e.tensor_copy(out=P_["idn_b"][:], in_=P_["idn_f"][:]), "dve")
        P.w("vector", "qxC", v_idn2)
        P.do("vector", lambda e: e.tensor_copy(out=P_["idn2_b"][:], in_=P_["idn2_f"][:]), "dve")
        P.w("vector", "qxC", v_tab)
        P.do("vector", lambda e: e.tensor_copy(out=tabs_b[:], in_=tabs_f[:]), "dve")
        P.w("scalar", "qxC", v_sync0)
        P.w("vector", "qwE", w_kr_ld)
        P.w("tensor", "qwE", w_dkv_ld)
        m1 = 0
        for c in range(16):
            m1 = P.do("tensor", lambda e, c=c: e.matmul(pb[0][0:1, 0:KVL], P_["ones_col"][:],
                                                        wdkv_b[:, c, :], start=(c == 0),
                                                        stop=(c == 15)),
                      "pe" if c == 15 else None)
        P.w("tensor", "qwE", w_kr_ld)
        m2 = 0
        for c in range(16):
            m2 = P.do("tensor", lambda e, c=c: e.matmul(pb[1][0:1, 0:128], P_["ones_col"][:],
                                                        wkr_b[:, c, :], start=(c == 0),
                                                        stop=(c == 15)),
                      "pe" if c == 15 else None)
        P.w("scalar", "pe", m1)
        P.do("scalar", lambda e: e.activation(out=P_["ncol_dkv"][:], in_=pb[0][0:1, 0:KVL],
                                              func=AF.Copy, scale=-1.0), "act")
        P.w("scalar", "pe", m2)
        P.do("scalar", lambda e: e.activation(out=P_["ncol_kr"][:], in_=pb[1][0:1, 0:128],
                                              func=AF.Copy, scale=-1.0), "act")
        P.join()

        apply_ck = [0] * 8
        tr_pe = [0] * 8
        tr_ck = [0] * 8
        xo_dma = [0] * 8
        for t in range(8):
            if t >= 2:
                P.w("sync", "dve", apply_ck[t - 2])
            xo_dma[t] = P.dma("sync", "qxA" if t % 2 == 0 else "qxB",
                              xo_st[t % 2][:], g["x_own"][ts(t, 128), :])
            P.w("vector", "qxA" if t % 2 == 0 else "qxB", xo_dma[t])
            rs_ck = P.do("vector", lambda e, t=t: e.reduce_sum(st1[:, t:t + 1], xo_st[t % 2][:],
                                                               axis=AX.X), "dve")
            P.w("scalar", "qxA" if t % 2 == 0 else "qxB", xo_dma[t])
            if t >= 2:
                P.w("scalar", "pe", tr_pe[t - 2])
            P.do("scalar", lambda e, t=t: e.activation(out=xn_t[t % 2][:], in_=xo_st[t % 2][:],
                                                       func=AF.Square,
                                                       accum_out=st1[:, 8 + t:9 + t]), "act")
            sq_ck = P.cnt["act"]
            P.w("vector", "act", sq_ck)
            dvs(lambda e, t=t: e.tensor_scalar(out=st1[:, 16 + t:17 + t],
                                               in0=st1[:, t:t + 1], scalar1=1.0 / D,
                                               scalar2=None, op0=ALU.mult))
            dvs(lambda e, t=t: e.tensor_scalar(out=st1[:, 40 + t:41 + t],
                                               in0=st1[:, 8 + t:9 + t], scalar1=1.0 / D,
                                               scalar2=1e-5, op0=ALU.mult, op1=ALU.add))
            dvs(lambda e, t=t: e.tensor_tensor(out=st1[:, 48 + t:49 + t],
                                               in0=st1[:, 16 + t:17 + t],
                                               in1=st1[:, 16 + t:17 + t], op=ALU.mult))
            dv = dvs(lambda e, t=t: e.tensor_tensor(out=st1[:, 40 + t:41 + t],
                                                    in0=st1[:, 40 + t:41 + t],
                                                    in1=st1[:, 48 + t:49 + t],
                                                    op=ALU.subtract))
            P.w("scalar", "dve", dv)
            sr = acs(lambda e, t=t: e.activation(out=st1[:, 48 + t:49 + t],
                                                 in_=st1[:, 40 + t:41 + t],
                                                 func=AF.Sqrt))
            P.w("vector", "act", sr)
            rcp = dvs(lambda e, t=t: e.reciprocal(out=st1[:, 24 + t:25 + t],
                                                  in_=st1[:, 48 + t:49 + t]))
            if t >= 2:
                P.w("vector", "pe", tr_pe[t - 2])
            apply_ck[t] = dvs(lambda e, t=t: e.tensor_scalar(
                out=xn_t[t % 2][:], in0=xo_st[t % 2][:],
                scalar1=st1[:, 16 + t:17 + t], scalar2=st1[:, 24 + t:25 + t],
                op0=ALU.subtract, op1=ALU.mult))
            P.w("tensor", "dve", apply_ck[t])
            if t >= 1:
                P.w("tensor", "dve", tr_ck[t - 1])
            for j in range(16):
                v = P.do("tensor", lambda e, t=t, j=j: e.transpose(
                    pbv(4 + j // 8)[:, ts(j % 8, 128)], xn_t[t % 2][:, ts(j, 128)],
                    P_["idn_b"][:]),
                    "pe" if j == 15 else None)
                if j == 15:
                    tr_pe[t] = v
            P.w("vector", "pe", tr_pe[t])
            P.do("vector", lambda e, t=t: e.tensor_copy(
                out=xnT[:, 0:8, ts(t, 128)],
                in_=pbv(4).rearrange("p (c r) -> p c r", c=8)))
            tr_ck[t] = P.do("vector", lambda e, t=t: e.tensor_copy(
                out=xnT[:, 8:16, ts(t, 128)],
                in_=pbv(5).rearrange("p (c r) -> p c r", c=8)), "dve")
        qln_pe = [0] * 8
        qln_ck = [0] * 8
        ap2_ck = [0] * 8
        for t in range(8):
            P.w("tensor", "dve", tr_ck[7] if t == 0 else tr_ck[t])
            if t >= 2:
                P.w("tensor", "act", ap2_ck[t - 2])
            mmq = 0
            for j in range(16):
                P.do("tensor", lambda e, t=t, j=j: e.matmul(pb[0][:, 0:512], xnT[:, j, ts(t, 128)],
                                                            wdq_b[:, j, 0:512],
                                                            start=(j == 0), stop=(j == 15)))
                mmq = P.do("tensor", lambda e, t=t, j=j: e.matmul(pb[1][:, 0:256],
                                                                  xnT[:, j, ts(t, 128)],
                                                                  wdq_b[:, j, 512:768],
                                                                  start=(j == 0), stop=(j == 15)),
                           "pe" if j == 15 else None)
            P.w("scalar", "pe", mmq)
            if t >= 2:
                P.w("scalar", "pe", qln_pe[t - 2])
            P.do("scalar", lambda e, t=t: e.activation(out=ql_t[t % 2][:, 0:512], in_=pb[0][:, 0:512],
                                                       func=AF.Square, accum_out=st1[:, t:t + 1]), "act")
            s2 = P.do("scalar", lambda e, t=t: e.activation(out=ql_t[t % 2][:, 512:768],
                                                            in_=pb[1][:, 0:256], func=AF.Square,
                                                            accum_out=st1[:, 8 + t:9 + t]), "act")
            P.w("vector", "act", s2)
            dvs(lambda e, t=t: e.tensor_tensor(out=st1[:, t:t + 1], in0=st1[:, t:t + 1],
                                               in1=st1[:, 8 + t:9 + t], op=ALU.add))
            dv2 = dvs(lambda e, t=t: e.tensor_scalar(out=st1[:, t:t + 1],
                                                     in0=st1[:, t:t + 1],
                                                     scalar1=1.0 / QL, scalar2=1e-6,
                                                     op0=ALU.mult, op1=ALU.add))
            P.w("scalar", "dve", dv2)
            sr2 = acs(lambda e, t=t: e.activation(out=st1[:, 8 + t:9 + t],
                                                  in_=st1[:, t:t + 1], func=AF.Sqrt))
            P.w("vector", "act", sr2)
            rc2 = dvs(lambda e, t=t: e.reciprocal(out=st1[:, 16 + t:17 + t],
                                                  in_=st1[:, 8 + t:9 + t]))
            P.w("scalar", "dve", rc2)
            acs(lambda e, t=t: e.activation(out=ql_t[t % 2][:, 0:512], in_=pb[0][:, 0:512],
                                            func=AF.Copy, scale=st1[:, 16 + t:17 + t]))
            ap2_ck[t] = acs(lambda e, t=t: e.activation(
                out=ql_t[t % 2][:, 512:768], in_=pb[1][:, 0:256],
                func=AF.Copy, scale=st1[:, 16 + t:17 + t]))
            P.w("tensor", "act", ap2_ck[t])
            if t >= 1:
                P.w("tensor", "dve", qln_ck[t - 1])
            for j in range(6):
                v = P.do("tensor", lambda e, t=t, j=j: e.transpose(
                    pbv(4)[:, ts(j, 128)], ql_t[t % 2][:, ts(j, 128)], P_["idn_b"][:]),
                    "pe" if j == 5 else None)
                if j == 5:
                    qln_pe[t] = v
            P.w("vector", "pe", qln_pe[t])
            qln_ck[t] = P.do("vector", lambda e, t=t: e.tensor_copy(
                out=qlnT[:, :, ts(t, 128)],
                in_=pbv(4)[:, 0:768].rearrange("p (c r) -> p c r", c=6)), "dve")
        P.w("scalar", "dve", tr_ck[7])
        sp_ck = P.dma("scalar", "qsC", g["xnT_sp"][:, :, :], xnT[:, :, :])
        CK["xnT_spill"] = sp_ck
        for _eng in ("gpsimd", "tensor", "vector", "scalar", "sync"):
            P.w(_eng, "dve", P.cnt["dve"])
            P.w(_eng, "act", P.cnt["act"])
            P.w(_eng, "pe", P.cnt["pe"])
        P.w("vector", "qsC", CK["xnT_spill"])
        kv_apply = [0, 0]
        xT_free = [0, 0, 0]
        xr_free = [0, 0]
        ckst_free = [0, 0]
        tr_evac = 0
        kpe_evac = 0
        muT_ack = 0
        kvb_pe = [0, 0]
        rs_dve = [0, 0]
        for t in range(2):
            for half in range(2):
                slot = (2 * t + half) % 3
                P.w("gpsimd", "pe", xT_free[slot])
                CK[("xT", t, half)] = P.dma(
                    "gpsimd", "qwT%d" % slot, xT_st[slot][:],
                    g["xT_c"].rearrange("(c p) n -> p c n", p=128)[:, :, 512 * t + 256 * half:
                                                                  512 * t + 256 * half + 256])
            P.w("gpsimd", "dve", ckst_free[t % 2])
            P.dma("gpsimd", "qwA" if t % 2 == 0 else "qwB", ck_st[t % 2][:],
                  g["cos_kc"].rearrange("(s p) r -> p s r", p=128)[:, 4 * t:4 * t + 4, :])
            tab_dma = P.dma("gpsimd", "qwA" if t % 2 == 0 else "qwB", sk_st[t % 2][:],
                            g["sin_kc"].rearrange("(s p) r -> p s r", p=128)[:, 4 * t:4 * t + 4, :])
            for s in range(4):
                sl = (4 * t + s) % 2
                P.w("gpsimd", "act", xr_free[sl])
                xr_dma = P.dma("gpsimd", "qwR%d" % sl, xr_st[sl][:],
                               g["x_rowc"][512 * t + 128 * s:512 * t + 128 * s + 128, :])
                P.w("vector", "qwR%d" % sl, xr_dma)
                red = P.do("vector", lambda e, sl=sl, s=s: e.reduce_sum(
                    st2[:, s:s + 1], xr_st[sl][:], axis=AX.X), "dve")
                P.w("scalar", "qwR%d" % sl, xr_dma)
                P.w("scalar", "dve", red)
                xr_free[sl] = P.do("scalar", lambda e, sl=sl, s=s: e.activation(
                    out=xr_st[sl][:], in_=xr_st[sl][:], func=AF.Square,
                    accum_out=st2[:, 4 + s:5 + s]), "act")
            sq_ck = P.cnt["act"]
            P.w("vector", "act", sq_ck)
            dvs(lambda e: e.tensor_scalar(out=st2[:, 8:12], in0=st2[:, 0:4],
                                          scalar1=1.0 / D, scalar2=None, op0=ALU.mult))
            dvs(lambda e: e.tensor_scalar(out=st2[:, 16:20], in0=st2[:, 4:8],
                                          scalar1=1.0 / D, scalar2=1e-5,
                                          op0=ALU.mult, op1=ALU.add))
            dvs(lambda e: e.tensor_tensor(out=st2[:, 20:24], in0=st2[:, 8:12],
                                          in1=st2[:, 8:12], op=ALU.mult))
            dvs(lambda e: e.tensor_tensor(out=st2[:, 16:20], in0=st2[:, 16:20],
                                          in1=st2[:, 20:24], op=ALU.subtract))
            dv_e = dvs(lambda e: e.tensor_scalar(out=st2[:, 24:28], in0=st2[:, 16:20],
                                                 scalar1=1e-6, scalar2=None, op0=ALU.mult))
            P.w("scalar", "dve", dv_e)
            srs = acs(lambda e: e.activation(out=st2[:, 20:24], in_=st2[:, 16:20],
                                             func=AF.Sqrt))
            P.w("vector", "act", srs)
            invs = dvs(lambda e: e.reciprocal(out=st2[:, 12:16], in_=st2[:, 20:24]))
            mu_cast = dvs(lambda e: e.tensor_copy(out=mu_bf[:, 0:4], in_=st2[:, 8:12]))
            P.w("tensor", "dve", mu_cast)
            P.w("tensor", "act", muT_ack)
            trm = 0
            for s in range(4):
                trm = P.do("tensor", lambda e, s=s: e.matmul(
                    pb[3][0:1, ts(s, 128)], mu_bf[:, s:s + 1],
                    P_["idn_b"][:], start=True, stop=True), "pe" if s == 3 else None)
            P.w("scalar", "pe", trm)
            P.w("scalar", "pe", CK.get(("rank1_pe", t - 1), 0))
            muT_ack = P.do("scalar", lambda e: e.activation(out=muT_row[:], in_=pb[3][0:1, 0:512],
                                                            func=AF.Copy), "act")
            for s in range(4):
                par = s % 2
                pkv = pb[par]
                pkp = pb[2] if par == 0 else pb[6]
                P.w("tensor", "qwT%d" % ((2 * t + s // 2) % 3), CK[("xT", t, s // 2)])
                P.w("tensor", "act", kv_apply[par])
                P.w("tensor", "act", muT_ack)
                for c in range(16):
                    P.do("tensor", lambda e, t=t, s=s, c=c, pkv=pkv: e.matmul(
                        pkv[:, 0:512], xT_st[(2 * t + s // 2) % 3][:, c, ts(s % 2, 128)],
                        wdkv_b[:, c, :], start=(c == 0), stop=False))
                    P.do("tensor", lambda e, t=t, s=s, c=c, pkp=pkp: e.matmul(
                        pkp[:, 0:128], xT_st[(2 * t + s // 2) % 3][:, c, ts(s % 2, 128)],
                        wkr_b[:, c, :], start=(c == 0), stop=False))
                kv_mm = P.do("tensor", lambda e, s=s, pkv=pkv: e.matmul(
                    pkv[:, 0:512], muT_row[0:1, ts(s, 128)], P_["ncol_dkv"][:],
                    start=False, stop=True), "pe")
                kpe_mm = P.do("tensor", lambda e, s=s, pkp=pkp: e.matmul(
                    pkp[:, 0:128], muT_row[0:1, ts(s, 128)], P_["ncol_kr"][:],
                    start=False, stop=True), "pe")
                if s == 3:
                    xT_free[(2 * t) % 3] = P.cnt["pe"]
                    xT_free[(2 * t + 1) % 3] = P.cnt["pe"]
                    CK[("rank1_pe", t)] = P.cnt["pe"]
                P.w("scalar", "pe", kv_mm)
                if (4 * t + s) >= 2:
                    P.w("scalar", "pe", kvb_pe[par])
                P.do("scalar", lambda e, par=par, pkv=pkv, s=s: e.activation(
                    out=kv_b[par][:], in_=pkv[:, 0:512], func=AF.Square,
                    accum_out=st2b[:, 4 * s:4 * s + 1]), "act")
                sq2 = P.cnt["act"]
                P.w("vector", "act", sq2)
                dvs(lambda e, s=s: e.tensor_scalar(
                    out=st2b[:, 4 * s + 1:4 * s + 2], in0=st2b[:, 4 * s:4 * s + 1],
                    scalar1=1.0 / KVL, scalar2=None, op0=ALU.mult))
                dvv = dvs(lambda e, s=s: e.tensor_tensor(
                    out=st2b[:, 4 * s + 1:4 * s + 2], in0=st2b[:, 4 * s + 1:4 * s + 2],
                    in1=st2[:, 24 + s:25 + s], op=ALU.add))
                P.w("scalar", "dve", dvv)
                sr2 = acs(lambda e, s=s: e.activation(
                    out=st2b[:, 4 * s + 2:4 * s + 3], in_=st2b[:, 4 * s + 1:4 * s + 2],
                    func=AF.Sqrt))
                P.w("vector", "act", sr2)
                rc2 = dvs(lambda e, s=s: e.reciprocal(
                    out=st2b[:, 4 * s + 3:4 * s + 4], in_=st2b[:, 4 * s + 2:4 * s + 3]))
                P.w("scalar", "dve", rc2)
                kv_apply[par] = acs(lambda e, par=par, pkv=pkv, s=s: e.activation(
                    out=kv_b[par][:], in_=pkv[:, 0:512], func=AF.Copy,
                    scale=st2b[:, 4 * s + 3:4 * s + 4]))
                P.w("scalar", "pe", kpe_mm)
                if s == 0:
                    P.w("scalar", "dve", CK.get(("rope", t - 1), 0))
                    P.w("scalar", "pe", CK.get(("kpetr", t - 1), 0))
                kpe_ap = P.do("scalar", lambda e, s=s, pkp=pkp: e.activation(
                    out=kpe_sb[:, s, :], in_=pkp[:, 0:128], func=AF.Copy,
                    scale=st2[:, 12 + s:13 + s]), "act")
                if s == 3:
                    CK[("kpe_ap", t)] = kpe_ap
                P.w("tensor", "act", kv_apply[par])
                P.w("tensor", "dve", tr_evac)
                for q in range(4):
                    v = P.do("tensor", lambda e, q=q, par=par: e.transpose(
                        pbv(4)[:, 256 * q + 128 * par:256 * q + 128 * par + 128],
                        kv_b[par][:, ts(q, 128)], P_["idn_b"][:]),
                        "pe" if q == 3 else None)
                kvb_pe[par] = v
                if par == 1:
                    P.w("vector", "pe", kvb_pe[1])
                    tr_evac = P.do("vector", lambda e, t=t, s=s: e.tensor_copy(
                        out=kv_own[:, :, 512 * t + 128 * (s - 1):512 * t + 128 * (s - 1) + 256],
                        in_=pbv(4).rearrange("p (q r) -> p q r", q=4)), "dve")
            P.w("vector", "act", CK[("kpe_ap", t)])
            P.w("vector", "qwA" if t % 2 == 0 else "qwB", tab_dma)

            def _x(i):
                return kpe_sb.rearrange("p s (h r) -> p s h r", h=4)[:, :, i, :]

            ctile = ck_st[t % 2]
            stile = sk_st[t % 2]
            dvs(lambda e: e.tensor_tensor(out=rtmp[0][:], in0=_x(1), in1=stile[:],
                                          op=ALU.mult))
            dvs(lambda e: e.tensor_tensor(out=rtmp[1][:], in0=_x(0), in1=ctile[:],
                                          op=ALU.mult))
            dvs(lambda e: e.tensor_tensor(out=_x(2), in0=_x(2), in1=stile[:],
                                          op=ALU.mult))
            dvs(lambda e: e.tensor_tensor(out=_x(3), in0=_x(3), in1=ctile[:],
                                          op=ALU.mult))
            dvs(lambda e: e.tensor_tensor(out=_x(1), in0=_x(2), in1=_x(3),
                                          op=ALU.add))
            dvs(lambda e: e.tensor_tensor(out=_x(0), in0=rtmp[1][:], in1=rtmp[0][:],
                                          op=ALU.subtract))
            dvs(lambda e: e.tensor_copy(out=_x(2), in_=_x(0)))
            ckst_free[t % 2] = dvs(lambda e: e.tensor_copy(out=_x(3), in_=_x(1)))
            rope_ck = ckst_free[t % 2]
            CK[("rope", t)] = rope_ck
            P.w("tensor", "dve", rope_ck)
            P.w("tensor", "dve", kpe_evac)
            tr2 = 0
            for s in range(4):
                tr2 = P.do("tensor", lambda e, s=s: e.transpose(pbv(5)[:, ts(s, 128)],
                                                                kpe_sb[:, s, :], P_["idn_b"][:]),
                           "pe" if s == 3 else None)
            CK[("kpetr", t)] = tr2
            P.w("vector", "pe", tr2)
            kpe_evac = P.do("vector", lambda e, t=t: e.tensor_copy(
                out=kpe_own[:, ts(t, 512)], in_=pbv(5)[:, 0:512]), "dve")
        P.w("scalar", "dve", max(tr_evac, kpe_evac))
        P.dma("scalar", "qsC", g["kv2_sh"][:, 0:4096],
              kv_own[:].rearrange("p c k -> p (c k)"))
        sp2 = P.dma("scalar", "qsC", g["kv2_sh"][:, 4096:5120], kpe_own[:])
        P.w("gpsimd", "qsC", sp2)
        P.do("gpsimd", lambda e: e.collective_compute(
            "AllGather", ALU.bypass, replica_groups=[list(range(8))],
            ins=[g["kv2_sh"].opt()], outs=[g["kv2_gx"].opt()]), "cc2")
        P.w("tensor", "dve", P.cnt["dve"])
        P.w("tensor", "act", P.cnt["act"])
        wq_dma = [0, 0]
        cvtq = [0, 0]
        q_done = [0] * NH
        for h in range(NH):
            par = h % 2
            P.w("gpsimd", "dve", cvtq[par])
            wq_dma[par] = P.dma("gpsimd", "qwA" if par == 0 else "qwB", wuq_st[par][:],
                                g["w_uq"].rearrange("(c p) n -> p c n", p=128)[:, :, ts(h, 192)])
            P.w("vector", "qwA" if par == 0 else "qwB", wq_dma[par])
            if h >= 2:
                P.w("vector", "pe", CK.get(("qmm", h - 2), 0))
            ts6 = 0
            for j in range(6):
                ts6 = P.do("vector", lambda e, j=j, par=par: e.tensor_scalar(
                    out=wuq_b[par][:, j, 0:192], in0=wuq_st[par][:, j, :],
                    scalar1=P_["gq_sb"][:, j:j + 1], scalar2=SCALE, op0=ALU.mult, op1=ALU.mult),
                    "dve" if j == 5 else None)
            P.w("vector", "dve", ts6)
            P.do("vector", lambda e, par=par: e.tensor_copy(out=wuq_b[par][:, :, 192:224],
                                                            in_=wuq_b[par][:, :, 160:192]))
            cvtq[par] = P.do("vector", lambda e, par=par: e.tensor_copy(
                out=wuq_b[par][:, :, 224:256], in_=wuq_b[par][:, :, 128:160]), "dve")
            P.w("tensor", "dve", cvtq[par])
            if h == 0:
                P.w("tensor", "dve", qln_ck[7])
            else:
                P.w("tensor", "dve", q_done[h - 1])
            qn_mm = 0
            for j in range(6):
                for nn in range(2):
                    qn_mm = P.do("tensor", lambda e, j=j, par=par, nn=nn: e.matmul(
                        pb[2 + nn][:, 0:512], wuq_b[par][:, j, 0:128],
                        qlnT[:, j, ts(nn, 512)], start=(j == 0), stop=(j == 5)),
                        "pe" if (j == 5 and nn == 1) else None)
            qr_mm = 0
            for j in range(6):
                for nn in range(2):
                    qr_mm = P.do("tensor", lambda e, j=j, par=par, nn=nn: e.matmul(
                        pb[4 + nn][:, 0:512], wuq_b[par][:, j, 128:256],
                        qlnT[:, j, ts(nn, 512)], start=(j == 0), stop=(j == 5)),
                        "pe" if (j == 5 and nn == 1) else None)
            CK[("qmm", h)] = qr_mm
            P.w("vector", "pe", qn_mm)
            P.do("vector", lambda e, h=h: e.tensor_copy(out=QnA[:, h, 0:512], in_=pb[2][:, 0:512]))
            P.do("vector", lambda e, h=h: e.tensor_copy(out=QnA[:, h, 512:1024], in_=pb[3][:, 0:512]))
            P.w("vector", "pe", qr_mm)
            P.do("vector", lambda e: e.tensor_tensor(
                out=qr_prod[:, 0:512], in0=pb[4][:, 0:512], in1=tabs_b[:, 0:512], op=ALU.mult))
            prod_ck = P.do("vector", lambda e: e.tensor_tensor(
                out=qr_prod[:, 512:1024], in0=pb[5][:, 0:512], in1=tabs_b[:, 512:1024],
                op=ALU.mult), "dve")
            P.w("tensor", "dve", prod_ck)
            P.do("tensor", lambda e: e.matmul(pb[6][0:RD, 0:512], P_["idn2_b"][:],
                                              qr_prod[:, 0:512], start=True, stop=True))
            fold_mm = P.do("tensor", lambda e: e.matmul(psD[0:RD, 0:512], P_["idn2_b"][:],
                                                        qr_prod[:, 512:1024],
                                                        start=True, stop=True), "pe")
            P.w("vector", "pe", fold_mm)
            if h >= 1:
                P.w("vector", "qxC", CK[("qrd", h - 1)])
            P.do("vector", lambda e: e.tensor_copy(out=qr_tmp[:, 0:512], in_=pb[6][0:RD, 0:512]))
            q_done[h] = P.do("vector", lambda e: e.tensor_copy(out=qr_tmp[:, 512:1024],
                                                               in_=psD[0:RD, 0:512]), "dve")
            P.w("sync", "dve", q_done[h])
            CK[("qrd", h)] = P.dma("sync", "qxC", QrA[RD * (h % 2):RD * (h % 2) + RD, h // 2, :],
                                   qr_tmp[:])
        CK["qra_done"] = CK[("qrd", NH - 1)]
        P.join()

        P.w("sync", "cc2", 1)
        P.w("sync", "qsC", CK["xnT_spill"])
        P.w("sync", "qxC", CK["qra_done"])
        for r in range(8):
            P.dma("sync", "qxC", kvT[:, :, ts(r, 1024)],
                  g["kv2_gx"][r, :, 0:4096].rearrange("p (c k) -> p c k", c=4))
            CK["kv_loaded"] = P.dma("sync", "qxC", kpeT[:, ts(r, 1024)],
                                    g["kv2_gx"][r, :, 4096:5120])
        P.join()

        P.w("tensor", "qxC", CK["kv_loaded"])
        cvt3 = 0
        spill = 0
        prod_pe = 0
        for j in range(2):
            P.w("scalar", "dve", cvt3)
            st_dma = P.dma("scalar", "qsA", wuk_st1[:],
                           g["w_uk_own"].rearrange("(q p) n -> p q n", p=128)[:, :, ts(j, 128)])
            st_dma = P.dma("scalar", "qsA", wuv_st1[:],
                           g["w_uv_own"].rearrange("(q p) n -> p q n", p=128)[:, :, ts(j, 128)])
            P.w("vector", "qsA", st_dma)
            P.w("vector", "pe", prod_pe)
            for q in range(4):
                P.do("vector", lambda e, q=q: e.tensor_scalar(
                    out=wuk_b1[:, q, :], in0=wuk_st1[:, q, :],
                    scalar1=P_["gkv_sb"][:, q:q + 1], scalar2=None, op0=ALU.mult))
            for q in range(4):
                cvt3 = P.do("vector", lambda e, q=q: e.tensor_scalar(
                    out=wuv_b1[:, q, :], in0=wuv_st1[:, q, :],
                    scalar1=P_["gkv_sb"][:, q:q + 1], scalar2=None, op0=ALU.mult),
                    "dve" if q == 3 else None)
            P.w("vector", "qsC", spill)
            kst_ev = [0, 0]
            for gge in range(16):
                bk = 5 + (gge % 2)
                P.w("tensor", "dve", cvt3)
                P.w("tensor", "dve", kst_ev[gge % 2])
                for q in range(4):
                    P.do("tensor", lambda e, q=q, gge=gge, bk=bk: e.matmul(
                        pb[bk][:, 0:512], wuk_b1[:, q, :], kvT[:, q, ts(gge, 512)],
                        start=(q == 0), stop=(q == 3)), "pe" if q == 3 else None)
                P.w("vector", "pe", P.cnt["pe"])
                kst_ev[gge % 2] = P.do("vector", lambda e, gge=gge, bk=bk: e.tensor_copy(
                    out=Kst[:, ts(gge, 512)], in_=pb[bk][:, 0:512]), "dve")
            vst_ev = [kst_ev[0], kst_ev[1]]
            for k in range(64):
                bk = 5 + (k % 2)
                P.w("tensor", "dve", vst_ev[k % 2])
                for q in range(4):
                    P.do("tensor", lambda e, q=q, k=k, bk=bk: e.matmul(
                        pb[bk][:, 0:128], kvT[:, q, ts(k, 128)], wuv_b1[:, q, :],
                        start=(q == 0), stop=(q == 3)), "pe" if q == 3 else None)
                P.w("vector", "pe", P.cnt["pe"])
                vst_ev[k % 2] = P.do("vector", lambda e, k=k, bk=bk: e.tensor_copy(
                    out=Vst[:, k, :], in_=pb[bk][:, 0:128]), "dve")
            prod_pe = P.cnt["pe"]
            P.w("scalar", "dve", max(vst_ev))
            P.dma("scalar", "qsC", g["kv3_sh%d" % j][:, 0:S], Kst[:])
            spill = P.dma("scalar", "qsC", g["kv3_sh%d" % j][:, S:2 * S],
                          Vst[:].rearrange("p t v -> p (t v)"))
            P.w("gpsimd", "qsC", spill)
            P.do("gpsimd", lambda e, j=j: e.collective_compute(
                "AllGather", ALU.bypass, replica_groups=[list(range(8))],
                ins=[g["kv3_sh%d" % j].opt()], outs=[g["kv3_gx%d" % j].opt()]), "cc3")
        P.w("sync", "dve", cvt3)
        P.w("gpsimd", "dve", cvt3)
        P.w("scalar", "qsC", spill)
        P.w("vector", "qsC", spill)
        head_norm = [0] * (NH + 1)
        expk = [0, 0]
        pvk = [0, 0]
        den_evac = 0
        kdma = [0, 0]
        vdma = [0, 0]
        pend = [None]
        tail_h = [None]

        def emit_pv(k, kpar, gpar, s4, mk, npr, hi, gidx, h):
            P.w("tensor", "dve", mk)
            if k == 0:
                P.w("tensor", "dve", CK.get("pv_ev1", 0))
                P.w("tensor", "act", CK.get("psd_free", 0))
            stf = (k == 0)
            if 16 * k < 512:
                na = 512 - 16 * k
                P.do("tensor", lambda e, kpar=kpar, gpar=gpar, s4=s4, k=k, na=na, stf=stf: e.matmul(
                    pb[0][:, 16 * k:512], V4_sb[gpar][:, s4, :], P_sb[kpar][:, 0:na],
                    start=stf, stop=(k == 31)))
                P.do("tensor", lambda e, kpar=kpar, gpar=gpar, s4=s4, na=na, npr=npr, stf=stf: e.matmul(
                    pb[1][:, 0:512], V4_sb[gpar][:, s4, :], P_sb[kpar][:, na:npr],
                    start=stf, stop=False))
                P.do("tensor", lambda e, kpar=kpar, k=k, na=na: e.matmul(
                    psD[0:1, 16 * k:512], P_["ones_col"][:], P_sb[kpar][:, 0:na],
                    start=(k == 0), stop=(k == 31)))
                pvk[kpar] = P.do("tensor", lambda e, kpar=kpar, k=k, na=na, npr=npr: e.matmul(
                    psD[32:33, 0:512], P_["ones_col"][:], P_sb[kpar][:, na:npr],
                    start=(k == 0), stop=False), "pe")
            else:
                P.do("tensor", lambda e, kpar=kpar, gpar=gpar, s4=s4, k=k, npr=npr: e.matmul(
                    pb[1][:, 16 * k - 512:512], V4_sb[gpar][:, s4, :], P_sb[kpar][:, 0:npr],
                    start=False, stop=(k == 63)))
                pvk[kpar] = P.do("tensor", lambda e, kpar=kpar, k=k, npr=npr: e.matmul(
                    psD[32:33, 16 * k - 512:512], P_["ones_col"][:], P_sb[kpar][:, 0:npr],
                    start=False, stop=(k == 63)), "pe")
            if s4 == 3:
                CK[("pvdone", gidx)] = P.cnt["pe"]
            if k == 31:
                P.w("vector", "pe", pvk[kpar])
                CK["pv_ev0"] = P.do("vector", lambda e, h=h: e.tensor_copy(
                    out=atT[:, h, 0:512], in_=pb[0][:, 0:512]), "dve")
            elif k == 63:
                P.w("vector", "pe", pvk[kpar])
                CK["pv_ev1"] = P.do("vector", lambda e, h=h: e.tensor_copy(
                    out=atT[:, h, 512:1024], in_=pb[1][:, 0:512]), "dve")

        def emit_tail(h, hi):
            P.w("scalar", "pe", pvk[0])
            P.w("scalar", "pe", pvk[1])
            acs(lambda e: e.activation(out=rec_b[:, 0:512], in_=psD[0:1, :],
                                       func=AF.Copy))
            den_ev = acs(lambda e: e.activation(out=rec_b[:, 512:1024],
                                                in_=psD[32:33, :], func=AF.Copy))
            P.w("tensor", "act", den_ev)
            P.do("tensor", lambda e: e.matmul(pb[6][:, 0:512], P_["ones_row"][:],
                                              rec_b[:, 0:512], start=True, stop=True))
            bc_mm = P.do("tensor", lambda e: e.matmul(psD[:, 0:512], P_["ones_row"][:],
                                                      rec_b[:, 512:1024],
                                                      start=True, stop=True), "pe")
            P.w("scalar", "pe", bc_mm)
            acs(lambda e: e.activation(out=bc_f[:, 0:512], in_=pb[6][:, 0:512],
                                       func=AF.Ln))
            CK["psd_free"] = acs(lambda e: e.activation(out=bc_f[:, 512:1024],
                                                        in_=psD[:, 0:512],
                                                        func=AF.Ln))
            acs(lambda e: e.activation(out=bc_f[:, 0:512], in_=bc_f[:, 0:512],
                                       func=AF.Exp, scale=-1.0))
            lastx = acs(lambda e: e.activation(out=bc_f[:, 512:1024],
                                               in_=bc_f[:, 512:1024],
                                               func=AF.Exp, scale=-1.0))
            P.w("vector", "act", lastx)
            dvs(lambda e, h=h: e.tensor_tensor(
                out=atT[:, h, 0:512], in0=atT[:, h, 0:512], in1=bc_f[:, 0:512],
                op=ALU.mult))
            head_norm[hi] = dvs(lambda e, h=h: e.tensor_tensor(
                out=atT[:, h, 512:1024], in0=atT[:, h, 512:1024],
                in1=bc_f[:, 512:1024], op=ALU.mult))

        for hi, h in enumerate(list(range(0, NH, 2)) + list(range(1, NH, 2))):
            j = h % 2
            rk = h // 2
            base = RD * (h % 2)
            hslot = h // 2
            P.w("sync", "cc3", j + 1)
            P.w("gpsimd", "cc3", j + 1)
            for gge in range(16):
                gidx = 16 * hi + gge
                gpar = gidx % 2
                P.w("sync", "pe", CK.get(("scdone", gidx - 2), 0))
                kdma[gpar] = P.dma("sync", "qxA" if gpar == 0 else "qxB", K_sb[gpar][:],
                                   g["kv3_gx%d" % j][rk, :, 512 * gge:512 * gge + 512])
                P.w("gpsimd", "pe", CK.get(("pvdone", gidx - 2), 0))
                vdma[gpar] = P.dma("gpsimd", "qwA" if gpar == 0 else "qwB", V4_sb[gpar][:],
                                   g["kv3_gx%d" % j][rk, :, S + 512 * gge:S + 512 * gge + 512]
                                   .rearrange("p (t v) -> p t v", t=4))
                P.w("tensor", "qxA" if gpar == 0 else "qxB", kdma[gpar])
                P.w("tensor", "qwA" if gpar == 0 else "qwB", vdma[gpar])
                for s4 in range(4):
                    k = 4 * gge + s4
                    kpar = k % 2
                    npr = 1024 - 16 * k
                    sb0 = 2 if kpar == 0 else 4
                    P.w("tensor", "act", expk[kpar])
                    n1 = min(npr, 512)
                    P.do("tensor", lambda e, sb0=sb0, gpar=gpar, s4=s4, h=h, k=k, n1=n1: e.matmul(
                        pb[sb0][:, 0:n1], K_sb[gpar][:, ts(s4, 128)],
                        QnA[:, h, 16 * k:16 * k + n1], start=True, stop=False))
                    if npr > 512:
                        P.do("tensor", lambda e, sb0=sb0, gpar=gpar, s4=s4, h=h, k=k, npr=npr: e.matmul(
                            pb[sb0 + 1][:, 0:npr - 512], K_sb[gpar][:, ts(s4, 128)],
                            QnA[:, h, 16 * k + 512:1024], start=True, stop=False))
                    scmm = P.do("tensor", lambda e, sb0=sb0, base=base, hslot=hslot, k=k, n1=n1, npr=npr: e.matmul(
                        pb[sb0][:, 0:n1], kpeT[base:base + RD, ts(k, 128)],
                        QrA[base:base + RD, hslot, 16 * k:16 * k + n1],
                        start=False, stop=True),
                        "pe" if npr <= 512 else None)
                    if npr > 512:
                        scmm = P.do("tensor", lambda e, sb0=sb0, base=base, hslot=hslot, k=k, npr=npr: e.matmul(
                            pb[sb0 + 1][:, 0:npr - 512], kpeT[base:base + RD, ts(k, 128)],
                            QrA[base:base + RD, hslot, 16 * k + 512:1024],
                            start=False, stop=True), "pe")
                    if s4 == 3:
                        CK[("scdone", gidx)] = scmm
                    P.w("scalar", "pe", scmm)
                    P.w("scalar", "pe", pvk[kpar])
                    expk[kpar] = P.do("scalar", lambda e, kpar=kpar, sb0=sb0, n1=n1: e.activation(
                        out=P_sb[kpar][:, 0:n1], in_=pb[sb0][:, 0:n1],
                        func=AF.Exp, bias=P_["expb"][:]), "act")
                    if npr > 512:
                        expk[kpar] = P.do("scalar", lambda e, kpar=kpar, sb0=sb0, npr=npr: e.activation(
                            out=P_sb[kpar][:, 512:npr], in_=pb[sb0 + 1][:, 0:npr - 512],
                            func=AF.Exp, bias=P_["expb"][:]), "act")
                    P.w("vector", "act", expk[kpar])
                    mk = P.do("vector", lambda e, kpar=kpar: e.tensor_tensor(
                        out=P_sb[kpar][:, 0:16], in0=P_sb[kpar][:, 0:16],
                        in1=P_["mask_b"][:], op=ALU.mult), "dve")
                    if tail_h[0] is not None and gge == 0 and s4 == 1:
                        emit_tail(*tail_h[0])
                        tail_h[0] = None
                    if pend[0] is not None:
                        emit_pv(**pend[0])
                    pend[0] = dict(k=k, kpar=kpar, gpar=gpar, s4=s4, mk=mk,
                                   npr=npr, hi=hi, gidx=gidx, h=h)
            emit_pv(**pend[0])
            pend[0] = None
            tail_h[0] = (h, hi)
        emit_tail(*tail_h[0])
        tail_h[0] = None
        P.join()

        evac4 = [0, 0]
        sp4 = [0, 0]
        wo_pe_done = 0
        for half in range(2):
            P.w("gpsimd", "pe", wo_pe_done)
            wo_dma = P.dma("gpsimd", "qwE", wo_b[:],
                           g["w_o"].rearrange("(c p) n -> p c n", p=128)[:, :, 1024 * half:1024 * half + 1024])
            for qt in range(8):
                ppar = qt % 2
                qsem = "qsA" if ppar == 0 else "qsB"
                b4 = 0 if ppar == 0 else 2
                P.w("tensor", "qwE", wo_dma)
                P.w("tensor", "dve", evac4[ppar])
                mm4 = 0
                for c in range(16):
                    for nn in range(2):
                        mm4 = P.do("tensor", lambda e, c=c, nn=nn, qt=qt, b4=b4: e.matmul(
                            pb[b4 + nn][:, 0:512],
                            atT[:, c, ts(qt, 128)], wo_b[:, c, ts(nn, 512)],
                            start=(c == 0), stop=(c == 15)),
                            "pe" if (c == 15 and nn == 1) else None)
                P.w("vector", "pe", mm4)
                P.w("vector", qsem, sp4[ppar])
                P.do("vector", lambda e, ppar=ppar, b4=b4: e.tensor_copy(
                    out=ao_st[ppar][:, 0:512], in_=pb[b4][:, 0:512]))
                evac4[ppar] = P.do("vector", lambda e, ppar=ppar, b4=b4: e.tensor_copy(
                    out=ao_st[ppar][:, 512:1024], in_=pb[b4 + 1][:, 0:512]), "dve")
                P.w("scalar", "dve", evac4[ppar])
                sp4[ppar] = P.dma("scalar", qsem,
                                  g["attn_sp"][ts(qt, 128), 1024 * half:1024 * half + 1024],
                                  ao_st[ppar][:])
            wo_pe_done = P.cnt["pe"]
        P.join()

        P.w("vector", "qsA", sp4[0])
        P.w("vector", "qsB", sp4[1])
        P.w("sync", "qsC", CK["xnT_spill"])
        rl = P.dma("sync", "qxC", xnT5[:], g["xnT_sp"][:, :, :])
        P.w("tensor", "qxC", rl)
        rs_mm = 0
        for c in range(16):
            for nn in range(2):
                rs_mm = P.do("tensor", lambda e, c=c, nn=nn: e.matmul(
                    psD[0:1, 0:512] if nn == 0 else psD[32:33, 0:512],
                    P_["ones_col"][:], xnT5[:, c, ts(nn, 512)],
                    start=(c == 0), stop=(c == 15)),
                    "pe" if (c == 15 and nn == 1) else None)
        P.w("scalar", "pe", rs_mm)
        P.do("scalar", lambda e: e.activation(out=rowsum_b[:, 0:512], in_=psD[0:1, :],
                                              func=AF.Copy))
        rs_ck = P.do("scalar", lambda e: e.activation(out=rowsum_b[:, 512:1024],
                                                      in_=psD[32:33, :], func=AF.Copy), "act")
        P.w("tensor", "act", rs_ck)
        silu_ck = [0, 0]
        stt_ck = [0, 0]
        g_pe = {}
        d_pe = {}
        dacc_ck = {}
        hs_done = 0
        d_last_pe = 0
        stt_last = 0
        for blk in range(4):
            for it in range(16):
                gi = 16 * blk + it
                wpar = gi % 2
                P.w("gpsimd", "pe", g_pe.get(gi - 2, 0))
                P.dma("gpsimd", "qwA" if wpar == 0 else "qwB", wg_b[wpar][:],
                      g["wgTr"][:, gi, :, :])
                wgu = P.dma("gpsimd", "qwA" if wpar == 0 else "qwB", wu_b[wpar][:],
                            g["wuTr"][:, gi, :, :])
                P.w("tensor", "qwA" if wpar == 0 else "qwB", wgu)
                P.w("tensor", "qxC", rl)
                P.w("tensor", "act", rs_ck)
                P.w("tensor", "act", silu_ck[0])
                P.w("tensor", "act", silu_ck[1])
                P.w("tensor", "dve", stt_ck[0])
                P.w("tensor", "dve", stt_ck[1])
                if blk >= 1 and it == 0:
                    P.w("tensor", "dve", dacc_ck.get((blk - 1, 15), 0))
                for c in range(16):
                    for rh in range(2):
                        P.do("tensor", lambda e, c=c, rh=rh, wpar=wpar: e.matmul(
                            pb[2 * rh][:, 0:512], wg_b[wpar][:, c, :],
                            xnT5[:, c, ts(rh, 512)], start=(c == 0), stop=False))
                for rh in range(2):
                    P.do("tensor", lambda e, rh=rh: e.matmul(
                        pb[2 * rh][:, 0:512], P_["neg128_row"][:],
                        rowsum_b[:, ts(rh, 512)], start=False, stop=True))
                for c in range(16):
                    for rh in range(2):
                        P.do("tensor", lambda e, c=c, rh=rh, wpar=wpar: e.matmul(
                            pb[2 * rh + 1][:, 0:512], wu_b[wpar][:, c, :],
                            xnT5[:, c, ts(rh, 512)], start=(c == 0), stop=False))
                gmm = 0
                for rh in range(2):
                    gmm = P.do("tensor", lambda e, rh=rh: e.matmul(
                        pb[2 * rh + 1][:, 0:512], P_["neg128_row"][:],
                        rowsum_b[:, ts(rh, 512)], start=False, stop=True),
                        "pe" if rh == 1 else None)
                g_pe[gi] = gmm
                P.w("scalar", "pe", gmm)
                P.w("scalar", "dve", stt_ck[0])
                P.w("scalar", "dve", stt_ck[1])
                for rh in range(2):
                    silu_ck[rh] = P.do("scalar", lambda e, rh=rh, gi=gi: e.activation(
                        out=hs_t[:, ts(rh, 512)], in_=pb[2 * rh][:, 0:512], func=AF.Silu,
                        scale=P_["sg_sb"][:, gi:gi + 1]), "act")
                P.w("vector", "act", silu_ck[1])
                if blk >= 1 and it == 0:
                    P.w("vector", "pe", d_last_pe)
                for rh in range(2):
                    stt_ck[rh] = P.do("vector", lambda e, rh=rh, gi=gi, it=it: e.scalar_tensor_tensor(
                        out=hT[:, it, ts(rh, 512)], in0=pb[2 * rh + 1][:, 0:512],
                        scalar=P_["su_sb"][:, gi:gi + 1], in1=hs_t[:, ts(rh, 512)],
                        op0=ALU.mult, op1=ALU.mult), "dve")
                P.w("tensor", "dve", stt_ck[0])
                P.w("tensor", "dve", stt_ck[1])
                for nn in range(2):
                    hs_mm = P.do("tensor", lambda e, it=it, nn=nn, blk=blk: e.matmul(
                        psD[0:1, 0:512] if nn == 0 else psD[32:33, 0:512],
                        P_["ones_col"][:], hT[:, it, ts(nn, 512)],
                        start=(blk == 0 and it == 0), stop=(blk == 3 and it == 15)),
                        "pe" if nn == 1 else None)
                if it == 15:
                    stt_last = max(stt_ck[0], stt_ck[1])
            if blk == 3:
                P.w("scalar", "pe", hs_mm)
                hf = P.do("scalar", lambda e: e.activation(out=hsum_f[:, 0:512],
                                                           in_=psD[0:1, :], func=AF.Copy), "act")
                hf = P.do("scalar", lambda e: e.activation(out=hsum_f[:, 512:1024],
                                                           in_=psD[32:33, :], func=AF.Copy), "act")
                P.w("vector", "act", hf)
                dvs(lambda e: e.tensor_copy(out=hsum_hi[:], in_=hsum_f[:]))
                hs_done = dvs(lambda e: e.tensor_tensor(
                    out=hsum_lo[:], in0=hsum_f[:], in1=hsum_hi[:],
                    op=ALU.subtract))
            for ht_i in range(16):
                dpar = ht_i % 2
                P.w("gpsimd", "pe", d_pe.get((blk, ht_i - 2), 0))
                wdd = P.dma("gpsimd", "qwC" if dpar == 0 else "qwD", wd_b[dpar][:],
                            g["wdTr"][:, ht_i, 16 * blk:16 * blk + 16, :])
                P.w("tensor", "qwC" if dpar == 0 else "qwD", wdd)
                P.w("tensor", "dve", stt_last)
                P.w("tensor", "dve", dacc_ck.get((blk, ht_i - 1), 0))
                if blk == 3:
                    P.w("tensor", "dve", hs_done)
                for c in range(16):
                    for nn in range(2):
                        dmm = P.do("tensor", lambda e, c=c, nn=nn, dpar=dpar, blk=blk: e.matmul(
                            pb[4 + nn][:, 0:512], wd_b[dpar][:, c, :],
                            hT[:, c, ts(nn, 512)], start=(c == 0),
                            stop=(blk != 3 and c == 15)),
                            "pe" if (blk != 3 and c == 15 and nn == 1) else None)
                if blk == 3:
                    for nn in range(2):
                        P.do("tensor", lambda e, nn=nn: e.matmul(
                            pb[4 + nn][:, 0:512], P_["neg128_row"][:],
                            hsum_hi[:, ts(nn, 512)], start=False, stop=False))
                    for nn in range(2):
                        dmm = P.do("tensor", lambda e, nn=nn: e.matmul(
                            pb[4 + nn][:, 0:512], P_["neg128_row"][:],
                            hsum_lo[:, ts(nn, 512)], start=False, stop=True),
                            "pe" if nn == 1 else None)
                d_pe[(blk, ht_i)] = dmm
                P.w("vector", "pe", dmm)
                if blk >= 1:
                    P.w("vector", "dve", dacc_ck[(blk - 1, ht_i)])
                if blk == 0:
                    P.do("vector", lambda e, ht_i=ht_i: e.tensor_copy(
                        out=acc[:, ht_i, 0:512], in_=pb[4][:, 0:512]))
                    dacc_ck[(blk, ht_i)] = P.do("vector", lambda e, ht_i=ht_i: e.tensor_copy(
                        out=acc[:, ht_i, 512:1024], in_=pb[5][:, 0:512]), "dve")
                else:
                    P.do("vector", lambda e, ht_i=ht_i: e.tensor_tensor(
                        out=acc[:, ht_i, 0:512], in0=acc[:, ht_i, 0:512],
                        in1=pb[4][:, 0:512], op=ALU.add))
                    dacc_ck[(blk, ht_i)] = P.do("vector", lambda e, ht_i=ht_i: e.tensor_tensor(
                        out=acc[:, ht_i, 512:1024], in0=acc[:, ht_i, 512:1024],
                        in1=pb[5][:, 0:512], op=ALU.add), "dve")
            d_last_pe = P.cnt["pe"]
        sc_ck = 0
        for ht_i in range(16):
            P.w("vector", "dve", dacc_ck[(3, ht_i)])
            sc_ck = dvs(lambda e, ht_i=ht_i: e.tensor_scalar(
                out=acc[:, ht_i, :], in0=acc[:, ht_i, :],
                scalar1=P_["sd_sb"][:, ht_i:ht_i + 1], scalar2=None, op0=ALU.mult))
        fin_dve = [0, 0]
        y_dmas = [0] * 8
        y_dma = 0
        for rt in range(8):
            rpar = rt % 2
            P.w("sync", "dve", sc_ck)
            P.w("sync", "dve", fin_dve[rpar])
            x5d = P.dma("sync", "qxA" if rpar == 0 else "qxB", x5_st[rpar][:],
                        g["x_own"][ts(rt, 128), :])
            P.w("scalar", "qsA" if rpar == 0 else "qsB", sp4[rpar])
            P.w("scalar", "dve", sc_ck)
            P.w("scalar", "dve", fin_dve[rpar])
            atd = P.dma("scalar", "qsA" if rpar == 0 else "qsB", at_st[rpar][:],
                        g["attn_sp"][ts(rt, 128), :])
            P.w("tensor", "dve", sc_ck)
            P.w("tensor", "dve", fin_dve[0])
            P.w("tensor", "dve", fin_dve[1])
            trf = 0
            for ht_i in range(16):
                trf = P.do("tensor", lambda e, ht_i=ht_i, rt=rt: e.transpose(
                    pb[ht_i // 4][:, ts(ht_i % 4, 128)], acc[:, ht_i, ts(rt, 128)],
                    P_["idn_f"][:]),
                    "pe" if ht_i == 15 else None)
            P.w("vector", "pe", trf)
            P.w("vector", "qxA" if rpar == 0 else "qxB", x5d)
            P.w("vector", "qsA" if rpar == 0 else "qsB", atd)
            if rt >= 2:
                P.w("vector", "qxD" if rpar == 0 else "qxE", y_dmas[rt - 2])
            for bb in range(4):
                dvs(lambda e, rpar=rpar, bb=bb: e.tensor_tensor(
                    out=out_st[rpar][:, ts(bb, 512)], in0=pb[bb][:, 0:512],
                    in1=x5_st[rpar][:, ts(bb, 512)], op=ALU.add))
            fin_dve[rpar] = dvs(lambda e, rpar=rpar: e.tensor_tensor(
                out=out_st[rpar][:], in0=out_st[rpar][:], in1=at_st[rpar][:], op=ALU.add))
            P.w("sync", "dve", fin_dve[rpar])
            y_dmas[rt] = y_dma = P.dma("sync", "qxD" if rpar == 0 else "qxE",
                                       g["y"][ts(rt, 128), :], out_st[rpar][:])
        for eng in ENGS:
            P.w(eng, "qxD", y_dmas[6])
            P.w(eng, "qxE", y_dmas[7])
        return P.cnt

    counts = []
    with nc.Block() as block:
        deco = {"sync": block.sync, "gpsimd": block.gpsimd, "vector": block.vector,
                "scalar": block.scalar, "tensor": block.tensor}
        for name in ENGS:
            holder = {}

            def run(eng_obj, name=name, holder=holder):
                Pp = Emit(nc, name, semh)
                holder["cnt"] = emit(Pp)

            deco[name](run)
            counts.append(dict(holder["cnt"]))
    for c in counts[1:]:
        assert c == counts[0], ("pass counter drift", counts[0], c)
    return nc


_CACHE = {}


def _get_nc():
    if "nc" not in _CACHE:
        _CACHE["nc"] = build_nc()
    return _CACHE["nc"]


def kernel(L0_attn_x, L0_attn_w_dq, L0_attn_g_q, L0_attn_w_uq, L0_attn_w_dkv,
           L0_attn_w_kr, L0_attn_g_kv, L0_attn_w_uk, L0_attn_w_uv, L0_attn_w_o,
           L0_ffn_gate_w, L0_ffn_gate_scale, L0_ffn_up_w, L0_ffn_up_scale,
           L0_ffn_down_w, L0_ffn_down_scale):
    x = np.ascontiguousarray(np.asarray(L0_attn_x, dtype=np.float32)[0])
    f32 = np.float32
    ac = np.ascontiguousarray
    cos_full, sin_full = _rope_tables_fp32()

    def _u8(w):
        a = np.ascontiguousarray(np.asarray(w, np.int32))
        return a.view(np.uint8)[:, 0::4]

    shared = {
        "w_dq": ac(np.asarray(L0_attn_w_dq, f32)),
        "w_uq": ac(np.asarray(L0_attn_w_uq, f32)),
        "gq_col": ac(np.asarray(L0_attn_g_q, f32).reshape(6, 128).T),
        "w_dkv": ac(np.asarray(L0_attn_w_dkv, f32)),
        "w_kr": ac(np.asarray(L0_attn_w_kr, f32)),
        "gkv_col": ac(np.asarray(L0_attn_g_kv, f32).reshape(4, 128).T),
        "w_o": ac(np.asarray(L0_attn_w_o, f32)),
        "wgTr": ac(_u8(L0_ffn_gate_w).T.reshape(16, 128, 64, 128).transpose(1, 2, 0, 3)),
        "wuTr": ac(_u8(L0_ffn_up_w).T.reshape(16, 128, 64, 128).transpose(1, 2, 0, 3)),
        "wdTr": ac(_u8(L0_ffn_down_w).T.reshape(64, 128, 16, 128).transpose(1, 2, 0, 3)),
        "sg_col": ac(np.asarray(L0_ffn_gate_scale, f32).reshape(64, 128).T),
        "su_col": ac(np.asarray(L0_ffn_up_scale, f32).reshape(64, 128).T),
        "sd_col": ac(np.asarray(L0_ffn_down_scale, f32).reshape(16, 128).T),
        "idn": np.eye(128, dtype=f32),
        "idn2": ac(np.tile(np.eye(RD, dtype=f32), (2, 1))),
    }

    in_maps = []
    for c in range(NCORE):
        pos = np.arange(c, S, 8)
        cc = cos_full[pos].T
        ss = sin_full[pos].T
        ctab = ac(np.concatenate([cc, cc], axis=0).astype(f32))
        stab = ac(np.concatenate([-ss, ss], axis=0).astype(f32))
        kr = np.arange(128, dtype=np.int64)[:, None]
        jj = np.arange(16, dtype=np.int64)[None, :]
        mask16 = (kr <= (c + 8 * jj)).astype(f32)
        m = dict(shared)
        m["x_own"] = ac(x[c::8])
        m["ctab"] = ctab
        m["stab"] = stab
        m["mask16"] = ac(mask16)
        m["w_uk_own"] = ac(np.asarray(L0_attn_w_uk, f32)[:, 256 * c:256 * c + 256])
        m["w_uv_own"] = ac(np.asarray(L0_attn_w_uv, f32)[:, 256 * c:256 * c + 256])
        m["xT_c"] = ac(x[1024 * c:1024 * c + 1024].T)
        m["x_rowc"] = ac(x[1024 * c:1024 * c + 1024])
        m["cos_kc"] = ac(cos_full[1024 * c:1024 * c + 1024])
        m["sin_kc"] = ac(sin_full[1024 * c:1024 * c + 1024])
        in_maps.append(m)

    nc = _get_nc()
    res = run_bass_kernel_spmd(nc, in_maps, core_ids=list(range(NCORE)))
    out = np.empty((S, D), dtype=np.float32)
    for c in range(NCORE):
        out[c::8] = res.results[c]["y"]
    return out.reshape(1, S, D)
